# version 18
# baseline (speedup 1.0000x reference)
import contextlib
import numpy as np
import ml_dtypes
import concourse.bass as bass
import concourse.mybir as mybir
from concourse.bass_utils import run_bass_kernel_spmd

F32 = mybir.dt.float32
BF16 = mybir.dt.bfloat16
I32 = mybir.dt.int32
OP = mybir.AluOpType
AF = mybir.ActivationFunctionType
AXX = mybir.AxisListType.X
BF = ml_dtypes.bfloat16

T = 2048
C = 1024
NT = T // 128
NCH = C // 128
NHP = 8
D = 64
U0 = 12.0
FFN0 = 4096
FFN1 = 3072
NEG = -1.0e30
C127H = float(np.float32(1.0 / 127.0))
C127L = float(np.float32(1.0 / 127.0 - np.float64(np.float32(1.0 / 127.0))))

SEMS = ("s_ld", "s_gp", "s_sc", "s_t", "s_v", "s_a", "s_g", "s_out")


class Prog:

    def __init__(self):
        self.ops = {k: [] for k in ("tensor", "vector", "scalar", "gpsimd", "sync")}
        self.cnt = {s: 0 for s in SEMS}
        self.sems = None

    def _add(self, eng, fn, inc=None, n=1):
        tgt = None
        if inc is not None:
            self.cnt[inc] += n
            tgt = self.cnt[inc]
        sem = inc

        def emit(e):
            ins = fn(e)
            if sem is not None:
                ins.then_inc(self.sems[sem], n)
        self.ops[eng].append(emit)
        return tgt

    def t(self, fn, inc=None, n=1):
        return self._add("tensor", fn, inc, n)

    def v(self, fn, inc=None, n=1):
        return self._add("vector", fn, inc, n)

    def s(self, fn, inc=None, n=1):
        return self._add("scalar", fn, inc, n)

    def g(self, fn, inc=None, n=1):
        return self._add("gpsimd", fn, inc, n)

    def y(self, fn, inc=None, n=1):
        return self._add("sync", fn, inc, n)

    def dma(self, eng, out_ap_fn, in_ap_fn, sem):
        return self._add(eng, lambda e: e.dma_start(out=out_ap_fn(), in_=in_ap_fn()),
                         sem, 16)

    def wait(self, eng, sem, val=None):
        v = self.cnt[sem] if val is None else val
        self.ops[eng].append(lambda e: e.wait_ge(self.sems[sem], v))

    def c(self, sem):
        return self.cnt[sem]


def _host_consts():
    half = D // 2
    theta = 1.0 / (10000.0 ** (np.arange(half, dtype=np.float64) / half))
    pos = np.arange(T, dtype=np.float64)
    freqs = pos[:, None] * theta[None, :]
    cosv = np.cos(freqs)
    sinv = np.sin(freqs)
    idx = np.arange(D) // 2
    cos_fm = cosv[:, idx].T
    sin_fm = sinv[:, idx].T
    cos_fm = np.concatenate([cos_fm, cos_fm], 0)
    sin_fm = np.concatenate([sin_fm, sin_fm], 0)
    cos_hi = cos_fm.astype(np.float32)
    sin_hi = sin_fm.astype(np.float32)
    cos_lo = (cos_fm - cos_hi).astype(np.float32)
    sin_lo = (sin_fm - sin_hi).astype(np.float32)
    perm = np.zeros((128, 128), dtype=np.float32)
    for m in range(128):
        base = (m // 64) * 64
        j = m % 64
        if j < 32:
            perm[base + j + 32, m] = -1.0
        else:
            perm[base + j - 32, m] = 1.0
    ident = np.eye(128, dtype=np.float32)
    out = {
        "c_cos_hi": cos_hi, "c_sin_hi": sin_hi,
        "c_cos_lo": np.asarray(cos_lo), "c_sin_lo": np.asarray(sin_lo),
        "c_perm": perm.astype(BF), "c_ident": ident.astype(BF),
    }
    r = np.arange(128)[:, None]
    cc = np.arange(256)[None, :]
    for o in range(2):
        m = np.zeros((128, 256), dtype=np.float32)
        m[(o * 128 + r > cc)] = NEG
        out[f"c_mask{o}"] = m.astype(BF)
    return out


def _ln_stats(p, xt, sc_sq, sums, sumsq, stat, eps, i):
    p.wait("scalar", "s_v")
    p.v(lambda e: e.tensor_reduce(sums[:, i:i + 1], xt, AXX, OP.add), inc="s_v")
    p.s(lambda e: e.activation(sc_sq[:, :], xt, AF.Square,
                               accum_out=sumsq[:, i:i + 1]), inc="s_a")
    p.wait("vector", "s_a")
    p.wait("vector", "s_v")
    mean = stat[:, 2 * i:2 * i + 1]
    rstd = stat[:, 2 * i + 1:2 * i + 2]
    p.v(lambda e: e.tensor_scalar(mean, sums[:, i:i + 1], 1.0 / C, None, OP.mult))
    p.v(lambda e: e.tensor_scalar(rstd, sumsq[:, i:i + 1], 1.0 / C, None,
                                  OP.mult), inc="s_v")
    p.wait("vector", "s_v")
    p.v(lambda e: e.tensor_tensor(sums[:, i:i + 1], mean, mean, OP.mult),
        inc="s_v")
    p.wait("vector", "s_v")
    p.v(lambda e: e.tensor_tensor(rstd, rstd, sums[:, i:i + 1], OP.subtract),
        inc="s_v")
    p.wait("vector", "s_v")
    p.v(lambda e: e.tensor_scalar(rstd, rstd, float(eps), None, OP.add),
        inc="s_v")
    p.wait("scalar", "s_v")
    p.s(lambda e: e.activation(rstd, rstd, AF.Ln), inc="s_a")
    p.wait("scalar", "s_a")
    p.s(lambda e: e.activation(rstd, rstd, AF.Exp, scale=-0.5), inc="s_a")
    return mean, rstd


def build(n_layer_stop=None, debug_name=None):
    nc = bass.Bass("TRN2", target_bir_lowering=False,
                   detect_race_conditions=False)

    x_in = nc.dram_tensor("x", [T, C], F32, kind="ExternalInput").ap()
    out = nc.dram_tensor("out", [T, C], F32, kind="ExternalOutput").ap()
    dbg = None
    if debug_name is not None:
        dbg = nc.dram_tensor("dbg", [T, C], F32, kind="ExternalOutput").ap()

    w_qkv = nc.dram_tensor("L0_attn_w_qkv", [C, 3 * C], F32, kind="ExternalInput").ap()
    l0_wproj = nc.dram_tensor("L0_attn_w_proj", [C, C], F32, kind="ExternalInput").ap()
    l0_wfc = nc.dram_tensor("L0_ffn_w_fc", [FFN0, C], F32, kind="ExternalInput").ap()
    l0_bfc = nc.dram_tensor("L0_ffn_b_fc", [FFN0], F32, kind="ExternalInput").ap()
    l0_wp = nc.dram_tensor("L0_ffn_w_proj", [C, FFN0], F32, kind="ExternalInput").ap()
    l0_bp = nc.dram_tensor("L0_ffn_b_proj", [C], F32, kind="ExternalInput").ap()
    l1_wq = nc.dram_tensor("L1_attn_Wq", [C, C], F32, kind="ExternalInput").ap()
    l1_bq = nc.dram_tensor("L1_attn_bq", [C], F32, kind="ExternalInput").ap()
    l1_wk = nc.dram_tensor("L1_attn_Wk", [C, C], F32, kind="ExternalInput").ap()
    l1_bk = nc.dram_tensor("L1_attn_bk", [C], F32, kind="ExternalInput").ap()
    l1_wv = nc.dram_tensor("L1_attn_Wv", [C, C], F32, kind="ExternalInput").ap()
    l1_bv = nc.dram_tensor("L1_attn_bv", [C], F32, kind="ExternalInput").ap()
    l1_wo = nc.dram_tensor("L1_attn_Wo", [C, C], F32, kind="ExternalInput").ap()
    l1_bo = nc.dram_tensor("L1_attn_bo", [C], F32, kind="ExternalInput").ap()
    l1_gamma = nc.dram_tensor("L1_attn_gamma", [C], F32, kind="ExternalInput").ap()
    l1_beta = nc.dram_tensor("L1_attn_beta", [C], F32, kind="ExternalInput").ap()
    l1_wg = nc.dram_tensor("L1_ffn_w_gate", [FFN1, C], I32, kind="ExternalInput").ap()
    l1_wu = nc.dram_tensor("L1_ffn_w_up", [FFN1, C], I32, kind="ExternalInput").ap()
    l1_wd = nc.dram_tensor("L1_ffn_w_down", [C, FFN1], I32, kind="ExternalInput").ap()

    cmaps = _host_consts()
    cdram = {k: nc.dram_tensor(k, list(v.shape),
                               BF16 if v.dtype == BF else F32,
                               kind="ExternalInput").ap()
             for k, v in cmaps.items()}
    wfcq_hbm = nc.dram_tensor("wfcq_scratch", [FFN0, C], BF16).ap()
    wprojq_hbm = nc.dram_tensor("wprojq_scratch", [C, FFN0], BF16).ap()
    wg_bf_hbm = nc.dram_tensor("wg_bf_scratch", [FFN1, C], BF16).ap()
    wu_bf_hbm = nc.dram_tensor("wu_bf_scratch", [FFN1, C], BF16).ap()
    wd_bf_hbm = nc.dram_tensor("wd_bf_scratch", [C, FFN1], BF16).ap()

    p = Prog()
    es = contextlib.ExitStack()
    with es:
        def sb(name, part, cols, dt):
            return es.enter_context(nc.sbuf_tensor(name, [part, cols], dt))

        def psum(name, part, cols, dt=F32):
            return es.enter_context(nc.psum_tensor(name, [part, cols], dt))

        def sbuf_group(*specs):
            hs, cms = [], []
            for (name, part, cols, dt) in specs:
                cm = nc.sbuf_tensor(name, [part, cols], dt)
                hs.append(cm.__enter__())
                cms.append(cm)

            def close():
                for cm in reversed(cms):
                    cm.__exit__(None, None, None)
            return hs, close

        x_tm = sb("x_tm", 128, NT * C, F32)
        ctx_tm = sb("ctx_tm", 128, NT * C, BF16)
        cb = {"c_ident": sb("c_ident_t", 128, 128, BF16)}
        sums = sb("sums", 128, NT, F32)
        sumsq = sb("sumsq", 128, NT, F32)
        stat = sb("stat", 128, 2 * NT, F32)
        sc_sq = sb("sc_sq", 128, C, F32)
        cvec = sb("cvec", 128, 4, F32)

        def psum_phase(*specs):
            hs, ctxs = [], []
            for (name, part, cols, dt) in specs:
                cm = nc.psum_tensor(name, [part, cols], dt)
                hs.append(cm.__enter__())
                ctxs.append(cm)
            for cm in reversed(ctxs):
                cm.__exit__(None, None, None)
            return hs

        def sbuf_phase(*specs):
            hs, ctxs = [], []
            for (name, part, cols, dt) in specs:
                cm = nc.sbuf_tensor(name, [part, cols], dt)
                hs.append(cm.__enter__())
                ctxs.append(cm)
            for cm in reversed(ctxs):
                cm.__exit__(None, None, None)
            return hs

        pt_bf, pt_bf2 = psum_phase(("pt_bf", 128, 4096, BF16),
                                   ("pt_bf2", 128, 4096, BF16))

        p.dma("sync", lambda: cb["c_ident"][:, :], lambda: cdram["c_ident"],
              "s_ld")
        p.dma("sync",
              lambda: x_tm[:, :].rearrange("p (i m) -> p i m", m=C),
              lambda: x_in.rearrange("(i p) m -> p i m", p=128),
              "s_ld")
        ld_done = p.c("s_ld")

        def load_consts(keys):
            for k in keys:
                v = cmaps[k]
                dt = BF16 if v.dtype == BF else F32
                cb[k] = None
            specs = [(k + "_t", cmaps[k].shape[0], cmaps[k].shape[1],
                      BF16 if cmaps[k].dtype == BF else F32) for k in keys]
            hs, close = sbuf_group(*specs)
            for k, h in zip(keys, hs):
                cb[k] = h
                p.dma("sync", lambda h=h: h[:, :], lambda dk=cdram[k]: dk,
                      "s_ld")
            p.wait("vector", "s_ld")
            p.wait("gpsimd", "s_ld")
            p.wait("tensor", "s_ld")
            p.wait("scalar", "s_ld")
            return close

        tp_bank_free = {0: 0, 1: 0}

        def emit_T_transposes(dst, dst_col0, dst_stride, src, src_col0,
                              src_stride, ntile, evac_eng="vector"):
            dcols = dst.shape[1] if hasattr(dst, "shape") else None
            for g0 in range(0, ntile, 4):
                n = min(4, ntile - g0)
                bank = (g0 // 4) % 2
                pt = pt_bf if bank == 0 else pt_bf2
                p.wait("tensor", "s_v", tp_bank_free[bank])

                def tr(e, g0=g0, n=n, pt=pt):
                    for j in range(n):
                        e.transpose(
                            pt[:, j * 1024:j * 1024 + 128],
                            src[:, src_col0 + (g0 + j) * src_stride:
                                src_col0 + (g0 + j) * src_stride + 128],
                            cb["c_ident"][:, :])
                    return e.nop()
                p.t(tr, inc="s_t")
                p.wait(evac_eng, "s_t", p.c("s_t"))

                def ev(e, g0=g0, n=n, pt=pt):
                    last = None
                    for j in range(n):
                        c0 = dst_col0 + (g0 + j) * dst_stride
                        last = (e.tensor_copy if evac_eng == "vector"
                                else e.copy)(dst[:, c0:c0 + 128],
                                             pt[:, j * 1024:j * 1024 + 128])
                    return last
                if evac_eng == "vector":
                    p.v(ev, inc="s_v")
                else:
                    p.s(ev, inc="s_v")
                tp_bank_free[bank] = p.c("s_v")

        def emit_xnT_tile(xnT, i, src_bf_tile):
            emit_T_transposes(xnT, i * 128, T, src_bf_tile, 0, 128, NCH)

        def emit_ln_xnT(eps, xnT):
            join_all()
            (xn_bf,), close_xb = sbuf_group(
                (f"xn_bf{p.c('s_v')}", 128, 2 * C, BF16))
            tp_bank_free[0] = p.c("s_v")
            tp_bank_free[1] = p.c("s_v")
            xb_free = {0: 0, 1: 0}
            for i in range(NT):
                xt = x_tm[:, i * C:(i + 1) * C]
                mean, rstd = _ln_stats(p, xt, sc_sq, sums, sumsq, stat, eps, i)
                p.wait("vector", "s_a", p.c("s_a"))
                slot = i % 2
                p.wait("vector", "s_t", xb_free[slot])
                p.wait("vector", "s_a")
                xb = xn_bf[:, slot * C:(slot + 1) * C]
                p.v(lambda e, xt=xt, xb=xb, m=mean, r=rstd:
                    e.tensor_scalar(xb, xt, m, r, OP.subtract, OP.mult),
                    inc="s_v")
                p.wait("tensor", "s_v", p.c("s_v"))
                import os as _os2
                _st = _os2.environ.get("K_STAGE", "")
                if _st != "ln" and not (_st == "ln1" and i > 0):
                    emit_xnT_tile(xnT, i, xb)
                xb_free[slot] = p.c("s_t")
            close_xb()

        def join_all(engines=("tensor", "vector", "scalar", "gpsimd", "sync")):
            for eng in engines:
                for s in SEMS:
                    if p.cnt[s]:
                        p.wait(eng, s)

        def dump_dbg(src_fn, cast=False):
            join_all(("gpsimd",))
            p.dma("gpsimd",
                  lambda: dbg.rearrange("(p a) m -> p (a m)", p=128),
                  src_fn, "s_gp")
            p.wait("gpsimd", "s_gp", p.c("s_gp"))

        p.v(lambda e: e.memset(cvec[:, 0:1], -U0))
        p.v(lambda e: e.memset(cvec[:, 2:3], 0.125))
        p.v(lambda e: e.memset(cvec[:, 1:2], 1.0), inc="s_v")

        p.wait("vector", "s_ld", ld_done)
        p.wait("scalar", "s_ld", ld_done)
        p.wait("tensor", "s_ld", ld_done)

        def emit_attention(L, xnT):
            causal = (L == 0)
            pstAB, pstAB2, pc0, pc1, pc2, pc3 = psum_phase(
                (f"pstAB{L}", 128, 1024, F32), (f"pstAB2{L}", 128, 1024, F32),
                (f"pc0_{L}", 128, 512, F32), (f"pc1_{L}", 128, 512, F32),
                (f"pc2_{L}", 128, 512, F32), (f"pc3_{L}", 128, 512, F32))
            pmm, = psum_phase((f"pmm{L}", 128, 1024, F32))
            pvt_a, pvt_b = psum_phase((f"pvt_a{L}", 128, 1024, BF16),
                                      (f"pvt_b{L}", 128, 1024, BF16))
            psts = [pstAB, pstAB2]
            pcs = [pc0, pc1, pc2, pc3]
            (wsl, qraw, rot_sb, r1, r2, qf, kf, vaa, vab, pT, rc, bql,
             qlo, klo) = sbuf_phase(
                (f"wsl{L}", 128, 3072, BF16),
                (f"qraw{L}", 128, T, BF16),
                (f"rot{L}", 128, T, BF16),
                (f"r1_{L}", 128, T, F32),
                (f"r2_{L}", 128, T, BF16),
                (f"qf{L}", 128, T, BF16),
                (f"kf{L}", 128, T, BF16),
                (f"vaa{L}", 128, 16 * 65, BF16),
                (f"vab{L}", 128, 16 * 65, BF16),
                (f"pT{L}", 128, 2 * 512, BF16),
                (f"rc{L}", 128, 8, F32),
                (f"bql{L}", 128, 24, F32),
                (f"qlo{L}", 128, 256, BF16),
                (f"klo{L}", 128, 256, BF16))
            join_all()
            p.v(lambda e: e.memset(vaa[:, :], 1.0))
            p.v(lambda e: e.memset(vab[:, :], 1.0), inc="s_v")
            if L == 1:
                for name, dram, col0 in (("q", l1_bq, 0), ("k", l1_bk, 8),
                                         ("v", l1_bv, 16)):
                    p.y(lambda e, c0=col0, d=dram: e.dma_start(
                        out=bass.AP(bql, c0, [[24, 128], [1, 8]]),
                        in_=d.rearrange("(j q) -> q j", q=128),
                        allow_slow_non_contiguous=True), inc="s_ld", n=16)
                p.wait("vector", "s_ld", p.c("s_ld"))
                p.v(lambda e: e.tensor_scalar(bql[:, 0:8], bql[:, 0:8], 0.125,
                                              None, OP.mult), inc="s_v")
                p.wait("scalar", "s_v")

            last_av = {"t": 0}
            evac_done = {"v": 0, "a": 0}
            wsl_free = 0

            for hp in range(NHP):
                p.wait("gpsimd", "s_t", wsl_free)
                for i3, (dram, off) in enumerate(
                        [(w_qkv if L == 0 else l1_wq, 0),
                         (w_qkv if L == 0 else l1_wk, C if L == 0 else 0),
                         (w_qkv if L == 0 else l1_wv, 2 * C if L == 0 else 0)]):
                    wid = 3 * C if L == 0 else C
                    p.dma("gpsimd",
                          lambda i3=i3: bass.AP(wsl, i3 * 1024,
                                                [[3072, 128], [128, 8], [1, 128]]),
                          lambda dram=dram, off=off, wid=wid, hp=hp:
                          bass.AP(dram.tensor, off + hp * 128,
                                  [[wid, 128], [128 * wid, 8], [1, 128]]),
                          "s_gp")
                wload = p.c("s_gp")

                def qkv_proj(qi, dst, bias_col, scale):
                    p.wait("tensor", "s_gp", wload)
                    p.wait("tensor", "s_v", evac_done["v"])
                    p.wait("tensor", "s_a")
                    p.wait("scalar", "s_v")
                    p.wait("scalar", "s_g")
                    for ch in range(4):
                        half = ch % 2
                        p.wait("tensor", "s_v", evac_done["a"])
                        def mm(e, qi=qi, ch=ch, half=half):
                            last = None
                            for c in range(NCH):
                                last = e.matmul(
                                    pmm[:, half * 512:(half + 1) * 512],
                                    wsl[:, qi * 1024 + c * 128:qi * 1024 + (c + 1) * 128],
                                    xnT[:, c * T + ch * 512:c * T + (ch + 1) * 512],
                                    start=(c == 0), stop=(c == NCH - 1))
                            return last
                        p.t(mm, inc="s_t")
                        p.wait("vector", "s_t", p.c("s_t"))
                        def ev(e, ch=ch, half=half, dst=dst, bias_col=bias_col,
                               scale=scale):
                            dsl = dst[:, ch * 512:(ch + 1) * 512]
                            psl = pmm[:, half * 512:(half + 1) * 512]
                            if bias_col is None:
                                return e.tensor_copy(dsl, psl)
                            bap = bql[:, bias_col:bias_col + 1]
                            if scale is not None:
                                return e.tensor_scalar(
                                    dsl, psl, cvec[:, 2:3], bap,
                                    OP.mult, OP.add)
                            return e.tensor_scalar(dsl, psl, bap, None,
                                                   OP.add)
                        p.v(ev, inc="s_v")
                        evac_done["a"] = p.c("s_v")

                if L == 0:
                    qkv_proj(0, qraw, None, None)
                    def rope(dstf, dlo):
                        for ch in range(4):
                            half = ch % 2
                            p.wait("tensor", "s_v", evac_done["a"])
                            def mmr(e, ch=ch, half=half):
                                return e.matmul(
                                    pmm[:, half * 512:(half + 1) * 512],
                                    cb["c_perm"][:, :],
                                    qraw[:, ch * 512:(ch + 1) * 512],
                                    start=True, stop=True)
                            p.t(mmr, inc="s_t")
                            p.wait("vector", "s_t", p.c("s_t"))
                            def evr(e, ch=ch, half=half):
                                return e.tensor_copy(
                                    rot_sb[:, ch * 512:(ch + 1) * 512],
                                    pmm[:, half * 512:(half + 1) * 512])
                            p.v(evr, inc="s_v")
                            evac_done["a"] = p.c("s_v")
                        p.wait("vector", "s_a")
                        def ropechain(e, dstf=dstf):
                            e.tensor_tensor(r1[:, :], qraw[:, :],
                                            cb["c_cos_hi"][:, :], OP.mult)
                            e.tensor_tensor(r2[:, :], rot_sb[:, :],
                                            cb["c_sin_hi"][:, :], OP.mult)
                            e.tensor_tensor(r1[:, :], r1[:, :], r2[:, :],
                                            OP.add)
                            e.tensor_tensor(r2[:, :], qraw[:, :],
                                            cb["c_cos_lo"][:, :], OP.mult)
                            e.tensor_tensor(r1[:, :], r1[:, :], r2[:, :],
                                            OP.add)
                            e.tensor_tensor(r2[:, :], rot_sb[:, :],
                                            cb["c_sin_lo"][:, :], OP.mult)
                            e.tensor_tensor(r1[:, :], r1[:, :], r2[:, :],
                                            OP.add)
                            return e.tensor_copy(dstf[:, :], r1[:, :])
                        p.v(ropechain, inc="s_v")
                        p.wait("vector", "s_v")
                        p.v(lambda e, dstf=dstf, dlo=dlo: e.scalar_tensor_tensor(
                            dlo[:, :], dstf[:, 0:256], -1.0, r1[:, 0:256],
                            OP.mult, OP.add), inc="s_v")
                        p.wait("vector", "s_v")
                    rope(qf, qlo)
                    qkv_proj(1, qraw, None, None)
                    rope(kf, klo)
                    qkv_proj(2, rot_sb, None, None)
                    vsrc = rot_sb
                else:
                    qkv_proj(0, qf, 0 + hp, 0.125)
                    qkv_proj(1, kf, 8 + hp, None)
                    qkv_proj(2, rot_sb, 16 + hp, None)
                    vsrc = rot_sb

                p.wait("tensor", "s_v", evac_done["a"])
                for st2 in range(0, 16, 2):
                    for j, pvt in ((0, pvt_a), (1, pvt_b)):
                        sc = st2 + j
                        if sc >= 16:
                            continue
                        p.wait("tensor", "s_v", evac_done["v"])
                        def trv(e, sc=sc, pvt=pvt, vsrc=vsrc):
                            return e.transpose(pvt[:, 0:128],
                                               vsrc[:, sc * 128:(sc + 1) * 128],
                                               cb["c_ident"][:, :])
                        p.t(trv, inc="s_t")
                        p.wait("vector", "s_t", p.c("s_t"))
                        def evv(e, sc=sc, pvt=pvt):
                            e.tensor_copy(vaa[:, sc * 65:sc * 65 + 64],
                                          pvt[:, 0:64])
                            return e.tensor_copy(vab[:, sc * 65:sc * 65 + 64],
                                                 pvt[:, 64:128])
                        p.v(evv, inc="s_v")
                        evac_done["v"] = p.c("s_v")

                p.wait("tensor", "s_v", evac_done["v"])
                for tq in range(8):
                    n_sc = min(16, 2 * tq + 2) if causal else 16
                    exp_guard = [p.c("s_a"), p.c("s_a")]
                    pT_guard = [0, 0]
                    for sc in range(n_sc):
                        dbuf = sc % 2
                        pst = psts[dbuf]
                        p.wait("tensor", "s_a", exp_guard[dbuf])
                        hilo = causal and tq == 0
                        def mst(e, sc=sc, tq=tq, pst=pst, hilo=hilo):
                            terms = [(kf, qf, True, not hilo)]
                            if hilo:
                                terms += [(kf, qlo, False, False),
                                          (klo, qf, False, True)]
                            last = None
                            for (kk, qq, st0, sp0) in terms:
                                kap = (kk[:, sc * 128:(sc + 1) * 128]
                                       if kk is not klo else kk[:, :]
                                       [:, sc * 128:(sc + 1) * 128])
                                qap = (qq[:, tq * 256:(tq + 1) * 256]
                                       if qq is not qlo else qq[:, 0:256])
                                e.matmul(pst[:, 256:512], kap[0:64, :],
                                         qap[0:64, :], start=st0, stop=sp0,
                                         tile_position=(0, 0))
                                last = e.matmul(pst[:, 512:768], kap[64:128, :],
                                                qap[64:128, :], start=st0,
                                                stop=sp0,
                                                tile_position=(64, 0))
                            return last
                        p.t(mst, inc="s_t")
                        mst_done = p.c("s_t")
                        masked = causal and (sc * 128 >= tq * 256)
                        if masked:
                            mk = "c_mask%d" % (sc - 2 * tq)
                            p.wait("vector", "s_t", mst_done)
                            p.v(lambda e, pst=pst, mk=mk: e.tensor_tensor(
                                pst[:, 256:512], pst[:, 256:512], cb[mk][:, :],
                                OP.add), inc="s_v")
                            p.v(lambda e, pst=pst, mk=mk: e.tensor_tensor(
                                pst[:, 512:768], pst[:, 512:768], cb[mk][:, :],
                                OP.add), inc="s_v")
                            p.wait("scalar", "s_v")
                        p.wait("scalar", "s_t", mst_done)
                        p.wait("scalar", "s_t", pT_guard[dbuf])
                        def evex(e, pst=pst, dbuf=dbuf):
                            return e.activation(pT[:, dbuf * 512:(dbuf + 1) * 512],
                                                pst[:, 256:768], AF.Exp,
                                                scale=(0.125 if L == 0 else 1.0),
                                                bias=cvec[:, 0:1])
                        p.s(evex, inc="s_a")
                        exp_guard[dbuf] = p.c("s_a")
                        p.wait("tensor", "s_a", exp_guard[dbuf])
                        def mav(e, sc=sc, dbuf=dbuf, first=(sc == 0),
                                last=(sc == n_sc - 1)):
                            last_i = None
                            for h2 in range(2):
                                last_i = e.matmul(
                                    pcs[h2][:, 0:65],
                                    pT[:, dbuf * 512 + h2 * 128:
                                       dbuf * 512 + (h2 + 1) * 128],
                                    vaa[:, sc * 65:(sc + 1) * 65],
                                    start=first, stop=last)
                                last_i = e.matmul(
                                    pcs[2 + h2][:, 0:65],
                                    pT[:, dbuf * 512 + 256 + h2 * 128:
                                       dbuf * 512 + 256 + (h2 + 1) * 128],
                                    vab[:, sc * 65:(sc + 1) * 65],
                                    start=first, stop=last)
                            return last_i
                        p.t(mav, inc="s_t")
                        pT_guard[dbuf] = p.c("s_t")
                    av_done = p.c("s_t")
                    p.wait("vector", "s_t", av_done)
                    for g in range(4):
                        head = 0 if g < 2 else 1
                        h2 = g % 2
                        tile = 2 * tq + h2
                        col = tile * C + (2 * hp + head) * D
                        p.v(lambda e, g=g: e.reciprocal(rc[:, g:g + 1],
                                                        pcs[g][:, 64:65]),
                            inc="s_v")
                        p.wait("vector", "s_v")
                        p.v(lambda e, g=g, col=col: e.tensor_scalar(
                            ctx_tm[:, col:col + D], pcs[g][:, 0:D],
                            rc[:, g:g + 1], None, OP.mult), inc="s_v")
                    p.wait("tensor", "s_v", p.c("s_v"))
                wsl_free = p.c("s_t")
            join_all()

        def emit_bcast(dst, dram_ap, n=C, off=0):
            p.g(lambda e: e.dma_start(
                out=dst[:, :],
                in_=bass.AP(dram_ap.tensor, off, [[0, 128], [1, n]])),
                inc="s_gp", n=16)

        def emit_attn_proj(L, xnT, slo=None):
            ptp, pproj = psum_phase((f"ptp{L}", 128, 4096, BF16),
                                    (f"pproj{L}", 128, 1024, F32))
            if L == 0:
                wpj, ctxT = sbuf_phase((f"wpj{L}", 128, NCH * C, BF16),
                                       (f"ctxT{L}", 128, C, BF16))
                yb = gf = bof = None
            else:
                wpj, ctxT, yb, xnf, bof, gaf, bef = sbuf_phase(
                    (f"wpj{L}", 128, NCH * C, BF16),
                    (f"ctxT{L}", 128, C, BF16),
                    (f"yb{L}", 128, C, F32),
                    (f"xnf{L}", 128, C, F32),
                    (f"bof{L}", 128, C, F32),
                    (f"gaf{L}", 128, C, F32),
                    (f"bef{L}", 128, C, F32))
            join_all()
            wsrc = l0_wproj if L == 0 else l1_wo
            p.dma("gpsimd",
                  lambda: bass.AP(wpj, 0, [[NCH * C, 128], [C, 8], [1, C]]),
                  lambda: bass.AP(wsrc.tensor, 0, [[C, 128], [128 * C, 8], [1, C]]),
                  "s_gp")
            if L == 1:
                emit_bcast(bof, l1_bo)
                emit_bcast(gaf, l1_gamma)
                emit_bcast(bef, l1_beta)
                p.wait("vector", "s_ld", p.c("s_ld"))
            p.wait("tensor", "s_gp", p.c("s_gp"))
            tp_free = 0
            import os as _os6
            _ks = _os6.environ.get("K_STAGE", "")
            _nt = 1 if _ks == "proj1" else (2 if _ks == "proj2" else NT)
            for i in range(_nt):
                srcs = [(ctx_tm, i * C, 128, ctxT, 0, 128, 8)]
                if _ks == "projnotp" and i > 0:
                    srcs = []
                if L == 1:
                    p.wait("vector", "s_a")
                    p.v(lambda e, i=i: e.tensor_scalar(
                        xnf[:, :], x_tm[:, i * C:(i + 1) * C],
                        stat[:, 2 * i:2 * i + 1], stat[:, 2 * i + 1:2 * i + 2],
                        OP.subtract, OP.mult), inc="s_v")
                p.wait("tensor", "s_a")
                p.wait("tensor", "s_v")
                for (stens, sc0, sstr, dtens, dc0, dstr, ntile) in srcs:
                    for g0 in (0, 4):
                        p.wait("tensor", "s_v", tp_free)
                        def trc(e, stens=stens, sc0=sc0, sstr=sstr, g0=g0):
                            for j in range(4):
                                e.transpose(ptp[:, j * 1024:j * 1024 + 128],
                                            stens[:, sc0 + (g0 + j) * sstr:
                                                  sc0 + (g0 + j) * sstr + 128],
                                            cb["c_ident"][:, :])
                            return e.nop()
                        p.t(trc, inc="s_t")
                        p.wait("vector", "s_t", p.c("s_t"))
                        def evc(e, dtens=dtens, dc0=dc0, dstr=dstr, g0=g0):
                            last = None
                            for j in range(4):
                                last = e.tensor_copy(
                                    dtens[:, dc0 + (g0 + j) * dstr:
                                          dc0 + (g0 + j) * dstr + 128],
                                    ptp[:, j * 1024:j * 1024 + 128])
                            return last
                        p.v(evc, inc="s_v")
                        tp_free = p.c("s_v")
                import os as _os4
                if _os4.environ.get("K_STAGE") == "projtp":
                    continue
                p.wait("tensor", "s_v", tp_free)
                p.wait("tensor", "s_a")
                p.wait("tensor", "s_v")
                for h in range(2):
                    def mmp(e, h=h):
                        last = None
                        for c in range(NCH):
                            last = e.matmul(pproj[:, h * 512:(h + 1) * 512],
                                            ctxT[:, c * 128:(c + 1) * 128],
                                            wpj[:, c * C + h * 512:
                                                c * C + (h + 1) * 512],
                                            start=(c == 0), stop=(c == NCH - 1))
                        return last
                    p.t(mmp, inc="s_t")
                mm_done = p.c("s_t")
                import os as _os5
                if _os5.environ.get("K_STAGE") == "projmm":
                    p.wait("scalar", "s_t", mm_done)
                    p.s(lambda e: e.activation(ctxT[:, 0:512], pproj[:, 0:512],
                                               AF.Identity), inc="s_a")
                    continue
                if _os5.environ.get("K_STAGE") == "projev":
                    p.wait("scalar", "s_t", mm_done)
                    p.s(lambda e: e.activation(ctxT[:, :], pproj[:, :],
                                               AF.Identity), inc="s_a")
                    continue
                if L == 0:
                    p.wait("scalar", "s_t", mm_done)
                    for h in range(2):
                        p.s(lambda e, i=i, h=h: e.activation(
                            ctx_tm[:, i * C + h * 512:i * C + (h + 1) * 512],
                            pproj[:, h * 512:(h + 1) * 512],
                            AF.Identity), inc="s_a")
                    if slo is not None and i < 4:
                        p.wait("vector", "s_a")
                        for h in range(2):
                            p.v(lambda e, i=i, h=h: e.scalar_tensor_tensor(
                                slo[:, i * C + h * 512:i * C + (h + 1) * 512],
                                ctx_tm[:, i * C + h * 512:
                                       i * C + (h + 1) * 512], -1.0,
                                pproj[:, h * 512:(h + 1) * 512],
                                OP.mult, OP.add), inc="s_v")
                else:
                    p.wait("vector", "s_t", mm_done)
                    for h in range(2):
                        p.v(lambda e, h=h: e.scalar_tensor_tensor(
                            yb[:, h * 512:(h + 1) * 512],
                            pproj[:, h * 512:(h + 1) * 512], 1.0,
                            bof[:, h * 512:(h + 1) * 512], OP.mult, OP.add),
                            inc="s_v")
                    p.wait("vector", "s_v")
                    p.v(lambda e: e.tensor_tensor(yb[:, :], yb[:, :],
                                                  xnf[:, :], OP.add),
                        inc="s_v")
                    p.wait("vector", "s_v")
                    mean, rstd = _ln_stats(p, yb[:, :], sc_sq, sums, sumsq,
                                           stat, 1e-3, i)
                    p.wait("vector", "s_a", p.c("s_a"))
                    p.v(lambda e, m=mean, r=rstd: e.tensor_scalar(
                        yb[:, :], yb[:, :], m, r, OP.subtract, OP.mult),
                        inc="s_v")
                    p.wait("vector", "s_v")
                    p.v(lambda e: e.tensor_tensor(yb[:, :], yb[:, :],
                                                  gaf[:, :], OP.mult),
                        inc="s_v")
                    p.wait("vector", "s_v")
                    p.v(lambda e, i=i: e.tensor_tensor(
                        ctx_tm[:, i * C:(i + 1) * C], yb[:, :], bef[:, :],
                        OP.add), inc="s_v")
            join_all()

        (xnT0,), close_xnT0 = sbuf_group(("xnT0", 128, NCH * T, BF16))
        close_l0c = load_consts(["c_cos_hi", "c_sin_hi", "c_cos_lo",
                                 "c_sin_lo", "c_perm", "c_mask0", "c_mask1"])
        emit_ln_xnT(1e-5, xnT0)
        emit_attention(0, xnT0)
        if debug_name == "ctxpre":
            join_all(("gpsimd",))
            p.dma("gpsimd", lambda: dbg.rearrange("(p a) m -> p (a m)", p=128),
                  lambda: ctx_tm[:, :], "s_gp")
            p.wait("gpsimd", "s_gp", p.c("s_gp"))
        close_l0c()
        close_xnT0()
        (slo,), close_slo = sbuf_group(("slo", 128, 4 * C, BF16))
        emit_attn_proj(0, None, slo=slo)

        def emit_quant(src_fn, dst_fn, qs, n_sec=1, veng="vector",
                       multi=False):
            amax, m_, inv, cmp, f_, G = (qs[k] for k in
                                         ("amax", "m", "inv", "cmp", "f", "G"))
            u, cbuf, d, m1, m2 = (qs[k] for k in ("u", "c", "d", "m1", "m2"))
            ef = qs.get("ef")
            nw = n_sec * 32
            def amaxes(e):
                last = None
                for s2 in range(n_sec):
                    src3 = src_fn(s2).rearrange("p (b k) -> p b k", k=32)
                    last = e.tensor_reduce(
                        amax[:, s2 * 32:(s2 + 1) * 32], src3, AXX, OP.max,
                        apply_absolute_value=True)
                return last
            p.v(amaxes, inc="s_v")
            p.wait("vector", "s_v")
            p.v(lambda e: e.tensor_scalar(m_[:, :nw], amax[:, :nw],
                                          float(2.0 ** -64), None, OP.max),
                inc="s_v")
            p.wait("vector", "s_v")
            p.v(lambda e: e.tensor_scalar(m_[:, :nw], m_[:, :nw],
                                          float(2.0 ** 64), None, OP.mult))
            p.v(lambda e: e.memset(inv[:, :nw], float(2.0 ** 67)),
                inc="s_v")
            if not multi:
                p.v(lambda e: e.memset(ef[:, :nw], -67.0), inc="s_v")
            p.wait("vector", "s_v")
            for bit in (64, 32, 16, 8, 4, 2, 1):
                k = min(bit, 16)
                rep = bit // k
                p.v(lambda e, bit=bit: e.tensor_scalar(
                    cmp[:, :nw], m_[:, :nw], float(2.0 ** bit), None,
                    OP.is_ge), inc="s_v")
                p.wait("vector", "s_v")
                p.v(lambda e, k=k: e.tensor_scalar(
                    f_[:, :nw], cmp[:, :nw], float(2.0 ** -k) - 1.0, 1.0,
                    OP.mult, OP.add), inc="s_v")
                if not multi:
                    p.v(lambda e, bit=bit: e.scalar_tensor_tensor(
                        ef[:, :nw], cmp[:, :nw], float(bit), ef[:, :nw],
                        OP.mult, OP.add), inc="s_v")
                p.wait("vector", "s_v")
                for _r in range(rep):
                    p.v(lambda e: e.tensor_tensor(m_[:, :nw], m_[:, :nw],
                                                  f_[:, :nw], OP.mult))
                    p.v(lambda e: e.tensor_tensor(inv[:, :nw], inv[:, :nw],
                                                  f_[:, :nw], OP.mult),
                        inc="s_v")
                    p.wait("vector", "s_v")
            p.v(lambda e: e.reciprocal(G[:, :nw], inv[:, :nw]), inc="s_v")
            p.wait("vector", "s_v")
            if not multi:
                def sop(fn):
                    p.v(fn, inc="s_v")
                    p.wait("vector", "s_v")
                sop(lambda e: e.tensor_scalar(cmp[:, :nw], ef[:, :nw],
                                              -16.5, None, OP.is_ge))
                sop(lambda e: e.tensor_scalar(m_[:, :nw], ef[:, :nw],
                                              -11.5, None, OP.is_le))
                sop(lambda e: e.tensor_tensor(cmp[:, :nw], cmp[:, :nw],
                                              m_[:, :nw], OP.mult))
                sop(lambda e: e.tensor_scalar(m_[:, :nw], ef[:, :nw],
                                              -8.5, None, OP.is_ge))
                sop(lambda e: e.tensor_scalar(amax[:, :nw], ef[:, :nw],
                                              -5.5, None, OP.is_le))
                sop(lambda e: e.tensor_tensor(m_[:, :nw], m_[:, :nw],
                                              amax[:, :nw], OP.mult))
                sop(lambda e: e.tensor_tensor(cmp[:, :nw], cmp[:, :nw],
                                              m_[:, :nw], OP.add))
                sop(lambda e: e.tensor_scalar(m_[:, :nw], ef[:, :nw],
                                              -3.0, None, OP.is_equal))
                sop(lambda e: e.tensor_tensor(cmp[:, :nw], cmp[:, :nw],
                                              m_[:, :nw], OP.add))
                sop(lambda e: e.tensor_scalar(m_[:, :nw], ef[:, :nw],
                                              -11.0, None, OP.is_equal))
                sop(lambda e: e.tensor_scalar(f_[:, :nw], cmp[:, :nw],
                                              -1.0, 1.0, OP.mult, OP.add))
                sop(lambda e: e.tensor_tensor(f_[:, :nw], f_[:, :nw],
                                              m_[:, :nw], OP.subtract))
            cX, nX, wA, w2, rA = (qs[k] for k in
                                  ("cx", "nx", "wa", "w2", "ra"))
            for s in range(n_sec):
                nb0 = s * 32
                src3 = src_fn(s).rearrange("p (b k) -> p b k", k=32)
                inv_b = inv[:, nb0:nb0 + 32].unsqueeze(2).broadcast_to(
                    [128, 32, 32])
                G_b = G[:, nb0:nb0 + 32].unsqueeze(2).broadcast_to(
                    [128, 32, 32])
                u3 = u[:, :].rearrange("p (b k) -> p b k", k=32)
                if not multi:
                    dirZ_b = f_[:, nb0:nb0 + 32].unsqueeze(2).broadcast_to(
                        [128, 32, 32])
                    dirP_b = m_[:, nb0:nb0 + 32].unsqueeze(2).broadcast_to(
                        [128, 32, 32])
                    def wchain(e, s=s, src3=src3, inv_b=inv_b, G_b=G_b,
                               dirZ_b=dirZ_b, dirP_b=dirP_b):
                        e.tensor_tensor(u3, src3, inv_b, OP.mult)
                        e.tensor_scalar(cbuf[:, :], u[:, :], 16.5, None,
                                        OP.add)
                        e.scalar_tensor_tensor(d[:, :], cbuf[:, :], -1.0,
                                               u[:, :], OP.mult, OP.add)
                        e.tensor_scalar(m1[:, :], d[:, :], -15.5, None,
                                        OP.is_ge)
                        e.tensor_scalar(m2[:, :], d[:, :], -16.5, None,
                                        OP.is_lt)
                        e.scalar_tensor_tensor(m1[:, :], m2[:, :], -1.0,
                                               m1[:, :], OP.mult, OP.add)
                        e.tensor_scalar(m2[:, :], d[:, :], -16.5, None,
                                        OP.is_equal)
                        e.tensor_scalar(wA[:, :], u[:, :], 0.0, None,
                                        OP.is_ge)
                        e.tensor_tensor(d[:, :], cbuf[:, :], m1[:, :],
                                        OP.add)
                        e.tensor_scalar(nX[:, :], d[:, :], 0.5, None,
                                        OP.mult)
                        e.tensor_scalar(cX[:, :], nX[:, :], 16.25, None,
                                        OP.add)
                        e.scalar_tensor_tensor(nX[:, :], cX[:, :], -1.0,
                                               nX[:, :], OP.mult, OP.add)
                        e.tensor_scalar(w2[:, :], nX[:, :], -16.0, None,
                                        OP.is_equal)
                        e.tensor_scalar(rA[:, :], w2[:, :], -1.0, 1.0,
                                        OP.mult, OP.add)
                        e.tensor_tensor(w2[:, :], wA[:, :], rA[:, :], OP.mult)
                        e.scalar_tensor_tensor(nX[:, :], w2[:, :], -2.0,
                                               wA[:, :], OP.mult, OP.add)
                        e.tensor_tensor(nX[:, :], nX[:, :], rA[:, :],
                                        OP.add)
                        nx3 = nX[:, :].rearrange("p (b k) -> p b k", k=32)
                        e.tensor_tensor(nx3, nx3, dirZ_b, OP.mult)
                        e.tensor_tensor(nx3, nx3, dirP_b, OP.add)
                        e.tensor_scalar(rA[:, :], wA[:, :], 2.0, -1.0,
                                        OP.mult, OP.add)
                        e.tensor_tensor(nX[:, :], nX[:, :], rA[:, :], OP.mult)
                        e.tensor_tensor(nX[:, :], nX[:, :], wA[:, :],
                                        OP.subtract)
                        e.tensor_tensor(nX[:, :], nX[:, :], m2[:, :],
                                        OP.mult)
                        e.tensor_tensor(d[:, :], d[:, :], nX[:, :], OP.add)
                        d3 = d[:, :].rearrange("p (b k) -> p b k", k=32)
                        return e.scalar_tensor_tensor(
                            dst_fn(s).rearrange("p (b k) -> p b k", k=32), d3,
                            -16.0, G_b, OP.add, OP.mult)
                    p.v(wchain, inc="s_v")
                    p.wait("vector", "s_v")
                else:
                    dstf = dst_fn(s)
                    dst3 = dstf.rearrange("p (b k) -> p b k", k=32)
                    def mchain(e, src3=src3, dstf=dstf, dst3=dst3,
                               inv_b=inv_b, G_b=G_b):
                        e.tensor_tensor(u3, src3, inv_b, OP.mult)
                        e.tensor_scalar(cbuf[:, :], u[:, :], 16.5, None,
                                        OP.add)
                        e.scalar_tensor_tensor(d[:, :], cbuf[:, :], -1.0,
                                               u[:, :], OP.mult, OP.add)
                        e.tensor_scalar(m1[:, :], d[:, :], -15.5, None,
                                        OP.is_ge)
                        e.tensor_scalar(m2[:, :], d[:, :], -16.5, None,
                                        OP.is_lt)
                        e.scalar_tensor_tensor(m1[:, :], m2[:, :], -1.0,
                                               m1[:, :], OP.mult, OP.add)
                        e.tensor_tensor(nX[:, :], cbuf[:, :], m1[:, :],
                                        OP.add)
                        e.tensor_scalar(wA[:, :], d[:, :], -17.5, None,
                                        OP.is_ge)
                        e.tensor_scalar(m2[:, :], d[:, :], -14.5, None,
                                        OP.is_lt)
                        e.tensor_tensor(wA[:, :], wA[:, :], m2[:, :], OP.mult)
                        e.scalar_tensor_tensor(dstf, nX[:, :], -16.0,
                                               wA[:, :], OP.add, OP.mult)
                        e.tensor_scalar(rA[:, :], wA[:, :], -1.0, 1.0,
                                        OP.mult, OP.add)
                        for ci, mulk in enumerate((2.0, 0.5)):
                            e.tensor_scalar(cX[:, :], u[:, :], mulk, 16.5,
                                            OP.mult, OP.add)
                            e.scalar_tensor_tensor(d[:, :], cX[:, :], -1.0,
                                                   u[:, :], OP.mult, OP.add)
                            e.tensor_scalar(m1[:, :], d[:, :], -15.5, None,
                                            OP.is_ge)
                            e.tensor_scalar(m2[:, :], d[:, :], -16.5, None,
                                            OP.is_lt)
                            e.scalar_tensor_tensor(m1[:, :], m2[:, :], -1.0,
                                                   m1[:, :], OP.mult, OP.add)
                            e.tensor_tensor(nX[:, :], cX[:, :], m1[:, :],
                                            OP.add)
                            e.tensor_scalar(w2[:, :], d[:, :], -17.5, None,
                                            OP.is_ge)
                            e.tensor_scalar(m2[:, :], d[:, :], -14.5, None,
                                            OP.is_lt)
                            e.tensor_tensor(w2[:, :], w2[:, :], m2[:, :],
                                            OP.mult)
                            e.tensor_tensor(w2[:, :], rA[:, :], w2[:, :],
                                            OP.mult)
                            e.scalar_tensor_tensor(nX[:, :], nX[:, :], -16.0,
                                                   w2[:, :], OP.add, OP.mult)
                            e.tensor_tensor(dstf, dstf, nX[:, :], OP.add)
                            if ci == 0:
                                e.tensor_tensor(rA[:, :], rA[:, :], w2[:, :],
                                                OP.subtract)
                        return e.tensor_tensor(dst3, dst3, G_b, OP.mult)
                    p.v(mchain, inc="s_v")
                    p.wait("vector", "s_v")

        def emit_ffn0(slo):
            SW = 4
            NSW = NT // SW
            qs_specs = [("q_amax", 128, 128, F32), ("q_m", 128, 128, F32),
                        ("q_inv", 128, 128, F32), ("q_cmp", 128, 128, F32),
                        ("q_f", 128, 128, F32), ("q_G", 128, 128, F32),
                        ("q_u", 128, C, F32), ("q_c", 128, C, I32),
                        ("q_m1", 128, C, BF16), ("q_m2", 128, C, BF16),
                        ("q_nx", 128, C, BF16),
                        ("q_wa", 128, C, BF16), ("q_w2", 128, C, BF16),
                        ("q_ra", 128, C, BF16), ("q_ef", 128, 128, F32)]
            names = ("amax", "m", "inv", "cmp", "f", "G", "u", "c",
                     "m1", "m2", "nx", "wa", "w2", "ra", "ef")
            hs, close_qs = sbuf_group(*qs_specs)
            qs = dict(zip(names, hs))
            qs["d"] = sc_sq
            qs["cx"] = qs["c"]
            (wbuf, wqbuf), close_prep = sbuf_group(
                ("wbuf", 128, 8 * C, F32), ("wqbuf", 128, 8 * C, BF16))
            join_all()
            for widx, (wsrc, whbm, rows, cols) in enumerate(
                    ((l0_wfc, wfcq_hbm, FFN0, C), (l0_wp, wprojq_hbm, C, FFN0))):
                ntile_r = rows // 128
                secs_per_tile = cols // C
                for q4 in range(4):
                    nt_q = ntile_r // 4
                    p.wait("gpsimd", "s_out")
                    def ldw(e, wsrc=wsrc, q4=q4, nt_q=nt_q, cols=cols):
                        return e.dma_start(
                            out=bass.AP(wbuf, 0, [[8 * C, 128], [cols, nt_q],
                                                  [1, cols]]),
                            in_=bass.AP(wsrc.tensor, q4 * nt_q * 128 * cols,
                                        [[cols, 128], [128 * cols, nt_q],
                                         [1, cols]]))
                    p.wait("sync", "s_v")
                    p.wait("sync", "s_g")
                    p.y(ldw, inc="s_ld", n=16)
                    p.wait("vector", "s_ld")
                    p.wait("gpsimd", "s_ld")
                    nsec = nt_q * secs_per_tile
                    for off in range(0, nsec, 4):
                        kk = min(4, nsec - off)
                        emit_quant(
                            lambda s, off=off:
                                wbuf[:, (off + s) * C:(off + s + 1) * C],
                            lambda s, off=off:
                                wqbuf[:, (off + s) * C:(off + s + 1) * C],
                            qs, n_sec=kk)
                    def stw(e, whbm=whbm, q4=q4, nt_q=nt_q, cols=cols):
                        return e.dma_start(
                            out=bass.AP(whbm.tensor, q4 * nt_q * 128 * cols,
                                        [[cols, 128], [128 * cols, nt_q],
                                         [1, cols]]),
                            in_=bass.AP(wqbuf, 0, [[8 * C, 128], [cols, nt_q],
                                                   [1, cols]]))
                    p.wait("gpsimd", "s_v")
                    p.g(stw, inc="s_gp", n=16)
            close_prep()
            (xq_sw, wfcqT, wpjT, g_acc, hbuf, hq, hqT,
             bfcb), close_sw = sbuf_group(
                ("xq_sw", 128, NCH * SW * 128, BF16),
                ("wfcqT", 128, 8 * C, BF16),
                ("wpjT", 128, 4 * C, BF16),
                ("g_acc", 128, SW * C, F32),
                ("hbuf", 128, 4 * C, F32),
                ("hq", 128, 4 * C, BF16),
                ("hqT", 128, 512, BF16),
                ("bfcb", 128, C, BF16))
            bpjb = bfcb
            pfc, ppj, ptq = psum_phase(("pfc", 128, 1024, F32),
                                       ("ppj", 128, 1024, F32),
                                       ("ptq", 128, 4096, BF16))
            join_all()
            emit_bcast(bpjb, l0_bp)
            st_guard = {"v": 0}

            def tq_transposes(stens, scol0, sstride, dtens, dcol0, dstride, n):
                for g0 in range(0, n, 4):
                    k = min(4, n - g0)
                    p.wait("tensor", "s_v", st_guard["v"])
                    p.wait("tensor", "s_a")
                    p.wait("tensor", "s_v")
                    def trq(e, g0=g0, k=k, stens=stens, scol0=scol0,
                            sstride=sstride):
                        for j in range(k):
                            e.transpose(ptq[:, j * 1024:j * 1024 + 128],
                                        stens[:, scol0 + (g0 + j) * sstride:
                                              scol0 + (g0 + j) * sstride + 128],
                                        cb["c_ident"][:, :])
                        return e.nop()
                    p.t(trq, inc="s_t")
                    p.wait("scalar", "s_t", p.c("s_t"))
                    def evq(e, g0=g0, k=k, dtens=dtens, dcol0=dcol0,
                            dstride=dstride):
                        last = None
                        for j in range(k):
                            last = e.copy(
                                dtens[:, dcol0 + (g0 + j) * dstride:
                                      dcol0 + (g0 + j) * dstride + 128],
                                ptq[:, j * 1024:j * 1024 + 128])
                        return last
                    p.s(evq, inc="s_v")
                    st_guard["v"] = p.c("s_v")

            for sw in range(NSW):
                tiles = list(range(sw * SW, (sw + 1) * SW))
                p.wait("vector", "s_a")
                def stage_xn(e, tiles=tiles):
                    last = None
                    for tloc, ti in enumerate(tiles):
                        last = e.tensor_scalar(
                            hbuf[:, tloc * C:(tloc + 1) * C],
                            x_tm[:, ti * C:(ti + 1) * C],
                            stat[:, 2 * ti:2 * ti + 1],
                            stat[:, 2 * ti + 1:2 * ti + 2],
                            OP.subtract, OP.mult)
                    return last
                p.v(stage_xn, inc="s_v")
                p.wait("vector", "s_v")
                emit_quant(lambda s: hbuf[:, s * C:(s + 1) * C],
                           lambda s: hq[:, s * C:(s + 1) * C],
                           qs, n_sec=4, multi=True)
                for tloc in range(4):
                    tq_transposes(hq, tloc * C, 128, xq_sw, tloc * 128,
                                  SW * 128, NCH)
                if debug_name == "xq0" and sw == 0:
                    join_all(("gpsimd",))
                    p.dma("gpsimd",
                          lambda: bass.AP(dbg.tensor, 0,
                                          [[4096, 128], [1, 4096]]),
                          lambda: xq_sw[:, :], "s_gp")
                    p.wait("gpsimd", "s_gp", p.c("s_gp"))
                for q4 in range(4):
                    p.wait("sync", "s_t")
                    for c in range(NCH):
                        p.y(lambda e, c=c, q4=q4: e.dma_start_transpose(
                            wfcqT[:, c * C:(c + 1) * C],
                            bass.AP(wfcq_hbm.tensor, q4 * 1024 * C + c * 128,
                                    [[C, 1024], [1, 128]])),
                            inc="s_sc", n=16)
                    p.wait("gpsimd", "s_v")
                    emit_bcast(bfcb, l0_bfc, off=q4 * C)
                    wload = p.c("s_sc")
                    bload = p.c("s_gp")
                    for tloc in range(4):
                        p.wait("tensor", "s_sc", wload)
                        p.wait("tensor", "s_v")
                        for h in range(2):
                            def mfc(e, h=h, tloc=tloc):
                                last = None
                                for c in range(NCH):
                                    last = e.matmul(
                                        pfc[:, h * 512:(h + 1) * 512],
                                        xq_sw[:, c * SW * 128 + tloc * 128:
                                              c * SW * 128 + (tloc + 1) * 128],
                                        wfcqT[:, c * C + h * 512:
                                              c * C + (h + 1) * 512],
                                        start=(c == 0), stop=(c == NCH - 1))
                                return last
                            p.t(mfc, inc="s_t")
                        fc_done = p.c("s_t")
                        p.wait("vector", "s_t", fc_done)
                        p.wait("vector", "s_gp", bload)
                        def evfc(e, tloc=tloc):
                            e.scalar_tensor_tensor(
                                hbuf[:, tloc * C:tloc * C + 512],
                                pfc[:, 0:512], 1.0, bfcb[:, 0:512],
                                OP.mult, OP.add)
                            return e.scalar_tensor_tensor(
                                hbuf[:, tloc * C + 512:(tloc + 1) * C],
                                pfc[:, 512:1024], 1.0, bfcb[:, 512:1024],
                                OP.mult, OP.add)
                        p.v(evfc, inc="s_v")
                    p.wait("scalar", "s_v")
                    def gelus(e):
                        last = None
                        for tloc in range(4):
                            last = e.activation(
                                hbuf[:, tloc * C:(tloc + 1) * C],
                                hbuf[:, tloc * C:(tloc + 1) * C], AF.Gelu)
                        return last
                    p.s(gelus, inc="s_a")
                    p.wait("vector", "s_a")
                    emit_quant(lambda s: hbuf[:, s * C:(s + 1) * C],
                               lambda s: hq[:, s * C:(s + 1) * C],
                               qs, n_sec=4, multi=True)
                    if debug_name == "h159" and sw == 0:
                        join_all(("gpsimd",))
                        p.dma("gpsimd",
                              lambda q4=q4: bass.AP(
                                  dbg.tensor, q4 * 128 * 1024,
                                  [[1024, 128], [1, 1024]]),
                              lambda: hbuf[:, C:2 * C], "s_gp")
                        p.dma("gpsimd",
                              lambda q4=q4: bass.AP(
                                  dbg.tensor, 512 * 1024 + q4 * 128 * 1024,
                                  [[1024, 128], [1, 1024]]),
                              lambda: hq[:, C:2 * C], "s_gp")
                        p.wait("gpsimd", "s_gp", p.c("s_gp"))
                        p.wait("vector", "s_gp", p.c("s_gp"))
                        p.wait("tensor", "s_gp", p.c("s_gp"))
                    for wh in range(2):
                        p.wait("sync", "s_t")
                        for j in range(4):
                            nglob = q4 * 8 + wh * 4 + j
                            p.y(lambda e, j=j, nglob=nglob:
                                e.dma_start_transpose(
                                    wpjT[:, j * C:(j + 1) * C],
                                    bass.AP(wprojq_hbm.tensor, nglob * 128,
                                            [[FFN0, C], [1, 128]])),
                                inc="s_sc", n=16)
                        wpl = p.c("s_sc")
                        for tloc in range(4):
                            tq_transposes(hq, tloc * C + wh * 512, 128,
                                          hqT, 0, 128, 4)
                            p.wait("tensor", "s_sc", wpl)
                            p.wait("tensor", "s_v")
                            for h in range(2):
                                def mpj(e, h=h, wh=wh):
                                    last = None
                                    for j in range(4):
                                        last = e.matmul(
                                            ppj[:, h * 512:(h + 1) * 512],
                                            hqT[:, j * 128:(j + 1) * 128],
                                            wpjT[:, j * C + h * 512:
                                                 j * C + (h + 1) * 512],
                                            start=(j == 0), stop=(j == 3))
                                    return last
                                p.t(mpj, inc="s_t")
                            pj_done = p.c("s_t")
                            p.wait("vector", "s_t", pj_done)
                            def evpj(e, tloc=tloc, first=(q4 == 0 and wh == 0)):
                                last = None
                                for h in range(2):
                                    ga = g_acc[:, tloc * C + h * 512:
                                               tloc * C + (h + 1) * 512]
                                    if first:
                                        last = e.tensor_copy(
                                            ga, ppj[:, h * 512:(h + 1) * 512])
                                    else:
                                        last = e.scalar_tensor_tensor(
                                            ga, ppj[:, h * 512:(h + 1) * 512],
                                            1.0, ga, OP.mult, OP.add)
                                return last
                            p.v(evpj, inc="s_v")
                            p.wait("tensor", "s_v")
                p.wait("gpsimd", "s_v")
                emit_bcast(bpjb, l0_bp)
                for tloc, ti in enumerate(tiles):
                    ga = lambda tloc=tloc: g_acc[:, tloc * C:(tloc + 1) * C]
                    p.wait("vector", "s_gp")
                    p.v(lambda e, tloc=tloc: e.tensor_tensor(
                        g_acc[:, tloc * C:(tloc + 1) * C],
                        g_acc[:, tloc * C:(tloc + 1) * C],
                        bpjb[:, :], OP.add), inc="s_v")
                    p.wait("scalar", "s_v")
                    p.s(lambda e, tloc=tloc: e.activation(
                        g_acc[:, tloc * C:(tloc + 1) * C],
                        g_acc[:, tloc * C:(tloc + 1) * C],
                        AF.Silu), inc="s_a")
                    p.wait("vector", "s_a")
                    hb = hbuf[:, tloc * C:(tloc + 1) * C]
                    if ti < 4:
                        u = qs["u"]
                        p.v(lambda e, ti=ti: e.tensor_tensor(
                            u[:, :], ctx_tm[:, ti * C:(ti + 1) * C],
                            slo[:, ti * C:(ti + 1) * C], OP.add), inc="s_v")
                        p.wait("vector", "s_v")
                        p.v(lambda e, ti=ti, tloc=tloc, u=u, hb=hb:
                            e.tensor_tensor(
                                hb, u[:, :],
                                g_acc[:, tloc * C:(tloc + 1) * C], OP.mult),
                            inc="s_v")
                    else:
                        p.v(lambda e, ti=ti, tloc=tloc, hb=hb: e.tensor_tensor(
                            hb, ctx_tm[:, ti * C:(ti + 1) * C],
                            g_acc[:, tloc * C:(tloc + 1) * C], OP.mult),
                            inc="s_v")
                    p.wait("vector", "s_v")
                    p.v(lambda e, ti=ti, hb=hb: e.tensor_tensor(
                        x_tm[:, ti * C:(ti + 1) * C],
                        x_tm[:, ti * C:(ti + 1) * C], hb, OP.add),
                        inc="s_v")
                    p.wait("vector", "s_v")
            join_all()
            close_sw()
            close_qs()

        emit_ffn0(slo)
        close_slo()
        if debug_name == "x1d":
            join_all(("gpsimd",))
            p.dma("gpsimd", lambda: dbg.rearrange("(p a) m -> p (a m)", p=128),
                  lambda: x_tm[:, :], "s_gp")
            p.wait("gpsimd", "s_gp", p.c("s_gp"))

        def emit_ffn1(xnT):
            HQ = 4
            HCQ = FFN1 // 128 // HQ
            TQ4 = 4
            (wtmp,), close_wt = sbuf_group(("wtmp", 128, 2 * FFN1, BF16))
            join_all()
            for wsrc, wdst, rows, cols in ((l1_wg, wg_bf_hbm, FFN1, C),
                                           (l1_wu, wu_bf_hbm, FFN1, C),
                                           (l1_wd, wd_bf_hbm, C, FFN1)):
                ntr = rows // 128
                per = (2 * FFN1) // cols
                for t0 in range(0, ntr, per):
                    k = min(per, ntr - t0)
                    p.wait("gpsimd", "s_gp")
                    p.g(lambda e, wsrc=wsrc, t0=t0, k=k, cols=cols:
                        e.dma_start(
                            out=bass.AP(wtmp, 0, [[2 * FFN1, 128], [cols, k],
                                                  [1, cols]]),
                            in_=bass.AP(wsrc.tensor, t0 * 128 * cols,
                                        [[cols, 128], [128 * cols, k],
                                         [1, cols]])), inc="s_gp", n=16)
                    p.wait("gpsimd", "s_gp")
                    p.g(lambda e, wdst=wdst, t0=t0, k=k, cols=cols:
                        e.dma_start(
                            out=bass.AP(wdst.tensor, t0 * 128 * cols,
                                        [[cols, 128], [128 * cols, k],
                                         [1, cols]]),
                            in_=bass.AP(wtmp, 0, [[2 * FFN1, 128], [cols, k],
                                                  [1, cols]])), inc="s_gp",
                        n=16)
            close_wt()
            (wgT, wuT, wdTq, hh, hl, hf, g_acc, sg, su), close_f1 = sbuf_group(
                ("wgT", 128, NCH * 768, BF16),
                ("wuT", 128, NCH * 768, BF16),
                ("wdTq", 128, HCQ * C, BF16),
                ("hh", 128, HCQ * 512, BF16),
                ("hl", 128, HCQ * 512, BF16),
                ("hf", 128, 512, F32),
                ("g1_acc", 128, 4 * C, F32),
                ("sg1", 128, 512, F32),
                ("su1", 128, 512, F32))
            pgu, pdn = psum_phase(("pgu", 128, 1024, F32),
                                  ("pdn", 128, 1024, F32))
            join_all()
            for tq4 in range(TQ4):
                tcol0 = tq4 * 512
                for hq in range(HQ):
                    p.wait("sync", "s_t")
                    p.wait("sync", "s_v")
                    for c in range(NCH):
                        p.y(lambda e, c=c, hq=hq: e.dma_start_transpose(
                            wgT[:, c * 768:(c + 1) * 768],
                            bass.AP(wg_bf_hbm.tensor, hq * 768 * C + c * 128,
                                    [[C, 768], [1, 128]])), inc="s_sc", n=16)
                        p.y(lambda e, c=c, hq=hq: e.dma_start_transpose(
                            wuT[:, c * 768:(c + 1) * 768],
                            bass.AP(wu_bf_hbm.tensor, hq * 768 * C + c * 128,
                                    [[C, 768], [1, 128]])), inc="s_sc", n=16)
                    for hcl in range(HCQ):
                        hc = hq * HCQ + hcl
                        p.y(lambda e, hcl=hcl, hc=hc: e.dma_start_transpose(
                            wdTq[:, hcl * C:(hcl + 1) * C],
                            bass.AP(wd_bf_hbm.tensor, hc * 128,
                                    [[FFN1, C], [1, 128]])), inc="s_sc", n=16)
                    wload = p.c("s_sc")
                    p.wait("tensor", "s_sc", wload)
                    for hcl in range(HCQ):
                        p.wait("tensor", "s_a")
                        p.wait("tensor", "s_v")
                        for which, wT, ps0 in (("g", wgT, 0), ("u", wuT, 512)):
                            def mgu(e, wT=wT, ps0=ps0, hcl=hcl, tcol0=tcol0):
                                last = None
                                for c in range(NCH):
                                    last = e.matmul(
                                        pgu[:, ps0:ps0 + 512],
                                        wT[:, c * 768 + hcl * 128:
                                           c * 768 + (hcl + 1) * 128],
                                        xnT[:, c * T + tcol0:
                                            c * T + tcol0 + 512],
                                        start=(c == 0), stop=(c == NCH - 1))
                                return last
                            p.t(mgu, inc="s_t")
                        gu_done = p.c("s_t")
                        p.wait("vector", "s_t", gu_done)
                        def evgu(e):
                            e.tensor_scalar(hf[:, :], pgu[:, 0:512], C127L,
                                            None, OP.mult)
                            e.scalar_tensor_tensor(
                                sg[:, :], pgu[:, 0:512], C127H, hf[:, :],
                                OP.mult, OP.add)
                            e.tensor_scalar(hf[:, :], pgu[:, 512:1024], C127L,
                                            None, OP.mult)
                            return e.scalar_tensor_tensor(
                                su[:, :], pgu[:, 512:1024], C127H, hf[:, :],
                                OP.mult, OP.add)
                        p.v(evgu, inc="s_v")
                        p.wait("scalar", "s_v")
                        p.s(lambda e: e.activation(sg[:, :], sg[:, :],
                                                   AF.Silu), inc="s_a")
                        p.wait("vector", "s_a")
                        def evhh(e, hcl=hcl):
                            e.tensor_tensor(hf[:, :], sg[:, :], su[:, :],
                                            OP.mult)
                            e.tensor_copy(hh[:, hcl * 512:(hcl + 1) * 512],
                                          hf[:, :])
                            return e.scalar_tensor_tensor(
                                hl[:, hcl * 512:(hcl + 1) * 512],
                                hh[:, hcl * 512:(hcl + 1) * 512], -1.0,
                                hf[:, :], OP.mult, OP.add)
                        p.v(evhh, inc="s_v")
                    hh_done = p.c("s_v")
                    p.wait("tensor", "s_v", hh_done)
                    for tloc in range(4):
                        ti = tq4 * 4 + tloc
                        for h in range(2):
                            def mdn(e, tloc=tloc, h=h):
                                last = None
                                for part in (hh, hl):
                                    for hcl in range(HCQ):
                                        last = e.matmul(
                                            pdn[:, h * 512:(h + 1) * 512],
                                            part[:, hcl * 512 + tloc * 128:
                                                 hcl * 512 + (tloc + 1) * 128],
                                            wdTq[:, hcl * C + h * 512:
                                                 hcl * C + (h + 1) * 512],
                                            start=(part is hh and hcl == 0),
                                            stop=(part is hl and
                                                  hcl == HCQ - 1))
                                return last
                            p.t(mdn, inc="s_t")
                        dn_done = p.c("s_t")
                        p.wait("vector", "s_t", dn_done)
                        for h in range(2):
                            if hq == 0:
                                p.v(lambda e, h=h, tloc=tloc: e.tensor_copy(
                                    g_acc[:, tloc * C + h * 512:
                                          tloc * C + (h + 1) * 512],
                                    pdn[:, h * 512:(h + 1) * 512]), inc="s_v")
                            else:
                                p.v(lambda e, h=h, tloc=tloc:
                                    e.scalar_tensor_tensor(
                                        g_acc[:, tloc * C + h * 512:
                                              tloc * C + (h + 1) * 512],
                                        pdn[:, h * 512:(h + 1) * 512], 1.0,
                                        g_acc[:, tloc * C + h * 512:
                                              tloc * C + (h + 1) * 512],
                                        OP.mult, OP.add), inc="s_v")
                        p.wait("tensor", "s_v")
                if debug_name == "f1d":
                    join_all(("gpsimd",))
                    p.dma("gpsimd",
                          lambda tq4=tq4: bass.AP(
                              dbg.tensor, tq4 * 128 * 4096,
                              [[4096, 128], [1, 4096]]),
                          lambda: g_acc[:, :], "s_gp")
                    p.wait("gpsimd", "s_gp", p.c("s_gp"))
                    p.wait("vector", "s_gp", p.c("s_gp"))
                    p.wait("scalar", "s_gp", p.c("s_gp"))
                for tloc in range(4):
                    ti = tq4 * 4 + tloc
                    ga = g_acc[:, tloc * C:(tloc + 1) * C]
                    def ev127(e, ga=ga):
                        e.tensor_scalar(sc_sq[:, :], ga, C127L, None, OP.mult)
                        return e.scalar_tensor_tensor(
                            ga, ga, C127H, sc_sq[:, :], OP.mult, OP.add)
                    p.v(ev127, inc="s_v")
                    p.wait("scalar", "s_v")
                    p.s(lambda e, ga=ga: e.activation(
                        ga, ga, AF.Silu), inc="s_a")
                    p.wait("vector", "s_a")
                    p.v(lambda e, ti=ti, ga=ga: e.tensor_tensor(
                        sc_sq[:, :], ctx_tm[:, ti * C:(ti + 1) * C],
                        ga, OP.mult), inc="s_v")
                    p.wait("vector", "s_v")
                    p.v(lambda e, ti=ti: e.tensor_tensor(
                        x_tm[:, ti * C:(ti + 1) * C],
                        x_tm[:, ti * C:(ti + 1) * C], sc_sq[:, :],
                        OP.add), inc="s_v")
                    p.wait("vector", "s_v")
            join_all()
            close_f1()

        (xnT1,), close_xnT1 = sbuf_group(("xnT1", 128, NCH * T, BF16))
        emit_ln_xnT(1e-5, xnT1)
        emit_attention(1, xnT1)
        if debug_name == "ctx1":
            join_all(("gpsimd",))
            p.dma("gpsimd", lambda: dbg.rearrange("(p a) m -> p (a m)", p=128),
                  lambda: ctx_tm[:, :], "s_gp")
            p.wait("gpsimd", "s_gp", p.c("s_gp"))
        emit_attn_proj(1, xnT1)
        if debug_name == "s1":
            join_all(("gpsimd",))
            p.dma("gpsimd", lambda: dbg.rearrange("(p a) m -> p (a m)", p=128),
                  lambda: ctx_tm[:, :], "s_gp")
            p.wait("gpsimd", "s_gp", p.c("s_gp"))
        emit_ffn1(xnT1)
        close_xnT1()

        if debug_name == "xnt0":
            dump_dbg(lambda: xnT[:, :])
        if debug_name == "ctx0":
            dump_dbg(lambda: ctx_tm[:, :])

        join_all(("sync",))
        p.dma("sync",
              lambda: out.rearrange("(i p) m -> p i m", p=128),
              lambda: x_tm[:, :].rearrange("p (i m) -> p i m", m=C),
              "s_out")
        out_done = p.c("s_out")
        p.wait("gpsimd", "s_out", out_done)

        with contextlib.ExitStack() as es2:
            sems = {s: es2.enter_context(nc.semaphore(s)) for s in SEMS}
            p.sems = sems
            with nc.Block() as block:
                @block.tensor
                def _(e):
                    for f in p.ops["tensor"]:
                        f(e)

                @block.vector
                def _(e):
                    for f in p.ops["vector"]:
                        f(e)

                @block.scalar
                def _(e):
                    for f in p.ops["scalar"]:
                        f(e)

                @block.gpsimd
                def _(e):
                    for f in p.ops["gpsimd"]:
                        f(e)

                @block.sync
                def _(e):
                    for f in p.ops["sync"]:
                        f(e)
    return nc


def _prep_inputs(inputs):
    d = dict(inputs)
    feed = {}
    feed["L0_attn_w_qkv"] = np.ascontiguousarray(d["L0_attn_w_qkv"], np.float32)
    feed["L0_attn_w_proj"] = np.ascontiguousarray(d["L0_attn_w_proj"], np.float32)
    feed["L0_ffn_w_fc"] = np.ascontiguousarray(d["L0_ffn_w_fc"], np.float32)
    feed["L0_ffn_b_fc"] = np.ascontiguousarray(d["L0_ffn_b_fc"], np.float32)
    feed["L0_ffn_w_proj"] = np.ascontiguousarray(d["L0_ffn_w_proj"], np.float32)
    feed["L0_ffn_b_proj"] = np.ascontiguousarray(d["L0_ffn_b_proj"], np.float32)
    feed["L1_attn_Wq"] = np.ascontiguousarray(d["L1_attn_Wq"], np.float32).reshape(C, C)
    feed["L1_attn_bq"] = np.ascontiguousarray(d["L1_attn_bq"], np.float32).reshape(C)
    feed["L1_attn_Wk"] = np.ascontiguousarray(d["L1_attn_Wk"], np.float32).reshape(C, C)
    feed["L1_attn_bk"] = np.ascontiguousarray(d["L1_attn_bk"], np.float32).reshape(C)
    feed["L1_attn_Wv"] = np.ascontiguousarray(d["L1_attn_Wv"], np.float32).reshape(C, C)
    feed["L1_attn_bv"] = np.ascontiguousarray(d["L1_attn_bv"], np.float32).reshape(C)
    feed["L1_attn_Wo"] = np.ascontiguousarray(d["L1_attn_Wo"], np.float32).reshape(C, C)
    feed["L1_attn_bo"] = np.ascontiguousarray(d["L1_attn_bo"], np.float32)
    feed["L1_attn_gamma"] = np.ascontiguousarray(d["L1_attn_gamma"], np.float32)
    feed["L1_attn_beta"] = np.ascontiguousarray(d["L1_attn_beta"], np.float32)
    feed["L1_ffn_w_gate"] = np.ascontiguousarray(d["L1_ffn_w_gate"], np.int32)
    feed["L1_ffn_w_up"] = np.ascontiguousarray(d["L1_ffn_w_up"], np.int32)
    feed["L1_ffn_w_down"] = np.ascontiguousarray(d["L1_ffn_w_down"], np.int32)
    feed.update(_host_consts())
    xs = np.ascontiguousarray(np.asarray(d["L0_attn_x"], np.float32))
    return feed, xs


def kernel(**inputs):
    feed, xs = _prep_inputs(inputs)
    nc = build()
    in_maps = []
    for b in range(8):
        m = dict(feed)
        m["x"] = xs[b]
        in_maps.append(m)
    res = run_bass_kernel_spmd(nc, in_maps, core_ids=list(range(8)))
    return np.stack([np.asarray(res.results[b]["out"]) for b in range(8)], axis=0)


# version 19
# speedup vs baseline: 1.3353x; 1.3353x over previous
import contextlib
import numpy as np
import ml_dtypes
import concourse.bass as bass
import concourse.mybir as mybir
from concourse.bass_utils import run_bass_kernel_spmd

F32 = mybir.dt.float32
BF16 = mybir.dt.bfloat16
I32 = mybir.dt.int32
OP = mybir.AluOpType
AF = mybir.ActivationFunctionType
AXX = mybir.AxisListType.X
BF = ml_dtypes.bfloat16

T = 2048
C = 1024
NT = T // 128
NCH = C // 128
NHP = 8
D = 64
U0 = 12.0
FFN0 = 4096
FFN1 = 3072
NEG = -1.0e30
C127H = float(np.float32(1.0 / 127.0))
C127L = float(np.float32(1.0 / 127.0 - np.float64(np.float32(1.0 / 127.0))))

SEMS = ("s_ld", "s_gp", "s_sc", "s_t", "s_v", "s_a", "s_g", "s_out")


class Prog:

    def __init__(self):
        self.ops = {k: [] for k in ("tensor", "vector", "scalar", "gpsimd", "sync")}
        self.cnt = {s: 0 for s in SEMS}
        self.sems = None

    def _add(self, eng, fn, inc=None, n=1):
        tgt = None
        if inc is not None:
            self.cnt[inc] += n
            tgt = self.cnt[inc]
        sem = inc

        def emit(e):
            ins = fn(e)
            if sem is not None:
                ins.then_inc(self.sems[sem], n)
        self.ops[eng].append(emit)
        return tgt

    def t(self, fn, inc=None, n=1):
        return self._add("tensor", fn, inc, n)

    def v(self, fn, inc=None, n=1):
        return self._add("vector", fn, inc, n)

    def s(self, fn, inc=None, n=1):
        return self._add("scalar", fn, inc, n)

    def g(self, fn, inc=None, n=1):
        return self._add("gpsimd", fn, inc, n)

    def y(self, fn, inc=None, n=1):
        return self._add("sync", fn, inc, n)

    def dma(self, eng, out_ap_fn, in_ap_fn, sem):
        return self._add(eng, lambda e: e.dma_start(out=out_ap_fn(), in_=in_ap_fn()),
                         sem, 16)

    def wait(self, eng, sem, val=None):
        v = self.cnt[sem] if val is None else val
        self.ops[eng].append(lambda e: e.wait_ge(self.sems[sem], v))

    def c(self, sem):
        return self.cnt[sem]


def _host_consts():
    half = D // 2
    theta = 1.0 / (10000.0 ** (np.arange(half, dtype=np.float64) / half))
    pos = np.arange(T, dtype=np.float64)
    freqs = pos[:, None] * theta[None, :]
    cosv = np.cos(freqs)
    sinv = np.sin(freqs)
    idx = np.arange(D) // 2
    cos_fm = cosv[:, idx].T
    sin_fm = sinv[:, idx].T
    cos_fm = np.concatenate([cos_fm, cos_fm], 0)
    sin_fm = np.concatenate([sin_fm, sin_fm], 0)
    cos_hi = cos_fm.astype(np.float32)
    sin_hi = sin_fm.astype(np.float32)
    cos_lo = (cos_fm - cos_hi).astype(np.float32)
    sin_lo = (sin_fm - sin_hi).astype(np.float32)
    perm = np.zeros((128, 128), dtype=np.float32)
    for m in range(128):
        base = (m // 64) * 64
        j = m % 64
        if j < 32:
            perm[base + j + 32, m] = -1.0
        else:
            perm[base + j - 32, m] = 1.0
    ident = np.eye(128, dtype=np.float32)
    out = {
        "c_cos_hi": cos_hi, "c_sin_hi": sin_hi,
        "c_cos_lo": np.asarray(cos_lo), "c_sin_lo": np.asarray(sin_lo),
        "c_perm": perm.astype(BF), "c_ident": ident.astype(BF),
    }
    r = np.arange(128)[:, None]
    cc = np.arange(256)[None, :]
    for o in range(2):
        m = np.zeros((128, 256), dtype=np.float32)
        m[(o * 128 + r > cc)] = NEG
        out[f"c_mask{o}"] = m.astype(BF)
    return out


def _ln_stats(p, xt, sc_sq, sums, sumsq, stat, eps, i):
    p.wait("scalar", "s_v")
    p.v(lambda e: e.tensor_reduce(sums[:, i:i + 1], xt, AXX, OP.add), inc="s_v")
    p.s(lambda e: e.activation(sc_sq[:, :], xt, AF.Square,
                               accum_out=sumsq[:, i:i + 1]), inc="s_a")
    p.wait("vector", "s_a")
    p.wait("vector", "s_v")
    mean = stat[:, 2 * i:2 * i + 1]
    rstd = stat[:, 2 * i + 1:2 * i + 2]
    p.v(lambda e: e.tensor_scalar(mean, sums[:, i:i + 1], 1.0 / C, None, OP.mult))
    p.v(lambda e: e.tensor_scalar(rstd, sumsq[:, i:i + 1], 1.0 / C, None,
                                  OP.mult), inc="s_v")
    p.wait("vector", "s_v")
    p.v(lambda e: e.tensor_tensor(sums[:, i:i + 1], mean, mean, OP.mult),
        inc="s_v")
    p.wait("vector", "s_v")
    p.v(lambda e: e.tensor_tensor(rstd, rstd, sums[:, i:i + 1], OP.subtract),
        inc="s_v")
    p.wait("vector", "s_v")
    p.v(lambda e: e.tensor_scalar(rstd, rstd, float(eps), None, OP.add),
        inc="s_v")
    p.wait("scalar", "s_v")
    p.s(lambda e: e.activation(rstd, rstd, AF.Ln), inc="s_a")
    p.wait("scalar", "s_a")
    p.s(lambda e: e.activation(rstd, rstd, AF.Exp, scale=-0.5), inc="s_a")
    return mean, rstd


def build(n_layer_stop=None, debug_name=None):
    nc = bass.Bass("TRN2", target_bir_lowering=False,
                   detect_race_conditions=False)

    x_in = nc.dram_tensor("x", [T, C], F32, kind="ExternalInput").ap()
    out = nc.dram_tensor("out", [T, C], F32, kind="ExternalOutput").ap()
    dbg = None
    if debug_name is not None:
        dbg = nc.dram_tensor("dbg", [T, C], F32, kind="ExternalOutput").ap()

    w_qkv = nc.dram_tensor("L0_attn_w_qkv", [C, 3 * C], F32, kind="ExternalInput").ap()
    l0_wproj = nc.dram_tensor("L0_attn_w_proj", [C, C], F32, kind="ExternalInput").ap()
    l0_wfc = nc.dram_tensor("L0_ffn_w_fc", [FFN0, C], F32, kind="ExternalInput").ap()
    l0_bfc = nc.dram_tensor("L0_ffn_b_fc", [FFN0], F32, kind="ExternalInput").ap()
    l0_wp = nc.dram_tensor("L0_ffn_w_proj", [C, FFN0], F32, kind="ExternalInput").ap()
    l0_bp = nc.dram_tensor("L0_ffn_b_proj", [C], F32, kind="ExternalInput").ap()
    l1_wq = nc.dram_tensor("L1_attn_Wq", [C, C], F32, kind="ExternalInput").ap()
    l1_bq = nc.dram_tensor("L1_attn_bq", [C], F32, kind="ExternalInput").ap()
    l1_wk = nc.dram_tensor("L1_attn_Wk", [C, C], F32, kind="ExternalInput").ap()
    l1_bk = nc.dram_tensor("L1_attn_bk", [C], F32, kind="ExternalInput").ap()
    l1_wv = nc.dram_tensor("L1_attn_Wv", [C, C], F32, kind="ExternalInput").ap()
    l1_bv = nc.dram_tensor("L1_attn_bv", [C], F32, kind="ExternalInput").ap()
    l1_wo = nc.dram_tensor("L1_attn_Wo", [C, C], F32, kind="ExternalInput").ap()
    l1_bo = nc.dram_tensor("L1_attn_bo", [C], F32, kind="ExternalInput").ap()
    l1_gamma = nc.dram_tensor("L1_attn_gamma", [C], F32, kind="ExternalInput").ap()
    l1_beta = nc.dram_tensor("L1_attn_beta", [C], F32, kind="ExternalInput").ap()
    l1_wg = nc.dram_tensor("L1_ffn_w_gate", [FFN1, C], I32, kind="ExternalInput").ap()
    l1_wu = nc.dram_tensor("L1_ffn_w_up", [FFN1, C], I32, kind="ExternalInput").ap()
    l1_wd = nc.dram_tensor("L1_ffn_w_down", [C, FFN1], I32, kind="ExternalInput").ap()

    cmaps = _host_consts()
    cdram = {k: nc.dram_tensor(k, list(v.shape),
                               BF16 if v.dtype == BF else F32,
                               kind="ExternalInput").ap()
             for k, v in cmaps.items()}
    wfcq_hbm = nc.dram_tensor("wfcq_scratch", [FFN0, C], BF16).ap()
    wprojq_hbm = nc.dram_tensor("wprojq_scratch", [C, FFN0], BF16).ap()
    wg_bf_hbm = nc.dram_tensor("wg_bf_scratch", [FFN1, C], BF16).ap()
    wu_bf_hbm = nc.dram_tensor("wu_bf_scratch", [FFN1, C], BF16).ap()
    wd_bf_hbm = nc.dram_tensor("wd_bf_scratch", [C, FFN1], BF16).ap()

    p = Prog()
    es = contextlib.ExitStack()
    with es:
        def sb(name, part, cols, dt):
            return es.enter_context(nc.sbuf_tensor(name, [part, cols], dt))

        def psum(name, part, cols, dt=F32):
            return es.enter_context(nc.psum_tensor(name, [part, cols], dt))

        def sbuf_group(*specs):
            hs, cms = [], []
            for (name, part, cols, dt) in specs:
                cm = nc.sbuf_tensor(name, [part, cols], dt)
                hs.append(cm.__enter__())
                cms.append(cm)

            def close():
                for cm in reversed(cms):
                    cm.__exit__(None, None, None)
            return hs, close

        x_tm = sb("x_tm", 128, NT * C, F32)
        ctx_tm = sb("ctx_tm", 128, NT * C, BF16)
        cb = {"c_ident": sb("c_ident_t", 128, 128, BF16)}
        sums = sb("sums", 128, NT, F32)
        sumsq = sb("sumsq", 128, NT, F32)
        stat = sb("stat", 128, 2 * NT, F32)
        sc_sq = sb("sc_sq", 128, C, F32)
        cvec = sb("cvec", 128, 4, F32)

        def psum_phase(*specs):
            hs, ctxs = [], []
            for (name, part, cols, dt) in specs:
                cm = nc.psum_tensor(name, [part, cols], dt)
                hs.append(cm.__enter__())
                ctxs.append(cm)
            for cm in reversed(ctxs):
                cm.__exit__(None, None, None)
            return hs

        def sbuf_phase(*specs):
            hs, ctxs = [], []
            for (name, part, cols, dt) in specs:
                cm = nc.sbuf_tensor(name, [part, cols], dt)
                hs.append(cm.__enter__())
                ctxs.append(cm)
            for cm in reversed(ctxs):
                cm.__exit__(None, None, None)
            return hs

        pt_bf, pt_bf2 = psum_phase(("pt_bf", 128, 4096, BF16),
                                   ("pt_bf2", 128, 4096, BF16))

        p.dma("sync", lambda: cb["c_ident"][:, :], lambda: cdram["c_ident"],
              "s_ld")
        p.dma("sync",
              lambda: x_tm[:, :].rearrange("p (i m) -> p i m", m=C),
              lambda: x_in.rearrange("(i p) m -> p i m", p=128),
              "s_ld")
        ld_done = p.c("s_ld")

        def load_consts(keys):
            for k in keys:
                v = cmaps[k]
                dt = BF16 if v.dtype == BF else F32
                cb[k] = None
            specs = [(k + "_t", cmaps[k].shape[0], cmaps[k].shape[1],
                      BF16 if cmaps[k].dtype == BF else F32) for k in keys]
            hs, close = sbuf_group(*specs)
            for k, h in zip(keys, hs):
                cb[k] = h
                p.dma("sync", lambda h=h: h[:, :], lambda dk=cdram[k]: dk,
                      "s_ld")
            p.wait("vector", "s_ld")
            p.wait("gpsimd", "s_ld")
            p.wait("tensor", "s_ld")
            p.wait("scalar", "s_ld")
            return close

        tp_bank_free = {0: 0, 1: 0}

        def emit_T_transposes(dst, dst_col0, dst_stride, src, src_col0,
                              src_stride, ntile, evac_eng="vector"):
            dcols = dst.shape[1] if hasattr(dst, "shape") else None
            for g0 in range(0, ntile, 4):
                n = min(4, ntile - g0)
                bank = (g0 // 4) % 2
                pt = pt_bf if bank == 0 else pt_bf2
                p.wait("tensor", "s_v", tp_bank_free[bank])

                def tr(e, g0=g0, n=n, pt=pt):
                    for j in range(n):
                        e.transpose(
                            pt[:, j * 1024:j * 1024 + 128],
                            src[:, src_col0 + (g0 + j) * src_stride:
                                src_col0 + (g0 + j) * src_stride + 128],
                            cb["c_ident"][:, :])
                    return e.nop()
                p.t(tr, inc="s_t")
                p.wait(evac_eng, "s_t", p.c("s_t"))

                def ev(e, g0=g0, n=n, pt=pt):
                    last = None
                    for j in range(n):
                        c0 = dst_col0 + (g0 + j) * dst_stride
                        last = (e.tensor_copy if evac_eng == "vector"
                                else e.copy)(dst[:, c0:c0 + 128],
                                             pt[:, j * 1024:j * 1024 + 128])
                    return last
                if evac_eng == "vector":
                    p.v(ev, inc="s_v")
                else:
                    p.s(ev, inc="s_v")
                tp_bank_free[bank] = p.c("s_v")

        def emit_xnT_tile(xnT, i, src_bf_tile):
            emit_T_transposes(xnT, i * 128, T, src_bf_tile, 0, 128, NCH)

        def emit_ln_xnT(eps, xnT):
            join_all()
            (xn_bf,), close_xb = sbuf_group(
                (f"xn_bf{p.c('s_v')}", 128, 2 * C, BF16))
            tp_bank_free[0] = p.c("s_v")
            tp_bank_free[1] = p.c("s_v")
            xb_free = {0: 0, 1: 0}
            for i in range(NT):
                xt = x_tm[:, i * C:(i + 1) * C]
                mean, rstd = _ln_stats(p, xt, sc_sq, sums, sumsq, stat, eps, i)
                p.wait("vector", "s_a", p.c("s_a"))
                slot = i % 2
                p.wait("vector", "s_t", xb_free[slot])
                p.wait("vector", "s_a")
                xb = xn_bf[:, slot * C:(slot + 1) * C]
                p.v(lambda e, xt=xt, xb=xb, m=mean, r=rstd:
                    e.tensor_scalar(xb, xt, m, r, OP.subtract, OP.mult),
                    inc="s_v")
                p.wait("tensor", "s_v", p.c("s_v"))
                import os as _os2
                _st = _os2.environ.get("K_STAGE", "")
                if _st != "ln" and not (_st == "ln1" and i > 0):
                    emit_xnT_tile(xnT, i, xb)
                xb_free[slot] = p.c("s_t")
            close_xb()

        def join_all(engines=("tensor", "vector", "scalar", "gpsimd", "sync")):
            for eng in engines:
                for s in SEMS:
                    if p.cnt[s]:
                        p.wait(eng, s)

        def dump_dbg(src_fn, cast=False):
            join_all(("gpsimd",))
            p.dma("gpsimd",
                  lambda: dbg.rearrange("(p a) m -> p (a m)", p=128),
                  src_fn, "s_gp")
            p.wait("gpsimd", "s_gp", p.c("s_gp"))

        p.v(lambda e: e.memset(cvec[:, 0:1], -U0))
        p.v(lambda e: e.memset(cvec[:, 2:3], 0.125))
        p.v(lambda e: e.memset(cvec[:, 1:2], 1.0), inc="s_v")

        p.wait("vector", "s_ld", ld_done)
        p.wait("scalar", "s_ld", ld_done)
        p.wait("tensor", "s_ld", ld_done)

        def emit_attention(L, xnT):
            causal = (L == 0)
            pstAB, pstAB2, pc0, pc1, pc2, pc3 = psum_phase(
                (f"pstAB{L}", 128, 1024, F32), (f"pstAB2{L}", 128, 1024, F32),
                (f"pc0_{L}", 128, 512, F32), (f"pc1_{L}", 128, 512, F32),
                (f"pc2_{L}", 128, 512, F32), (f"pc3_{L}", 128, 512, F32))
            pmm, = psum_phase((f"pmm{L}", 128, 1024, F32))
            pvt_a, pvt_b = psum_phase((f"pvt_a{L}", 128, 1024, BF16),
                                      (f"pvt_b{L}", 128, 1024, BF16))
            psts = [pstAB, pstAB2]
            pcs = [pc0, pc1, pc2, pc3]
            (wsl, qraw, rot_sb, r1, r2, qf, kf, vaa, vab, pT, rc, bql,
             qlo, klo) = sbuf_phase(
                (f"wsl{L}", 128, 3072, BF16),
                (f"qraw{L}", 128, T, BF16),
                (f"rot{L}", 128, T, BF16),
                (f"r1_{L}", 128, T, F32),
                (f"r2_{L}", 128, T, BF16),
                (f"qf{L}", 128, T, BF16),
                (f"kf{L}", 128, T, BF16),
                (f"vaa{L}", 128, 16 * 65, BF16),
                (f"vab{L}", 128, 16 * 65, BF16),
                (f"pT{L}", 128, 2 * 512, BF16),
                (f"rc{L}", 128, 8, F32),
                (f"bql{L}", 128, 24, F32),
                (f"qlo{L}", 128, 256, BF16),
                (f"klo{L}", 128, 256, BF16))
            join_all()
            p.v(lambda e: e.memset(vaa[:, :], 1.0))
            p.v(lambda e: e.memset(vab[:, :], 1.0), inc="s_v")
            if L == 1:
                for name, dram, col0 in (("q", l1_bq, 0), ("k", l1_bk, 8),
                                         ("v", l1_bv, 16)):
                    p.y(lambda e, c0=col0, d=dram: e.dma_start(
                        out=bass.AP(bql, c0, [[24, 128], [1, 8]]),
                        in_=d.rearrange("(j q) -> q j", q=128),
                        allow_slow_non_contiguous=True), inc="s_ld", n=16)
                p.wait("vector", "s_ld", p.c("s_ld"))
                p.v(lambda e: e.tensor_scalar(bql[:, 0:8], bql[:, 0:8], 0.125,
                                              None, OP.mult), inc="s_v")
                p.wait("scalar", "s_v")

            last_av = {"t": 0}
            evac_done = {"v": 0, "a": 0}
            evac_half = {0: 0, 1: 0}
            wsl_free = 0

            for hp in range(NHP):
                p.wait("gpsimd", "s_t", wsl_free)
                for i3, (dram, off) in enumerate(
                        [(w_qkv if L == 0 else l1_wq, 0),
                         (w_qkv if L == 0 else l1_wk, C if L == 0 else 0),
                         (w_qkv if L == 0 else l1_wv, 2 * C if L == 0 else 0)]):
                    wid = 3 * C if L == 0 else C
                    p.dma("gpsimd",
                          lambda i3=i3: bass.AP(wsl, i3 * 1024,
                                                [[3072, 128], [128, 8], [1, 128]]),
                          lambda dram=dram, off=off, wid=wid, hp=hp:
                          bass.AP(dram.tensor, off + hp * 128,
                                  [[wid, 128], [128 * wid, 8], [1, 128]]),
                          "s_gp")
                wload = p.c("s_gp")

                def qkv_proj(qi, dst, bias_col, scale):
                    p.wait("tensor", "s_gp", wload)
                    p.wait("tensor", "s_v", evac_done["v"])
                    p.wait("tensor", "s_a")
                    p.wait("scalar", "s_v")
                    p.wait("scalar", "s_g")
                    for ch in range(4):
                        half = ch % 2
                        p.wait("tensor", "s_v", evac_half[half])
                        def mm(e, qi=qi, ch=ch, half=half):
                            last = None
                            for c in range(NCH):
                                last = e.matmul(
                                    pmm[:, half * 512:(half + 1) * 512],
                                    wsl[:, qi * 1024 + c * 128:qi * 1024 + (c + 1) * 128],
                                    xnT[:, c * T + ch * 512:c * T + (ch + 1) * 512],
                                    start=(c == 0), stop=(c == NCH - 1))
                            return last
                        p.t(mm, inc="s_t")
                        p.wait("vector", "s_t", p.c("s_t"))
                        def ev(e, ch=ch, half=half, dst=dst, bias_col=bias_col,
                               scale=scale):
                            dsl = dst[:, ch * 512:(ch + 1) * 512]
                            psl = pmm[:, half * 512:(half + 1) * 512]
                            if bias_col is None:
                                return e.tensor_copy(dsl, psl)
                            bap = bql[:, bias_col:bias_col + 1]
                            if scale is not None:
                                return e.tensor_scalar(
                                    dsl, psl, cvec[:, 2:3], bap,
                                    OP.mult, OP.add)
                            return e.tensor_scalar(dsl, psl, bap, None,
                                                   OP.add)
                        p.v(ev, inc="s_v")
                        evac_done["a"] = p.c("s_v")
                        evac_half[half] = p.c("s_v")

                if L == 0:
                    qkv_proj(0, qraw, None, None)
                    def rope(dstf, dlo):
                        for ch in range(4):
                            half = ch % 2
                            p.wait("tensor", "s_v", evac_half[half])
                            def mmr(e, ch=ch, half=half):
                                return e.matmul(
                                    pmm[:, half * 512:(half + 1) * 512],
                                    cb["c_perm"][:, :],
                                    qraw[:, ch * 512:(ch + 1) * 512],
                                    start=True, stop=True)
                            p.t(mmr, inc="s_t")
                            p.wait("vector", "s_t", p.c("s_t"))
                            def evr(e, ch=ch, half=half):
                                return e.tensor_copy(
                                    rot_sb[:, ch * 512:(ch + 1) * 512],
                                    pmm[:, half * 512:(half + 1) * 512])
                            p.v(evr, inc="s_v")
                            evac_done["a"] = p.c("s_v")
                            evac_half[half] = p.c("s_v")
                        p.wait("vector", "s_a")
                        def ropechain(e, dstf=dstf):
                            e.tensor_tensor(r1[:, :], qraw[:, :],
                                            cb["c_cos_hi"][:, :], OP.mult)
                            e.tensor_tensor(r2[:, :], rot_sb[:, :],
                                            cb["c_sin_hi"][:, :], OP.mult)
                            e.tensor_tensor(r1[:, :], r1[:, :], r2[:, :],
                                            OP.add)
                            e.tensor_tensor(r2[:, :], qraw[:, :],
                                            cb["c_cos_lo"][:, :], OP.mult)
                            e.tensor_tensor(r1[:, :], r1[:, :], r2[:, :],
                                            OP.add)
                            e.tensor_tensor(r2[:, :], rot_sb[:, :],
                                            cb["c_sin_lo"][:, :], OP.mult)
                            e.tensor_tensor(r1[:, :], r1[:, :], r2[:, :],
                                            OP.add)
                            return e.tensor_copy(dstf[:, :], r1[:, :])
                        p.v(ropechain, inc="s_v")
                        p.wait("vector", "s_v")
                        p.v(lambda e, dstf=dstf, dlo=dlo: e.scalar_tensor_tensor(
                            dlo[:, :], dstf[:, 0:256], -1.0, r1[:, 0:256],
                            OP.mult, OP.add), inc="s_v")
                        p.wait("vector", "s_v")
                    rope(qf, qlo)
                    qkv_proj(1, qraw, None, None)
                    rope(kf, klo)
                    qkv_proj(2, rot_sb, None, None)
                    vsrc = rot_sb
                else:
                    qkv_proj(0, qf, 0 + hp, 0.125)
                    qkv_proj(1, kf, 8 + hp, None)
                    qkv_proj(2, rot_sb, 16 + hp, None)
                    vsrc = rot_sb

                p.wait("tensor", "s_v", evac_done["a"])
                for st2 in range(0, 16, 2):
                    for j, pvt in ((0, pvt_a), (1, pvt_b)):
                        sc = st2 + j
                        if sc >= 16:
                            continue
                        p.wait("tensor", "s_v", evac_done["v"])
                        def trv(e, sc=sc, pvt=pvt, vsrc=vsrc):
                            return e.transpose(pvt[:, 0:128],
                                               vsrc[:, sc * 128:(sc + 1) * 128],
                                               cb["c_ident"][:, :])
                        p.t(trv, inc="s_t")
                        p.wait("vector", "s_t", p.c("s_t"))
                        def evv(e, sc=sc, pvt=pvt):
                            e.tensor_copy(vaa[:, sc * 65:sc * 65 + 64],
                                          pvt[:, 0:64])
                            return e.tensor_copy(vab[:, sc * 65:sc * 65 + 64],
                                                 pvt[:, 64:128])
                        p.v(evv, inc="s_v")
                        evac_done["v"] = p.c("s_v")

                p.wait("tensor", "s_v", evac_done["v"])
                for tq in range(8):
                    n_sc = min(16, 2 * tq + 2) if causal else 16
                    exp_guard = [p.c("s_a"), p.c("s_a")]
                    pT_guard = [0, 0]

                    def emit_mav(sc, n_sc=n_sc):
                        dbuf = sc % 2
                        p.wait("tensor", "s_a", exp_guard[dbuf])
                        def mav(e, sc=sc, dbuf=dbuf, first=(sc == 0),
                                last=(sc == n_sc - 1)):
                            last_i = None
                            for h2 in range(2):
                                last_i = e.matmul(
                                    pcs[h2][:, 0:65],
                                    pT[:, dbuf * 512 + h2 * 128:
                                       dbuf * 512 + (h2 + 1) * 128],
                                    vaa[:, sc * 65:(sc + 1) * 65],
                                    start=first, stop=last)
                                last_i = e.matmul(
                                    pcs[2 + h2][:, 0:65],
                                    pT[:, dbuf * 512 + 256 + h2 * 128:
                                       dbuf * 512 + 256 + (h2 + 1) * 128],
                                    vab[:, sc * 65:(sc + 1) * 65],
                                    start=first, stop=last)
                            return last_i
                        p.t(mav, inc="s_t")
                        pT_guard[dbuf] = p.c("s_t")

                    for sc in range(n_sc):
                        dbuf = sc % 2
                        pst = psts[dbuf]
                        p.wait("tensor", "s_a", exp_guard[dbuf])
                        hilo = causal and tq == 0
                        def mst(e, sc=sc, tq=tq, pst=pst, hilo=hilo):
                            terms = [(kf, qf, True, not hilo)]
                            if hilo:
                                terms += [(kf, qlo, False, False),
                                          (klo, qf, False, True)]
                            last = None
                            for (kk, qq, st0, sp0) in terms:
                                kap = (kk[:, sc * 128:(sc + 1) * 128]
                                       if kk is not klo else kk[:, :]
                                       [:, sc * 128:(sc + 1) * 128])
                                qap = (qq[:, tq * 256:(tq + 1) * 256]
                                       if qq is not qlo else qq[:, 0:256])
                                e.matmul(pst[:, 256:512], kap[0:64, :],
                                         qap[0:64, :], start=st0, stop=sp0,
                                         tile_position=(0, 0))
                                last = e.matmul(pst[:, 512:768], kap[64:128, :],
                                                qap[64:128, :], start=st0,
                                                stop=sp0,
                                                tile_position=(64, 0))
                            return last
                        p.t(mst, inc="s_t")
                        mst_done = p.c("s_t")
                        masked = causal and (sc * 128 >= tq * 256)
                        if masked:
                            mk = "c_mask%d" % (sc - 2 * tq)
                            p.wait("vector", "s_t", mst_done)
                            p.v(lambda e, pst=pst, mk=mk: e.tensor_tensor(
                                pst[:, 256:512], pst[:, 256:512], cb[mk][:, :],
                                OP.add), inc="s_v")
                            p.v(lambda e, pst=pst, mk=mk: e.tensor_tensor(
                                pst[:, 512:768], pst[:, 512:768], cb[mk][:, :],
                                OP.add), inc="s_v")
                            p.wait("scalar", "s_v")
                        p.wait("scalar", "s_t", mst_done)
                        p.wait("scalar", "s_t", pT_guard[dbuf])
                        def evex(e, pst=pst, dbuf=dbuf):
                            return e.activation(pT[:, dbuf * 512:(dbuf + 1) * 512],
                                                pst[:, 256:768], AF.Exp,
                                                scale=(0.125 if L == 0 else 1.0),
                                                bias=cvec[:, 0:1])
                        p.s(evex, inc="s_a")
                        exp_guard[dbuf] = p.c("s_a")
                        if sc > 0:
                            emit_mav(sc - 1)
                    emit_mav(n_sc - 1)
                    av_done = p.c("s_t")
                    p.wait("vector", "s_t", av_done)
                    for g in range(4):
                        head = 0 if g < 2 else 1
                        h2 = g % 2
                        tile = 2 * tq + h2
                        col = tile * C + (2 * hp + head) * D
                        p.v(lambda e, g=g: e.reciprocal(rc[:, g:g + 1],
                                                        pcs[g][:, 64:65]),
                            inc="s_v")
                        p.wait("vector", "s_v")
                        p.v(lambda e, g=g, col=col: e.tensor_scalar(
                            ctx_tm[:, col:col + D], pcs[g][:, 0:D],
                            rc[:, g:g + 1], None, OP.mult), inc="s_v")
                    p.wait("tensor", "s_v", p.c("s_v"))
                wsl_free = p.c("s_t")
            join_all()

        def emit_bcast(dst, dram_ap, n=C, off=0):
            p.g(lambda e: e.dma_start(
                out=dst[:, :],
                in_=bass.AP(dram_ap.tensor, off, [[0, 128], [1, n]])),
                inc="s_gp", n=16)

        def emit_attn_proj(L, xnT, slo=None):
            ptp, pproj = psum_phase((f"ptp{L}", 128, 4096, BF16),
                                    (f"pproj{L}", 128, 1024, F32))
            if L == 0:
                wpj, ctxT = sbuf_phase((f"wpj{L}", 128, NCH * C, BF16),
                                       (f"ctxT{L}", 128, C, BF16))
                yb = gf = bof = None
            else:
                wpj, ctxT, yb, xnf, bof, gaf, bef = sbuf_phase(
                    (f"wpj{L}", 128, NCH * C, BF16),
                    (f"ctxT{L}", 128, C, BF16),
                    (f"yb{L}", 128, C, F32),
                    (f"xnf{L}", 128, C, F32),
                    (f"bof{L}", 128, C, F32),
                    (f"gaf{L}", 128, C, F32),
                    (f"bef{L}", 128, C, F32))
            join_all()
            wsrc = l0_wproj if L == 0 else l1_wo
            p.dma("gpsimd",
                  lambda: bass.AP(wpj, 0, [[NCH * C, 128], [C, 8], [1, C]]),
                  lambda: bass.AP(wsrc.tensor, 0, [[C, 128], [128 * C, 8], [1, C]]),
                  "s_gp")
            if L == 1:
                emit_bcast(bof, l1_bo)
                emit_bcast(gaf, l1_gamma)
                emit_bcast(bef, l1_beta)
                p.wait("vector", "s_ld", p.c("s_ld"))
            p.wait("tensor", "s_gp", p.c("s_gp"))
            tp_free = 0
            import os as _os6
            _ks = _os6.environ.get("K_STAGE", "")
            _nt = 1 if _ks == "proj1" else (2 if _ks == "proj2" else NT)
            for i in range(_nt):
                srcs = [(ctx_tm, i * C, 128, ctxT, 0, 128, 8)]
                if _ks == "projnotp" and i > 0:
                    srcs = []
                if L == 1:
                    p.wait("vector", "s_a")
                    p.v(lambda e, i=i: e.tensor_scalar(
                        xnf[:, :], x_tm[:, i * C:(i + 1) * C],
                        stat[:, 2 * i:2 * i + 1], stat[:, 2 * i + 1:2 * i + 2],
                        OP.subtract, OP.mult), inc="s_v")
                p.wait("tensor", "s_a")
                p.wait("tensor", "s_v")
                for (stens, sc0, sstr, dtens, dc0, dstr, ntile) in srcs:
                    for g0 in (0, 4):
                        p.wait("tensor", "s_v", tp_free)
                        def trc(e, stens=stens, sc0=sc0, sstr=sstr, g0=g0):
                            for j in range(4):
                                e.transpose(ptp[:, j * 1024:j * 1024 + 128],
                                            stens[:, sc0 + (g0 + j) * sstr:
                                                  sc0 + (g0 + j) * sstr + 128],
                                            cb["c_ident"][:, :])
                            return e.nop()
                        p.t(trc, inc="s_t")
                        p.wait("vector", "s_t", p.c("s_t"))
                        def evc(e, dtens=dtens, dc0=dc0, dstr=dstr, g0=g0):
                            last = None
                            for j in range(4):
                                last = e.tensor_copy(
                                    dtens[:, dc0 + (g0 + j) * dstr:
                                          dc0 + (g0 + j) * dstr + 128],
                                    ptp[:, j * 1024:j * 1024 + 128])
                            return last
                        p.v(evc, inc="s_v")
                        tp_free = p.c("s_v")
                import os as _os4
                if _os4.environ.get("K_STAGE") == "projtp":
                    continue
                p.wait("tensor", "s_v", tp_free)
                p.wait("tensor", "s_a")
                p.wait("tensor", "s_v")
                for h in range(2):
                    def mmp(e, h=h):
                        last = None
                        for c in range(NCH):
                            last = e.matmul(pproj[:, h * 512:(h + 1) * 512],
                                            ctxT[:, c * 128:(c + 1) * 128],
                                            wpj[:, c * C + h * 512:
                                                c * C + (h + 1) * 512],
                                            start=(c == 0), stop=(c == NCH - 1))
                        return last
                    p.t(mmp, inc="s_t")
                mm_done = p.c("s_t")
                import os as _os5
                if _os5.environ.get("K_STAGE") == "projmm":
                    p.wait("scalar", "s_t", mm_done)
                    p.s(lambda e: e.activation(ctxT[:, 0:512], pproj[:, 0:512],
                                               AF.Identity), inc="s_a")
                    continue
                if _os5.environ.get("K_STAGE") == "projev":
                    p.wait("scalar", "s_t", mm_done)
                    p.s(lambda e: e.activation(ctxT[:, :], pproj[:, :],
                                               AF.Identity), inc="s_a")
                    continue
                if L == 0:
                    p.wait("scalar", "s_t", mm_done)
                    for h in range(2):
                        p.s(lambda e, i=i, h=h: e.activation(
                            ctx_tm[:, i * C + h * 512:i * C + (h + 1) * 512],
                            pproj[:, h * 512:(h + 1) * 512],
                            AF.Identity), inc="s_a")
                    if slo is not None and i < 4:
                        p.wait("vector", "s_a")
                        for h in range(2):
                            p.v(lambda e, i=i, h=h: e.scalar_tensor_tensor(
                                slo[:, i * C + h * 512:i * C + (h + 1) * 512],
                                ctx_tm[:, i * C + h * 512:
                                       i * C + (h + 1) * 512], -1.0,
                                pproj[:, h * 512:(h + 1) * 512],
                                OP.mult, OP.add), inc="s_v")
                else:
                    p.wait("vector", "s_t", mm_done)
                    for h in range(2):
                        p.v(lambda e, h=h: e.scalar_tensor_tensor(
                            yb[:, h * 512:(h + 1) * 512],
                            pproj[:, h * 512:(h + 1) * 512], 1.0,
                            bof[:, h * 512:(h + 1) * 512], OP.mult, OP.add),
                            inc="s_v")
                    p.wait("vector", "s_v")
                    p.v(lambda e: e.tensor_tensor(yb[:, :], yb[:, :],
                                                  xnf[:, :], OP.add),
                        inc="s_v")
                    p.wait("vector", "s_v")
                    mean, rstd = _ln_stats(p, yb[:, :], sc_sq, sums, sumsq,
                                           stat, 1e-3, i)
                    p.wait("vector", "s_a", p.c("s_a"))
                    p.v(lambda e, m=mean, r=rstd: e.tensor_scalar(
                        yb[:, :], yb[:, :], m, r, OP.subtract, OP.mult),
                        inc="s_v")
                    p.wait("vector", "s_v")
                    p.v(lambda e: e.tensor_tensor(yb[:, :], yb[:, :],
                                                  gaf[:, :], OP.mult),
                        inc="s_v")
                    p.wait("vector", "s_v")
                    p.v(lambda e, i=i: e.tensor_tensor(
                        ctx_tm[:, i * C:(i + 1) * C], yb[:, :], bef[:, :],
                        OP.add), inc="s_v")
            join_all()

        (xnT0,), close_xnT0 = sbuf_group(("xnT0", 128, NCH * T, BF16))
        close_l0c = load_consts(["c_cos_hi", "c_sin_hi", "c_cos_lo",
                                 "c_sin_lo", "c_perm", "c_mask0", "c_mask1"])
        emit_ln_xnT(1e-5, xnT0)
        emit_attention(0, xnT0)
        if debug_name == "ctxpre":
            join_all(("gpsimd",))
            p.dma("gpsimd", lambda: dbg.rearrange("(p a) m -> p (a m)", p=128),
                  lambda: ctx_tm[:, :], "s_gp")
            p.wait("gpsimd", "s_gp", p.c("s_gp"))
        close_l0c()
        close_xnT0()
        (slo,), close_slo = sbuf_group(("slo", 128, 4 * C, BF16))
        emit_attn_proj(0, None, slo=slo)

        def emit_quant(src_fn, dst_fn, qs, n_sec=1, veng="vector",
                       multi=False):
            amax, m_, inv, cmp, f_, G = (qs[k] for k in
                                         ("amax", "m", "inv", "cmp", "f", "G"))
            u, cbuf, d, m1, m2 = (qs[k] for k in ("u", "c", "d", "m1", "m2"))
            ef = qs.get("ef")
            nw = n_sec * 32
            def amaxes(e):
                last = None
                for s2 in range(n_sec):
                    src3 = src_fn(s2).rearrange("p (b k) -> p b k", k=32)
                    last = e.tensor_reduce(
                        amax[:, s2 * 32:(s2 + 1) * 32], src3, AXX, OP.max,
                        apply_absolute_value=True)
                return last
            p.v(amaxes, inc="s_v")
            p.wait("vector", "s_v")
            p.v(lambda e: e.tensor_scalar(m_[:, :nw], amax[:, :nw],
                                          float(2.0 ** -64), None, OP.max),
                inc="s_v")
            p.wait("vector", "s_v")
            p.v(lambda e: e.tensor_scalar(m_[:, :nw], m_[:, :nw],
                                          float(2.0 ** 64), None, OP.mult))
            p.v(lambda e: e.memset(inv[:, :nw], float(2.0 ** 67)),
                inc="s_v")
            if not multi:
                p.v(lambda e: e.memset(ef[:, :nw], -67.0), inc="s_v")
            p.wait("vector", "s_v")
            for bit in (64, 32, 16, 8, 4, 2, 1):
                k = min(bit, 16)
                rep = bit // k
                p.v(lambda e, bit=bit: e.tensor_scalar(
                    cmp[:, :nw], m_[:, :nw], float(2.0 ** bit), None,
                    OP.is_ge), inc="s_v")
                p.wait("vector", "s_v")
                p.v(lambda e, k=k: e.tensor_scalar(
                    f_[:, :nw], cmp[:, :nw], float(2.0 ** -k) - 1.0, 1.0,
                    OP.mult, OP.add), inc="s_v")
                if not multi:
                    p.v(lambda e, bit=bit: e.scalar_tensor_tensor(
                        ef[:, :nw], cmp[:, :nw], float(bit), ef[:, :nw],
                        OP.mult, OP.add), inc="s_v")
                p.wait("vector", "s_v")
                for _r in range(rep):
                    p.v(lambda e: e.tensor_tensor(m_[:, :nw], m_[:, :nw],
                                                  f_[:, :nw], OP.mult))
                    p.v(lambda e: e.tensor_tensor(inv[:, :nw], inv[:, :nw],
                                                  f_[:, :nw], OP.mult),
                        inc="s_v")
                    p.wait("vector", "s_v")
            p.v(lambda e: e.reciprocal(G[:, :nw], inv[:, :nw]), inc="s_v")
            p.wait("vector", "s_v")
            if not multi:
                def sop(fn):
                    p.v(fn, inc="s_v")
                    p.wait("vector", "s_v")
                sop(lambda e: e.tensor_scalar(cmp[:, :nw], ef[:, :nw],
                                              -16.5, None, OP.is_ge))
                sop(lambda e: e.tensor_scalar(m_[:, :nw], ef[:, :nw],
                                              -11.5, None, OP.is_le))
                sop(lambda e: e.tensor_tensor(cmp[:, :nw], cmp[:, :nw],
                                              m_[:, :nw], OP.mult))
                sop(lambda e: e.tensor_scalar(m_[:, :nw], ef[:, :nw],
                                              -8.5, None, OP.is_ge))
                sop(lambda e: e.tensor_scalar(amax[:, :nw], ef[:, :nw],
                                              -5.5, None, OP.is_le))
                sop(lambda e: e.tensor_tensor(m_[:, :nw], m_[:, :nw],
                                              amax[:, :nw], OP.mult))
                sop(lambda e: e.tensor_tensor(cmp[:, :nw], cmp[:, :nw],
                                              m_[:, :nw], OP.add))
                sop(lambda e: e.tensor_scalar(m_[:, :nw], ef[:, :nw],
                                              -3.0, None, OP.is_equal))
                sop(lambda e: e.tensor_tensor(cmp[:, :nw], cmp[:, :nw],
                                              m_[:, :nw], OP.add))
                sop(lambda e: e.tensor_scalar(m_[:, :nw], ef[:, :nw],
                                              -11.0, None, OP.is_equal))
                sop(lambda e: e.tensor_scalar(f_[:, :nw], cmp[:, :nw],
                                              -1.0, 1.0, OP.mult, OP.add))
                sop(lambda e: e.tensor_tensor(f_[:, :nw], f_[:, :nw],
                                              m_[:, :nw], OP.subtract))
            cX, nX, wA, w2, rA = (qs[k] for k in
                                  ("cx", "nx", "wa", "w2", "ra"))
            for s in range(n_sec):
                nb0 = s * 32
                src3 = src_fn(s).rearrange("p (b k) -> p b k", k=32)
                inv_b = inv[:, nb0:nb0 + 32].unsqueeze(2).broadcast_to(
                    [128, 32, 32])
                G_b = G[:, nb0:nb0 + 32].unsqueeze(2).broadcast_to(
                    [128, 32, 32])
                u3 = u[:, :].rearrange("p (b k) -> p b k", k=32)
                if not multi:
                    dirZ_b = f_[:, nb0:nb0 + 32].unsqueeze(2).broadcast_to(
                        [128, 32, 32])
                    dirP_b = m_[:, nb0:nb0 + 32].unsqueeze(2).broadcast_to(
                        [128, 32, 32])
                    def wchain(e, s=s, src3=src3, inv_b=inv_b, G_b=G_b,
                               dirZ_b=dirZ_b, dirP_b=dirP_b):
                        e.tensor_tensor(u3, src3, inv_b, OP.mult)
                        e.tensor_scalar(cbuf[:, :], u[:, :], 16.5, None,
                                        OP.add)
                        e.scalar_tensor_tensor(d[:, :], cbuf[:, :], -1.0,
                                               u[:, :], OP.mult, OP.add)
                        e.tensor_scalar(m1[:, :], d[:, :], -15.5, None,
                                        OP.is_ge)
                        e.tensor_scalar(m2[:, :], d[:, :], -16.5, None,
                                        OP.is_lt)
                        e.scalar_tensor_tensor(m1[:, :], m2[:, :], -1.0,
                                               m1[:, :], OP.mult, OP.add)
                        e.tensor_scalar(m2[:, :], d[:, :], -16.5, None,
                                        OP.is_equal)
                        e.tensor_scalar(wA[:, :], u[:, :], 0.0, None,
                                        OP.is_ge)
                        e.tensor_tensor(d[:, :], cbuf[:, :], m1[:, :],
                                        OP.add)
                        e.tensor_scalar(nX[:, :], d[:, :], 0.5, None,
                                        OP.mult)
                        e.tensor_scalar(cX[:, :], nX[:, :], 16.25, None,
                                        OP.add)
                        e.scalar_tensor_tensor(nX[:, :], cX[:, :], -1.0,
                                               nX[:, :], OP.mult, OP.add)
                        e.tensor_scalar(w2[:, :], nX[:, :], -16.0, None,
                                        OP.is_equal)
                        e.tensor_scalar(rA[:, :], w2[:, :], -1.0, 1.0,
                                        OP.mult, OP.add)
                        e.tensor_tensor(w2[:, :], wA[:, :], rA[:, :], OP.mult)
                        e.scalar_tensor_tensor(nX[:, :], w2[:, :], -2.0,
                                               wA[:, :], OP.mult, OP.add)
                        e.tensor_tensor(nX[:, :], nX[:, :], rA[:, :],
                                        OP.add)
                        nx3 = nX[:, :].rearrange("p (b k) -> p b k", k=32)
                        e.tensor_tensor(nx3, nx3, dirZ_b, OP.mult)
                        e.tensor_tensor(nx3, nx3, dirP_b, OP.add)
                        e.tensor_scalar(rA[:, :], wA[:, :], 2.0, -1.0,
                                        OP.mult, OP.add)
                        e.tensor_tensor(nX[:, :], nX[:, :], rA[:, :], OP.mult)
                        e.tensor_tensor(nX[:, :], nX[:, :], wA[:, :],
                                        OP.subtract)
                        e.tensor_tensor(nX[:, :], nX[:, :], m2[:, :],
                                        OP.mult)
                        e.tensor_tensor(d[:, :], d[:, :], nX[:, :], OP.add)
                        d3 = d[:, :].rearrange("p (b k) -> p b k", k=32)
                        return e.scalar_tensor_tensor(
                            dst_fn(s).rearrange("p (b k) -> p b k", k=32), d3,
                            -16.0, G_b, OP.add, OP.mult)
                    p.v(wchain, inc="s_v")
                    p.wait("vector", "s_v")
                else:
                    dstf = dst_fn(s)
                    dst3 = dstf.rearrange("p (b k) -> p b k", k=32)
                    def mchain(e, src3=src3, dstf=dstf, dst3=dst3,
                               inv_b=inv_b, G_b=G_b):
                        e.tensor_tensor(u3, src3, inv_b, OP.mult)
                        e.tensor_scalar(cbuf[:, :], u[:, :], 16.5, None,
                                        OP.add)
                        e.scalar_tensor_tensor(d[:, :], cbuf[:, :], -1.0,
                                               u[:, :], OP.mult, OP.add)
                        e.tensor_scalar(m1[:, :], d[:, :], -15.5, None,
                                        OP.is_ge)
                        e.tensor_scalar(m2[:, :], d[:, :], -16.5, None,
                                        OP.is_lt)
                        e.scalar_tensor_tensor(m1[:, :], m2[:, :], -1.0,
                                               m1[:, :], OP.mult, OP.add)
                        e.tensor_tensor(nX[:, :], cbuf[:, :], m1[:, :],
                                        OP.add)
                        e.tensor_scalar(wA[:, :], d[:, :], -17.5, None,
                                        OP.is_ge)
                        e.tensor_scalar(m2[:, :], d[:, :], -14.5, None,
                                        OP.is_lt)
                        e.tensor_tensor(wA[:, :], wA[:, :], m2[:, :], OP.mult)
                        e.scalar_tensor_tensor(dstf, nX[:, :], -16.0,
                                               wA[:, :], OP.add, OP.mult)
                        e.tensor_scalar(rA[:, :], wA[:, :], -1.0, 1.0,
                                        OP.mult, OP.add)
                        for ci, mulk in enumerate((2.0, 0.5)):
                            e.tensor_scalar(cX[:, :], u[:, :], mulk, 16.5,
                                            OP.mult, OP.add)
                            e.scalar_tensor_tensor(d[:, :], cX[:, :], -1.0,
                                                   u[:, :], OP.mult, OP.add)
                            e.tensor_scalar(m1[:, :], d[:, :], -15.5, None,
                                            OP.is_ge)
                            e.tensor_scalar(m2[:, :], d[:, :], -16.5, None,
                                            OP.is_lt)
                            e.scalar_tensor_tensor(m1[:, :], m2[:, :], -1.0,
                                                   m1[:, :], OP.mult, OP.add)
                            e.tensor_tensor(nX[:, :], cX[:, :], m1[:, :],
                                            OP.add)
                            e.tensor_scalar(w2[:, :], d[:, :], -17.5, None,
                                            OP.is_ge)
                            e.tensor_scalar(m2[:, :], d[:, :], -14.5, None,
                                            OP.is_lt)
                            e.tensor_tensor(w2[:, :], w2[:, :], m2[:, :],
                                            OP.mult)
                            e.tensor_tensor(w2[:, :], rA[:, :], w2[:, :],
                                            OP.mult)
                            e.scalar_tensor_tensor(nX[:, :], nX[:, :], -16.0,
                                                   w2[:, :], OP.add, OP.mult)
                            e.tensor_tensor(dstf, dstf, nX[:, :], OP.add)
                            if ci == 0:
                                e.tensor_tensor(rA[:, :], rA[:, :], w2[:, :],
                                                OP.subtract)
                        return e.tensor_tensor(dst3, dst3, G_b, OP.mult)
                    p.v(mchain, inc="s_v")
                    p.wait("vector", "s_v")

        def emit_ffn0(slo):
            SW = 4
            NSW = NT // SW
            qs_specs = [("q_amax", 128, 128, F32), ("q_m", 128, 128, F32),
                        ("q_inv", 128, 128, F32), ("q_cmp", 128, 128, F32),
                        ("q_f", 128, 128, F32), ("q_G", 128, 128, F32),
                        ("q_u", 128, C, F32), ("q_c", 128, C, I32),
                        ("q_m1", 128, C, BF16), ("q_m2", 128, C, BF16),
                        ("q_nx", 128, C, BF16),
                        ("q_wa", 128, C, BF16), ("q_w2", 128, C, BF16),
                        ("q_ra", 128, C, BF16), ("q_ef", 128, 128, F32)]
            names = ("amax", "m", "inv", "cmp", "f", "G", "u", "c",
                     "m1", "m2", "nx", "wa", "w2", "ra", "ef")
            hs, close_qs = sbuf_group(*qs_specs)
            qs = dict(zip(names, hs))
            qs["d"] = sc_sq
            qs["cx"] = qs["c"]
            (wbuf, wqbuf), close_prep = sbuf_group(
                ("wbuf", 128, 8 * C, F32), ("wqbuf", 128, 8 * C, BF16))
            join_all()
            for widx, (wsrc, whbm, rows, cols) in enumerate(
                    ((l0_wfc, wfcq_hbm, FFN0, C), (l0_wp, wprojq_hbm, C, FFN0))):
                ntile_r = rows // 128
                secs_per_tile = cols // C
                for q4 in range(4):
                    nt_q = ntile_r // 4
                    p.wait("gpsimd", "s_out")
                    def ldw(e, wsrc=wsrc, q4=q4, nt_q=nt_q, cols=cols):
                        return e.dma_start(
                            out=bass.AP(wbuf, 0, [[8 * C, 128], [cols, nt_q],
                                                  [1, cols]]),
                            in_=bass.AP(wsrc.tensor, q4 * nt_q * 128 * cols,
                                        [[cols, 128], [128 * cols, nt_q],
                                         [1, cols]]))
                    p.wait("sync", "s_v")
                    p.wait("sync", "s_g")
                    p.y(ldw, inc="s_ld", n=16)
                    p.wait("vector", "s_ld")
                    p.wait("gpsimd", "s_ld")
                    nsec = nt_q * secs_per_tile
                    for off in range(0, nsec, 4):
                        kk = min(4, nsec - off)
                        emit_quant(
                            lambda s, off=off:
                                wbuf[:, (off + s) * C:(off + s + 1) * C],
                            lambda s, off=off:
                                wqbuf[:, (off + s) * C:(off + s + 1) * C],
                            qs, n_sec=kk)
                    def stw(e, whbm=whbm, q4=q4, nt_q=nt_q, cols=cols):
                        return e.dma_start(
                            out=bass.AP(whbm.tensor, q4 * nt_q * 128 * cols,
                                        [[cols, 128], [128 * cols, nt_q],
                                         [1, cols]]),
                            in_=bass.AP(wqbuf, 0, [[8 * C, 128], [cols, nt_q],
                                                   [1, cols]]))
                    p.wait("gpsimd", "s_v")
                    p.g(stw, inc="s_gp", n=16)
            close_prep()
            (xq_sw, wfcqT, wpjT, g_acc, hbuf, hq, hqT,
             bfcb), close_sw = sbuf_group(
                ("xq_sw", 128, NCH * SW * 128, BF16),
                ("wfcqT", 128, 8 * C, BF16),
                ("wpjT", 128, 4 * C, BF16),
                ("g_acc", 128, SW * C, F32),
                ("hbuf", 128, 4 * C, F32),
                ("hq", 128, 4 * C, BF16),
                ("hqT", 128, 512, BF16),
                ("bfcb", 128, C, BF16))
            bpjb = bfcb
            pfc, ppj, ptq = psum_phase(("pfc", 128, 1024, F32),
                                       ("ppj", 128, 1024, F32),
                                       ("ptq", 128, 4096, BF16))
            join_all()
            emit_bcast(bpjb, l0_bp)
            st_guard = {"v": 0}

            def tq_transposes(stens, scol0, sstride, dtens, dcol0, dstride, n):
                for g0 in range(0, n, 4):
                    k = min(4, n - g0)
                    p.wait("tensor", "s_v", st_guard["v"])
                    p.wait("tensor", "s_a")
                    p.wait("tensor", "s_v")
                    def trq(e, g0=g0, k=k, stens=stens, scol0=scol0,
                            sstride=sstride):
                        for j in range(k):
                            e.transpose(ptq[:, j * 1024:j * 1024 + 128],
                                        stens[:, scol0 + (g0 + j) * sstride:
                                              scol0 + (g0 + j) * sstride + 128],
                                        cb["c_ident"][:, :])
                        return e.nop()
                    p.t(trq, inc="s_t")
                    p.wait("scalar", "s_t", p.c("s_t"))
                    def evq(e, g0=g0, k=k, dtens=dtens, dcol0=dcol0,
                            dstride=dstride):
                        last = None
                        for j in range(k):
                            last = e.copy(
                                dtens[:, dcol0 + (g0 + j) * dstride:
                                      dcol0 + (g0 + j) * dstride + 128],
                                ptq[:, j * 1024:j * 1024 + 128])
                        return last
                    p.s(evq, inc="s_v")
                    st_guard["v"] = p.c("s_v")

            for sw in range(NSW):
                tiles = list(range(sw * SW, (sw + 1) * SW))
                p.wait("vector", "s_a")
                def stage_xn(e, tiles=tiles):
                    last = None
                    for tloc, ti in enumerate(tiles):
                        last = e.tensor_scalar(
                            hbuf[:, tloc * C:(tloc + 1) * C],
                            x_tm[:, ti * C:(ti + 1) * C],
                            stat[:, 2 * ti:2 * ti + 1],
                            stat[:, 2 * ti + 1:2 * ti + 2],
                            OP.subtract, OP.mult)
                    return last
                p.v(stage_xn, inc="s_v")
                p.wait("vector", "s_v")
                emit_quant(lambda s: hbuf[:, s * C:(s + 1) * C],
                           lambda s: hq[:, s * C:(s + 1) * C],
                           qs, n_sec=4, multi=True)
                for tloc in range(4):
                    tq_transposes(hq, tloc * C, 128, xq_sw, tloc * 128,
                                  SW * 128, NCH)
                if debug_name == "xq0" and sw == 0:
                    join_all(("gpsimd",))
                    p.dma("gpsimd",
                          lambda: bass.AP(dbg.tensor, 0,
                                          [[4096, 128], [1, 4096]]),
                          lambda: xq_sw[:, :], "s_gp")
                    p.wait("gpsimd", "s_gp", p.c("s_gp"))
                for q4 in range(4):
                    p.wait("sync", "s_t")
                    for c in range(NCH):
                        p.y(lambda e, c=c, q4=q4: e.dma_start_transpose(
                            wfcqT[:, c * C:(c + 1) * C],
                            bass.AP(wfcq_hbm.tensor, q4 * 1024 * C + c * 128,
                                    [[C, 1024], [1, 128]])),
                            inc="s_sc", n=16)
                    p.wait("gpsimd", "s_v")
                    emit_bcast(bfcb, l0_bfc, off=q4 * C)
                    wload = p.c("s_sc")
                    bload = p.c("s_gp")
                    for tloc in range(4):
                        p.wait("tensor", "s_sc", wload)
                        p.wait("tensor", "s_v")
                        for h in range(2):
                            def mfc(e, h=h, tloc=tloc):
                                last = None
                                for c in range(NCH):
                                    last = e.matmul(
                                        pfc[:, h * 512:(h + 1) * 512],
                                        xq_sw[:, c * SW * 128 + tloc * 128:
                                              c * SW * 128 + (tloc + 1) * 128],
                                        wfcqT[:, c * C + h * 512:
                                              c * C + (h + 1) * 512],
                                        start=(c == 0), stop=(c == NCH - 1))
                                return last
                            p.t(mfc, inc="s_t")
                        fc_done = p.c("s_t")
                        p.wait("vector", "s_t", fc_done)
                        p.wait("vector", "s_gp", bload)
                        def evfc(e, tloc=tloc):
                            e.scalar_tensor_tensor(
                                hbuf[:, tloc * C:tloc * C + 512],
                                pfc[:, 0:512], 1.0, bfcb[:, 0:512],
                                OP.mult, OP.add)
                            return e.scalar_tensor_tensor(
                                hbuf[:, tloc * C + 512:(tloc + 1) * C],
                                pfc[:, 512:1024], 1.0, bfcb[:, 512:1024],
                                OP.mult, OP.add)
                        p.v(evfc, inc="s_v")
                    p.wait("scalar", "s_v")
                    def gelus(e):
                        last = None
                        for tloc in range(4):
                            last = e.activation(
                                hbuf[:, tloc * C:(tloc + 1) * C],
                                hbuf[:, tloc * C:(tloc + 1) * C], AF.Gelu)
                        return last
                    p.s(gelus, inc="s_a")
                    p.wait("vector", "s_a")
                    emit_quant(lambda s: hbuf[:, s * C:(s + 1) * C],
                               lambda s: hq[:, s * C:(s + 1) * C],
                               qs, n_sec=4, multi=True)
                    if debug_name == "h159" and sw == 0:
                        join_all(("gpsimd",))
                        p.dma("gpsimd",
                              lambda q4=q4: bass.AP(
                                  dbg.tensor, q4 * 128 * 1024,
                                  [[1024, 128], [1, 1024]]),
                              lambda: hbuf[:, C:2 * C], "s_gp")
                        p.dma("gpsimd",
                              lambda q4=q4: bass.AP(
                                  dbg.tensor, 512 * 1024 + q4 * 128 * 1024,
                                  [[1024, 128], [1, 1024]]),
                              lambda: hq[:, C:2 * C], "s_gp")
                        p.wait("gpsimd", "s_gp", p.c("s_gp"))
                        p.wait("vector", "s_gp", p.c("s_gp"))
                        p.wait("tensor", "s_gp", p.c("s_gp"))
                    for wh in range(2):
                        p.wait("sync", "s_t")
                        for j in range(4):
                            nglob = q4 * 8 + wh * 4 + j
                            p.y(lambda e, j=j, nglob=nglob:
                                e.dma_start_transpose(
                                    wpjT[:, j * C:(j + 1) * C],
                                    bass.AP(wprojq_hbm.tensor, nglob * 128,
                                            [[FFN0, C], [1, 128]])),
                                inc="s_sc", n=16)
                        wpl = p.c("s_sc")
                        for tloc in range(4):
                            tq_transposes(hq, tloc * C + wh * 512, 128,
                                          hqT, 0, 128, 4)
                            p.wait("tensor", "s_sc", wpl)
                            p.wait("tensor", "s_v")
                            for h in range(2):
                                def mpj(e, h=h, wh=wh):
                                    last = None
                                    for j in range(4):
                                        last = e.matmul(
                                            ppj[:, h * 512:(h + 1) * 512],
                                            hqT[:, j * 128:(j + 1) * 128],
                                            wpjT[:, j * C + h * 512:
                                                 j * C + (h + 1) * 512],
                                            start=(j == 0), stop=(j == 3))
                                    return last
                                p.t(mpj, inc="s_t")
                            pj_done = p.c("s_t")
                            p.wait("vector", "s_t", pj_done)
                            def evpj(e, tloc=tloc, first=(q4 == 0 and wh == 0)):
                                last = None
                                for h in range(2):
                                    ga = g_acc[:, tloc * C + h * 512:
                                               tloc * C + (h + 1) * 512]
                                    if first:
                                        last = e.tensor_copy(
                                            ga, ppj[:, h * 512:(h + 1) * 512])
                                    else:
                                        last = e.scalar_tensor_tensor(
                                            ga, ppj[:, h * 512:(h + 1) * 512],
                                            1.0, ga, OP.mult, OP.add)
                                return last
                            p.v(evpj, inc="s_v")
                            p.wait("tensor", "s_v")
                p.wait("gpsimd", "s_v")
                emit_bcast(bpjb, l0_bp)
                for tloc, ti in enumerate(tiles):
                    ga = lambda tloc=tloc: g_acc[:, tloc * C:(tloc + 1) * C]
                    p.wait("vector", "s_gp")
                    p.v(lambda e, tloc=tloc: e.tensor_tensor(
                        g_acc[:, tloc * C:(tloc + 1) * C],
                        g_acc[:, tloc * C:(tloc + 1) * C],
                        bpjb[:, :], OP.add), inc="s_v")
                    p.wait("scalar", "s_v")
                    p.s(lambda e, tloc=tloc: e.activation(
                        g_acc[:, tloc * C:(tloc + 1) * C],
                        g_acc[:, tloc * C:(tloc + 1) * C],
                        AF.Silu), inc="s_a")
                    p.wait("vector", "s_a")
                    hb = hbuf[:, tloc * C:(tloc + 1) * C]
                    if ti < 4:
                        u = qs["u"]
                        p.v(lambda e, ti=ti: e.tensor_tensor(
                            u[:, :], ctx_tm[:, ti * C:(ti + 1) * C],
                            slo[:, ti * C:(ti + 1) * C], OP.add), inc="s_v")
                        p.wait("vector", "s_v")
                        p.v(lambda e, ti=ti, tloc=tloc, u=u, hb=hb:
                            e.tensor_tensor(
                                hb, u[:, :],
                                g_acc[:, tloc * C:(tloc + 1) * C], OP.mult),
                            inc="s_v")
                    else:
                        p.v(lambda e, ti=ti, tloc=tloc, hb=hb: e.tensor_tensor(
                            hb, ctx_tm[:, ti * C:(ti + 1) * C],
                            g_acc[:, tloc * C:(tloc + 1) * C], OP.mult),
                            inc="s_v")
                    p.wait("vector", "s_v")
                    p.v(lambda e, ti=ti, hb=hb: e.tensor_tensor(
                        x_tm[:, ti * C:(ti + 1) * C],
                        x_tm[:, ti * C:(ti + 1) * C], hb, OP.add),
                        inc="s_v")
                    p.wait("vector", "s_v")
            join_all()
            close_sw()
            close_qs()

        emit_ffn0(slo)
        close_slo()
        if debug_name == "x1d":
            join_all(("gpsimd",))
            p.dma("gpsimd", lambda: dbg.rearrange("(p a) m -> p (a m)", p=128),
                  lambda: x_tm[:, :], "s_gp")
            p.wait("gpsimd", "s_gp", p.c("s_gp"))

        def emit_ffn1(xnT):
            HQ = 4
            HCQ = FFN1 // 128 // HQ
            TQ4 = 4
            (wtmp,), close_wt = sbuf_group(("wtmp", 128, 2 * FFN1, BF16))
            join_all()
            for wsrc, wdst, rows, cols in ((l1_wg, wg_bf_hbm, FFN1, C),
                                           (l1_wu, wu_bf_hbm, FFN1, C),
                                           (l1_wd, wd_bf_hbm, C, FFN1)):
                ntr = rows // 128
                per = (2 * FFN1) // cols
                for t0 in range(0, ntr, per):
                    k = min(per, ntr - t0)
                    p.wait("gpsimd", "s_gp")
                    p.g(lambda e, wsrc=wsrc, t0=t0, k=k, cols=cols:
                        e.dma_start(
                            out=bass.AP(wtmp, 0, [[2 * FFN1, 128], [cols, k],
                                                  [1, cols]]),
                            in_=bass.AP(wsrc.tensor, t0 * 128 * cols,
                                        [[cols, 128], [128 * cols, k],
                                         [1, cols]])), inc="s_gp", n=16)
                    p.wait("gpsimd", "s_gp")
                    p.g(lambda e, wdst=wdst, t0=t0, k=k, cols=cols:
                        e.dma_start(
                            out=bass.AP(wdst.tensor, t0 * 128 * cols,
                                        [[cols, 128], [128 * cols, k],
                                         [1, cols]]),
                            in_=bass.AP(wtmp, 0, [[2 * FFN1, 128], [cols, k],
                                                  [1, cols]])), inc="s_gp",
                        n=16)
            close_wt()
            (wgT, wuT, wdTq, hh, hl, hf, g_acc, sg, su), close_f1 = sbuf_group(
                ("wgT", 128, NCH * 768, BF16),
                ("wuT", 128, NCH * 768, BF16),
                ("wdTq", 128, HCQ * C, BF16),
                ("hh", 128, HCQ * 512, BF16),
                ("hl", 128, HCQ * 512, BF16),
                ("hf", 128, 512, F32),
                ("g1_acc", 128, 4 * C, F32),
                ("sg1", 128, 512, F32),
                ("su1", 128, 512, F32))
            pgu, pdn = psum_phase(("pgu", 128, 1024, F32),
                                  ("pdn", 128, 1024, F32))
            join_all()
            for tq4 in range(TQ4):
                tcol0 = tq4 * 512
                for hq in range(HQ):
                    p.wait("sync", "s_t")
                    p.wait("sync", "s_v")
                    for c in range(NCH):
                        p.y(lambda e, c=c, hq=hq: e.dma_start_transpose(
                            wgT[:, c * 768:(c + 1) * 768],
                            bass.AP(wg_bf_hbm.tensor, hq * 768 * C + c * 128,
                                    [[C, 768], [1, 128]])), inc="s_sc", n=16)
                        p.y(lambda e, c=c, hq=hq: e.dma_start_transpose(
                            wuT[:, c * 768:(c + 1) * 768],
                            bass.AP(wu_bf_hbm.tensor, hq * 768 * C + c * 128,
                                    [[C, 768], [1, 128]])), inc="s_sc", n=16)
                    for hcl in range(HCQ):
                        hc = hq * HCQ + hcl
                        p.y(lambda e, hcl=hcl, hc=hc: e.dma_start_transpose(
                            wdTq[:, hcl * C:(hcl + 1) * C],
                            bass.AP(wd_bf_hbm.tensor, hc * 128,
                                    [[FFN1, C], [1, 128]])), inc="s_sc", n=16)
                    wload = p.c("s_sc")
                    p.wait("tensor", "s_sc", wload)
                    for hcl in range(HCQ):
                        p.wait("tensor", "s_a")
                        p.wait("tensor", "s_v")
                        for which, wT, ps0 in (("g", wgT, 0), ("u", wuT, 512)):
                            def mgu(e, wT=wT, ps0=ps0, hcl=hcl, tcol0=tcol0):
                                last = None
                                for c in range(NCH):
                                    last = e.matmul(
                                        pgu[:, ps0:ps0 + 512],
                                        wT[:, c * 768 + hcl * 128:
                                           c * 768 + (hcl + 1) * 128],
                                        xnT[:, c * T + tcol0:
                                            c * T + tcol0 + 512],
                                        start=(c == 0), stop=(c == NCH - 1))
                                return last
                            p.t(mgu, inc="s_t")
                        gu_done = p.c("s_t")
                        p.wait("vector", "s_t", gu_done)
                        def evgu(e):
                            e.tensor_scalar(hf[:, :], pgu[:, 0:512], C127L,
                                            None, OP.mult)
                            e.scalar_tensor_tensor(
                                sg[:, :], pgu[:, 0:512], C127H, hf[:, :],
                                OP.mult, OP.add)
                            e.tensor_scalar(hf[:, :], pgu[:, 512:1024], C127L,
                                            None, OP.mult)
                            return e.scalar_tensor_tensor(
                                su[:, :], pgu[:, 512:1024], C127H, hf[:, :],
                                OP.mult, OP.add)
                        p.v(evgu, inc="s_v")
                        p.wait("scalar", "s_v")
                        p.s(lambda e: e.activation(sg[:, :], sg[:, :],
                                                   AF.Silu), inc="s_a")
                        p.wait("vector", "s_a")
                        def evhh(e, hcl=hcl):
                            e.tensor_tensor(hf[:, :], sg[:, :], su[:, :],
                                            OP.mult)
                            e.tensor_copy(hh[:, hcl * 512:(hcl + 1) * 512],
                                          hf[:, :])
                            return e.scalar_tensor_tensor(
                                hl[:, hcl * 512:(hcl + 1) * 512],
                                hh[:, hcl * 512:(hcl + 1) * 512], -1.0,
                                hf[:, :], OP.mult, OP.add)
                        p.v(evhh, inc="s_v")
                    hh_done = p.c("s_v")
                    p.wait("tensor", "s_v", hh_done)
                    for tloc in range(4):
                        ti = tq4 * 4 + tloc
                        for h in range(2):
                            def mdn(e, tloc=tloc, h=h):
                                last = None
                                for part in (hh, hl):
                                    for hcl in range(HCQ):
                                        last = e.matmul(
                                            pdn[:, h * 512:(h + 1) * 512],
                                            part[:, hcl * 512 + tloc * 128:
                                                 hcl * 512 + (tloc + 1) * 128],
                                            wdTq[:, hcl * C + h * 512:
                                                 hcl * C + (h + 1) * 512],
                                            start=(part is hh and hcl == 0),
                                            stop=(part is hl and
                                                  hcl == HCQ - 1))
                                return last
                            p.t(mdn, inc="s_t")
                        dn_done = p.c("s_t")
                        p.wait("vector", "s_t", dn_done)
                        for h in range(2):
                            if hq == 0:
                                p.v(lambda e, h=h, tloc=tloc: e.tensor_copy(
                                    g_acc[:, tloc * C + h * 512:
                                          tloc * C + (h + 1) * 512],
                                    pdn[:, h * 512:(h + 1) * 512]), inc="s_v")
                            else:
                                p.v(lambda e, h=h, tloc=tloc:
                                    e.scalar_tensor_tensor(
                                        g_acc[:, tloc * C + h * 512:
                                              tloc * C + (h + 1) * 512],
                                        pdn[:, h * 512:(h + 1) * 512], 1.0,
                                        g_acc[:, tloc * C + h * 512:
                                              tloc * C + (h + 1) * 512],
                                        OP.mult, OP.add), inc="s_v")
                        p.wait("tensor", "s_v")
                if debug_name == "f1d":
                    join_all(("gpsimd",))
                    p.dma("gpsimd",
                          lambda tq4=tq4: bass.AP(
                              dbg.tensor, tq4 * 128 * 4096,
                              [[4096, 128], [1, 4096]]),
                          lambda: g_acc[:, :], "s_gp")
                    p.wait("gpsimd", "s_gp", p.c("s_gp"))
                    p.wait("vector", "s_gp", p.c("s_gp"))
                    p.wait("scalar", "s_gp", p.c("s_gp"))
                for tloc in range(4):
                    ti = tq4 * 4 + tloc
                    ga = g_acc[:, tloc * C:(tloc + 1) * C]
                    def ev127(e, ga=ga):
                        e.tensor_scalar(sc_sq[:, :], ga, C127L, None, OP.mult)
                        return e.scalar_tensor_tensor(
                            ga, ga, C127H, sc_sq[:, :], OP.mult, OP.add)
                    p.v(ev127, inc="s_v")
                    p.wait("scalar", "s_v")
                    p.s(lambda e, ga=ga: e.activation(
                        ga, ga, AF.Silu), inc="s_a")
                    p.wait("vector", "s_a")
                    p.v(lambda e, ti=ti, ga=ga: e.tensor_tensor(
                        sc_sq[:, :], ctx_tm[:, ti * C:(ti + 1) * C],
                        ga, OP.mult), inc="s_v")
                    p.wait("vector", "s_v")
                    p.v(lambda e, ti=ti: e.tensor_tensor(
                        x_tm[:, ti * C:(ti + 1) * C],
                        x_tm[:, ti * C:(ti + 1) * C], sc_sq[:, :],
                        OP.add), inc="s_v")
                    p.wait("vector", "s_v")
            join_all()
            close_f1()

        (xnT1,), close_xnT1 = sbuf_group(("xnT1", 128, NCH * T, BF16))
        emit_ln_xnT(1e-5, xnT1)
        emit_attention(1, xnT1)
        if debug_name == "ctx1":
            join_all(("gpsimd",))
            p.dma("gpsimd", lambda: dbg.rearrange("(p a) m -> p (a m)", p=128),
                  lambda: ctx_tm[:, :], "s_gp")
            p.wait("gpsimd", "s_gp", p.c("s_gp"))
        emit_attn_proj(1, xnT1)
        if debug_name == "s1":
            join_all(("gpsimd",))
            p.dma("gpsimd", lambda: dbg.rearrange("(p a) m -> p (a m)", p=128),
                  lambda: ctx_tm[:, :], "s_gp")
            p.wait("gpsimd", "s_gp", p.c("s_gp"))
        emit_ffn1(xnT1)
        close_xnT1()

        if debug_name == "xnt0":
            dump_dbg(lambda: xnT[:, :])
        if debug_name == "ctx0":
            dump_dbg(lambda: ctx_tm[:, :])

        join_all(("sync",))
        p.dma("sync",
              lambda: out.rearrange("(i p) m -> p i m", p=128),
              lambda: x_tm[:, :].rearrange("p (i m) -> p i m", m=C),
              "s_out")
        out_done = p.c("s_out")
        p.wait("gpsimd", "s_out", out_done)

        with contextlib.ExitStack() as es2:
            sems = {s: es2.enter_context(nc.semaphore(s)) for s in SEMS}
            p.sems = sems
            with nc.Block() as block:
                @block.tensor
                def _(e):
                    for f in p.ops["tensor"]:
                        f(e)

                @block.vector
                def _(e):
                    for f in p.ops["vector"]:
                        f(e)

                @block.scalar
                def _(e):
                    for f in p.ops["scalar"]:
                        f(e)

                @block.gpsimd
                def _(e):
                    for f in p.ops["gpsimd"]:
                        f(e)

                @block.sync
                def _(e):
                    for f in p.ops["sync"]:
                        f(e)
    return nc


def _prep_inputs(inputs):
    d = dict(inputs)
    feed = {}
    feed["L0_attn_w_qkv"] = np.ascontiguousarray(d["L0_attn_w_qkv"], np.float32)
    feed["L0_attn_w_proj"] = np.ascontiguousarray(d["L0_attn_w_proj"], np.float32)
    feed["L0_ffn_w_fc"] = np.ascontiguousarray(d["L0_ffn_w_fc"], np.float32)
    feed["L0_ffn_b_fc"] = np.ascontiguousarray(d["L0_ffn_b_fc"], np.float32)
    feed["L0_ffn_w_proj"] = np.ascontiguousarray(d["L0_ffn_w_proj"], np.float32)
    feed["L0_ffn_b_proj"] = np.ascontiguousarray(d["L0_ffn_b_proj"], np.float32)
    feed["L1_attn_Wq"] = np.ascontiguousarray(d["L1_attn_Wq"], np.float32).reshape(C, C)
    feed["L1_attn_bq"] = np.ascontiguousarray(d["L1_attn_bq"], np.float32).reshape(C)
    feed["L1_attn_Wk"] = np.ascontiguousarray(d["L1_attn_Wk"], np.float32).reshape(C, C)
    feed["L1_attn_bk"] = np.ascontiguousarray(d["L1_attn_bk"], np.float32).reshape(C)
    feed["L1_attn_Wv"] = np.ascontiguousarray(d["L1_attn_Wv"], np.float32).reshape(C, C)
    feed["L1_attn_bv"] = np.ascontiguousarray(d["L1_attn_bv"], np.float32).reshape(C)
    feed["L1_attn_Wo"] = np.ascontiguousarray(d["L1_attn_Wo"], np.float32).reshape(C, C)
    feed["L1_attn_bo"] = np.ascontiguousarray(d["L1_attn_bo"], np.float32)
    feed["L1_attn_gamma"] = np.ascontiguousarray(d["L1_attn_gamma"], np.float32)
    feed["L1_attn_beta"] = np.ascontiguousarray(d["L1_attn_beta"], np.float32)
    feed["L1_ffn_w_gate"] = np.ascontiguousarray(d["L1_ffn_w_gate"], np.int32)
    feed["L1_ffn_w_up"] = np.ascontiguousarray(d["L1_ffn_w_up"], np.int32)
    feed["L1_ffn_w_down"] = np.ascontiguousarray(d["L1_ffn_w_down"], np.int32)
    feed.update(_host_consts())
    xs = np.ascontiguousarray(np.asarray(d["L0_attn_x"], np.float32))
    return feed, xs


def kernel(**inputs):
    feed, xs = _prep_inputs(inputs)
    nc = build()
    in_maps = []
    for b in range(8):
        m = dict(feed)
        m["x"] = xs[b]
        in_maps.append(m)
    res = run_bass_kernel_spmd(nc, in_maps, core_ids=list(range(8)))
    return np.stack([np.asarray(res.results[b]["out"]) for b in range(8)], axis=0)


# version 20
# speedup vs baseline: 1.3874x; 1.0390x over previous
import contextlib
import numpy as np
import ml_dtypes
import concourse.bass as bass
import concourse.mybir as mybir
from concourse.bass_utils import run_bass_kernel_spmd

F32 = mybir.dt.float32
BF16 = mybir.dt.bfloat16
I32 = mybir.dt.int32
OP = mybir.AluOpType
AF = mybir.ActivationFunctionType
AXX = mybir.AxisListType.X
BF = ml_dtypes.bfloat16

T = 2048
C = 1024
NT = T // 128
NCH = C // 128
NHP = 8
D = 64
U0 = 12.0
FFN0 = 4096
FFN1 = 3072
NEG = -1.0e30
C127H = float(np.float32(1.0 / 127.0))
C127L = float(np.float32(1.0 / 127.0 - np.float64(np.float32(1.0 / 127.0))))

SEMS = ("s_ld", "s_gp", "s_sc", "s_t", "s_v", "s_a", "s_g", "s_out")


class Prog:

    def __init__(self):
        self.ops = {k: [] for k in ("tensor", "vector", "scalar", "gpsimd", "sync")}
        self.cnt = {s: 0 for s in SEMS}
        self.sems = None

    def _add(self, eng, fn, inc=None, n=1):
        tgt = None
        if inc is not None:
            self.cnt[inc] += n
            tgt = self.cnt[inc]
        sem = inc

        def emit(e):
            ins = fn(e)
            if sem is not None:
                ins.then_inc(self.sems[sem], n)
        self.ops[eng].append(emit)
        return tgt

    def t(self, fn, inc=None, n=1):
        return self._add("tensor", fn, inc, n)

    def v(self, fn, inc=None, n=1):
        return self._add("vector", fn, inc, n)

    def s(self, fn, inc=None, n=1):
        return self._add("scalar", fn, inc, n)

    def g(self, fn, inc=None, n=1):
        return self._add("gpsimd", fn, inc, n)

    def y(self, fn, inc=None, n=1):
        return self._add("sync", fn, inc, n)

    def dma(self, eng, out_ap_fn, in_ap_fn, sem):
        return self._add(eng, lambda e: e.dma_start(out=out_ap_fn(), in_=in_ap_fn()),
                         sem, 16)

    def wait(self, eng, sem, val=None):
        v = self.cnt[sem] if val is None else val
        self.ops[eng].append(lambda e: e.wait_ge(self.sems[sem], v))

    def c(self, sem):
        return self.cnt[sem]


def _host_consts():
    half = D // 2
    theta = 1.0 / (10000.0 ** (np.arange(half, dtype=np.float64) / half))
    pos = np.arange(T, dtype=np.float64)
    freqs = pos[:, None] * theta[None, :]
    cosv = np.cos(freqs)
    sinv = np.sin(freqs)
    idx = np.arange(D) // 2
    cos_fm = cosv[:, idx].T
    sin_fm = sinv[:, idx].T
    cos_fm = np.concatenate([cos_fm, cos_fm], 0)
    sin_fm = np.concatenate([sin_fm, sin_fm], 0)
    cos_hi = cos_fm.astype(np.float32)
    sin_hi = sin_fm.astype(np.float32)
    cos_lo = (cos_fm - cos_hi).astype(np.float32)
    sin_lo = (sin_fm - sin_hi).astype(np.float32)
    perm = np.zeros((128, 128), dtype=np.float32)
    for m in range(128):
        base = (m // 64) * 64
        j = m % 64
        if j < 32:
            perm[base + j + 32, m] = -1.0
        else:
            perm[base + j - 32, m] = 1.0
    ident = np.eye(128, dtype=np.float32)
    out = {
        "c_cos_hi": cos_hi, "c_sin_hi": sin_hi,
        "c_cos_lo": np.asarray(cos_lo), "c_sin_lo": np.asarray(sin_lo),
        "c_perm": perm.astype(BF), "c_ident": ident.astype(BF),
    }
    r = np.arange(128)[:, None]
    cc = np.arange(256)[None, :]
    for o in range(2):
        m = np.zeros((128, 256), dtype=np.float32)
        m[(o * 128 + r > cc)] = NEG
        out[f"c_mask{o}"] = m.astype(BF)
    return out


def _ln_stats(p, xt, sc_sq, sums, sumsq, stat, eps, i):
    p.wait("scalar", "s_v")
    p.v(lambda e: e.tensor_reduce(sums[:, i:i + 1], xt, AXX, OP.add), inc="s_v")
    p.s(lambda e: e.activation(sc_sq[:, :], xt, AF.Square,
                               accum_out=sumsq[:, i:i + 1]), inc="s_a")
    p.wait("vector", "s_a")
    p.wait("vector", "s_v")
    mean = stat[:, 2 * i:2 * i + 1]
    rstd = stat[:, 2 * i + 1:2 * i + 2]
    p.v(lambda e: e.tensor_scalar(mean, sums[:, i:i + 1], 1.0 / C, None, OP.mult))
    p.v(lambda e: e.tensor_scalar(rstd, sumsq[:, i:i + 1], 1.0 / C, None,
                                  OP.mult), inc="s_v")
    p.wait("vector", "s_v")
    p.v(lambda e: e.tensor_tensor(sums[:, i:i + 1], mean, mean, OP.mult),
        inc="s_v")
    p.wait("vector", "s_v")
    p.v(lambda e: e.tensor_tensor(rstd, rstd, sums[:, i:i + 1], OP.subtract),
        inc="s_v")
    p.wait("vector", "s_v")
    p.v(lambda e: e.tensor_scalar(rstd, rstd, float(eps), None, OP.add),
        inc="s_v")
    p.wait("scalar", "s_v")
    p.s(lambda e: e.activation(rstd, rstd, AF.Ln), inc="s_a")
    p.wait("scalar", "s_a")
    p.s(lambda e: e.activation(rstd, rstd, AF.Exp, scale=-0.5), inc="s_a")
    return mean, rstd


def build(n_layer_stop=None, debug_name=None):
    nc = bass.Bass("TRN2", target_bir_lowering=False,
                   detect_race_conditions=False)

    x_in = nc.dram_tensor("x", [T, C], F32, kind="ExternalInput").ap()
    out = nc.dram_tensor("out", [T, C], F32, kind="ExternalOutput").ap()
    dbg = None
    if debug_name is not None:
        dbg = nc.dram_tensor("dbg", [T, C], F32, kind="ExternalOutput").ap()

    w_qkv = nc.dram_tensor("L0_attn_w_qkv", [C, 3 * C], F32, kind="ExternalInput").ap()
    l0_wproj = nc.dram_tensor("L0_attn_w_proj", [C, C], F32, kind="ExternalInput").ap()
    l0_wfc = nc.dram_tensor("L0_ffn_w_fc", [FFN0, C], F32, kind="ExternalInput").ap()
    l0_bfc = nc.dram_tensor("L0_ffn_b_fc", [FFN0], F32, kind="ExternalInput").ap()
    l0_wp = nc.dram_tensor("L0_ffn_w_proj", [C, FFN0], F32, kind="ExternalInput").ap()
    l0_bp = nc.dram_tensor("L0_ffn_b_proj", [C], F32, kind="ExternalInput").ap()
    l1_wq = nc.dram_tensor("L1_attn_Wq", [C, C], F32, kind="ExternalInput").ap()
    l1_bq = nc.dram_tensor("L1_attn_bq", [C], F32, kind="ExternalInput").ap()
    l1_wk = nc.dram_tensor("L1_attn_Wk", [C, C], F32, kind="ExternalInput").ap()
    l1_bk = nc.dram_tensor("L1_attn_bk", [C], F32, kind="ExternalInput").ap()
    l1_wv = nc.dram_tensor("L1_attn_Wv", [C, C], F32, kind="ExternalInput").ap()
    l1_bv = nc.dram_tensor("L1_attn_bv", [C], F32, kind="ExternalInput").ap()
    l1_wo = nc.dram_tensor("L1_attn_Wo", [C, C], F32, kind="ExternalInput").ap()
    l1_bo = nc.dram_tensor("L1_attn_bo", [C], F32, kind="ExternalInput").ap()
    l1_gamma = nc.dram_tensor("L1_attn_gamma", [C], F32, kind="ExternalInput").ap()
    l1_beta = nc.dram_tensor("L1_attn_beta", [C], F32, kind="ExternalInput").ap()
    l1_wg = nc.dram_tensor("L1_ffn_w_gate", [FFN1, C], I32, kind="ExternalInput").ap()
    l1_wu = nc.dram_tensor("L1_ffn_w_up", [FFN1, C], I32, kind="ExternalInput").ap()
    l1_wd = nc.dram_tensor("L1_ffn_w_down", [C, FFN1], I32, kind="ExternalInput").ap()

    cmaps = _host_consts()
    cdram = {k: nc.dram_tensor(k, list(v.shape),
                               BF16 if v.dtype == BF else F32,
                               kind="ExternalInput").ap()
             for k, v in cmaps.items()}
    wfcq_hbm = nc.dram_tensor("wfcq_scratch", [FFN0, C], BF16).ap()
    wprojq_hbm = nc.dram_tensor("wprojq_scratch", [C, FFN0], BF16).ap()
    wg_bf_hbm = nc.dram_tensor("wg_bf_scratch", [FFN1, C], BF16).ap()
    wu_bf_hbm = nc.dram_tensor("wu_bf_scratch", [FFN1, C], BF16).ap()
    wd_bf_hbm = nc.dram_tensor("wd_bf_scratch", [C, FFN1], BF16).ap()

    p = Prog()
    es = contextlib.ExitStack()
    with es:
        def sb(name, part, cols, dt):
            return es.enter_context(nc.sbuf_tensor(name, [part, cols], dt))

        def psum(name, part, cols, dt=F32):
            return es.enter_context(nc.psum_tensor(name, [part, cols], dt))

        def sbuf_group(*specs):
            hs, cms = [], []
            for (name, part, cols, dt) in specs:
                cm = nc.sbuf_tensor(name, [part, cols], dt)
                hs.append(cm.__enter__())
                cms.append(cm)

            def close():
                for cm in reversed(cms):
                    cm.__exit__(None, None, None)
            return hs, close

        x_tm = sb("x_tm", 128, NT * C, F32)
        ctx_tm = sb("ctx_tm", 128, NT * C, BF16)
        cb = {"c_ident": sb("c_ident_t", 128, 128, BF16)}
        sums = sb("sums", 128, NT, F32)
        sumsq = sb("sumsq", 128, NT, F32)
        stat = sb("stat", 128, 2 * NT, F32)
        sc_sq = sb("sc_sq", 128, C, F32)
        cvec = sb("cvec", 128, 4, F32)

        def psum_phase(*specs):
            hs, ctxs = [], []
            for (name, part, cols, dt) in specs:
                cm = nc.psum_tensor(name, [part, cols], dt)
                hs.append(cm.__enter__())
                ctxs.append(cm)
            for cm in reversed(ctxs):
                cm.__exit__(None, None, None)
            return hs

        def sbuf_phase(*specs):
            hs, ctxs = [], []
            for (name, part, cols, dt) in specs:
                cm = nc.sbuf_tensor(name, [part, cols], dt)
                hs.append(cm.__enter__())
                ctxs.append(cm)
            for cm in reversed(ctxs):
                cm.__exit__(None, None, None)
            return hs

        pt_bf, pt_bf2 = psum_phase(("pt_bf", 128, 4096, BF16),
                                   ("pt_bf2", 128, 4096, BF16))

        p.dma("sync", lambda: cb["c_ident"][:, :], lambda: cdram["c_ident"],
              "s_ld")
        p.dma("sync",
              lambda: x_tm[:, :].rearrange("p (i m) -> p i m", m=C),
              lambda: x_in.rearrange("(i p) m -> p i m", p=128),
              "s_ld")
        ld_done = p.c("s_ld")

        def load_consts(keys):
            for k in keys:
                v = cmaps[k]
                dt = BF16 if v.dtype == BF else F32
                cb[k] = None
            specs = [(k + "_t", cmaps[k].shape[0], cmaps[k].shape[1],
                      BF16 if cmaps[k].dtype == BF else F32) for k in keys]
            hs, close = sbuf_group(*specs)
            for k, h in zip(keys, hs):
                cb[k] = h
                p.dma("sync", lambda h=h: h[:, :], lambda dk=cdram[k]: dk,
                      "s_ld")
            p.wait("vector", "s_ld")
            p.wait("gpsimd", "s_ld")
            p.wait("tensor", "s_ld")
            p.wait("scalar", "s_ld")
            return close

        tp_bank_free = {0: 0, 1: 0}

        def emit_T_transposes(dst, dst_col0, dst_stride, src, src_col0,
                              src_stride, ntile, evac_eng="vector"):
            dcols = dst.shape[1] if hasattr(dst, "shape") else None
            for g0 in range(0, ntile, 4):
                n = min(4, ntile - g0)
                bank = (g0 // 4) % 2
                pt = pt_bf if bank == 0 else pt_bf2
                p.wait("tensor", "s_v", tp_bank_free[bank])

                def tr(e, g0=g0, n=n, pt=pt):
                    for j in range(n):
                        e.transpose(
                            pt[:, j * 1024:j * 1024 + 128],
                            src[:, src_col0 + (g0 + j) * src_stride:
                                src_col0 + (g0 + j) * src_stride + 128],
                            cb["c_ident"][:, :])
                    return e.nop()
                p.t(tr, inc="s_t")
                p.wait(evac_eng, "s_t", p.c("s_t"))

                def ev(e, g0=g0, n=n, pt=pt):
                    last = None
                    for j in range(n):
                        c0 = dst_col0 + (g0 + j) * dst_stride
                        last = (e.tensor_copy if evac_eng == "vector"
                                else e.copy)(dst[:, c0:c0 + 128],
                                             pt[:, j * 1024:j * 1024 + 128])
                    return last
                if evac_eng == "vector":
                    p.v(ev, inc="s_v")
                else:
                    p.s(ev, inc="s_v")
                tp_bank_free[bank] = p.c("s_v")

        def emit_xnT_tile(xnT, i, src_bf_tile):
            emit_T_transposes(xnT, i * 128, T, src_bf_tile, 0, 128, NCH)

        def emit_ln_xnT(eps, xnT):
            join_all()
            (xn_bf,), close_xb = sbuf_group(
                (f"xn_bf{p.c('s_v')}", 128, 2 * C, BF16))
            tp_bank_free[0] = p.c("s_v")
            tp_bank_free[1] = p.c("s_v")
            xb_free = {0: 0, 1: 0}
            for i in range(NT):
                xt = x_tm[:, i * C:(i + 1) * C]
                mean, rstd = _ln_stats(p, xt, sc_sq, sums, sumsq, stat, eps, i)
                p.wait("vector", "s_a", p.c("s_a"))
                slot = i % 2
                p.wait("vector", "s_t", xb_free[slot])
                p.wait("vector", "s_a")
                xb = xn_bf[:, slot * C:(slot + 1) * C]
                p.v(lambda e, xt=xt, xb=xb, m=mean, r=rstd:
                    e.tensor_scalar(xb, xt, m, r, OP.subtract, OP.mult),
                    inc="s_v")
                p.wait("tensor", "s_v", p.c("s_v"))
                import os as _os2
                _st = _os2.environ.get("K_STAGE", "")
                if _st != "ln" and not (_st == "ln1" and i > 0):
                    emit_xnT_tile(xnT, i, xb)
                xb_free[slot] = p.c("s_t")
            close_xb()

        def join_all(engines=("tensor", "vector", "scalar", "gpsimd", "sync")):
            for eng in engines:
                for s in SEMS:
                    if p.cnt[s]:
                        p.wait(eng, s)

        def dump_dbg(src_fn, cast=False):
            join_all(("gpsimd",))
            p.dma("gpsimd",
                  lambda: dbg.rearrange("(p a) m -> p (a m)", p=128),
                  src_fn, "s_gp")
            p.wait("gpsimd", "s_gp", p.c("s_gp"))

        p.v(lambda e: e.memset(cvec[:, 0:1], -U0))
        p.v(lambda e: e.memset(cvec[:, 2:3], 0.125))
        p.v(lambda e: e.memset(cvec[:, 1:2], 1.0), inc="s_v")

        p.wait("vector", "s_ld", ld_done)
        p.wait("scalar", "s_ld", ld_done)
        p.wait("tensor", "s_ld", ld_done)

        def emit_attention(L, xnT):
            causal = (L == 0)
            pstAB, pstAB2, pc0, pc1, pc2, pc3 = psum_phase(
                (f"pstAB{L}", 128, 1024, F32), (f"pstAB2{L}", 128, 1024, F32),
                (f"pc0_{L}", 128, 512, F32), (f"pc1_{L}", 128, 512, F32),
                (f"pc2_{L}", 128, 512, F32), (f"pc3_{L}", 128, 512, F32))
            pmm, = psum_phase((f"pmm{L}", 128, 1024, F32))
            pvt_a, pvt_b = psum_phase((f"pvt_a{L}", 128, 1024, BF16),
                                      (f"pvt_b{L}", 128, 1024, BF16))
            psts = [pstAB, pstAB2]
            pcs = [pc0, pc1, pc2, pc3]
            (wsl, qraw, rot_sb, r1, r2, qf, kf, vaa, vab, pT, rc, bql,
             qlo, klo) = sbuf_phase(
                (f"wsl{L}", 128, 3072, BF16),
                (f"qraw{L}", 128, T, BF16),
                (f"rot{L}", 128, T, BF16),
                (f"r1_{L}", 128, T, F32),
                (f"r2_{L}", 128, T, BF16),
                (f"qf{L}", 128, T, BF16),
                (f"kf{L}", 128, T, BF16),
                (f"vaa{L}", 128, 16 * 65, BF16),
                (f"vab{L}", 128, 16 * 65, BF16),
                (f"pT{L}", 128, 2 * 512, BF16),
                (f"rc{L}", 128, 8, F32),
                (f"bql{L}", 128, 24, F32),
                (f"qlo{L}", 128, 256, BF16),
                (f"klo{L}", 128, 256, BF16))
            join_all()
            p.v(lambda e: e.memset(vaa[:, :], 1.0))
            p.v(lambda e: e.memset(vab[:, :], 1.0), inc="s_v")
            if L == 1:
                for name, dram, col0 in (("q", l1_bq, 0), ("k", l1_bk, 8),
                                         ("v", l1_bv, 16)):
                    p.y(lambda e, c0=col0, d=dram: e.dma_start(
                        out=bass.AP(bql, c0, [[24, 128], [1, 8]]),
                        in_=d.rearrange("(j q) -> q j", q=128),
                        allow_slow_non_contiguous=True), inc="s_ld", n=16)
                p.wait("vector", "s_ld", p.c("s_ld"))
                p.v(lambda e: e.tensor_scalar(bql[:, 0:8], bql[:, 0:8], 0.125,
                                              None, OP.mult), inc="s_v")
                p.wait("scalar", "s_v")

            last_av = {"t": 0}
            evac_done = {"v": 0, "a": 0}
            evac_half = {0: 0, 1: 0}
            wsl_free = 0

            for hp in range(NHP):
                p.wait("gpsimd", "s_t", wsl_free)
                for i3, (dram, off) in enumerate(
                        [(w_qkv if L == 0 else l1_wq, 0),
                         (w_qkv if L == 0 else l1_wk, C if L == 0 else 0),
                         (w_qkv if L == 0 else l1_wv, 2 * C if L == 0 else 0)]):
                    wid = 3 * C if L == 0 else C
                    p.dma("gpsimd",
                          lambda i3=i3: bass.AP(wsl, i3 * 1024,
                                                [[3072, 128], [128, 8], [1, 128]]),
                          lambda dram=dram, off=off, wid=wid, hp=hp:
                          bass.AP(dram.tensor, off + hp * 128,
                                  [[wid, 128], [128 * wid, 8], [1, 128]]),
                          "s_gp")
                wload = p.c("s_gp")

                def qkv_proj(qi, dst, bias_col, scale):
                    p.wait("tensor", "s_gp", wload)
                    p.wait("tensor", "s_v", evac_done["v"])
                    p.wait("tensor", "s_a")
                    p.wait("scalar", "s_v")
                    p.wait("scalar", "s_g")
                    for ch in range(4):
                        half = ch % 2
                        p.wait("tensor", "s_v", evac_half[half])
                        def mm(e, qi=qi, ch=ch, half=half):
                            last = None
                            for c in range(NCH):
                                last = e.matmul(
                                    pmm[:, half * 512:(half + 1) * 512],
                                    wsl[:, qi * 1024 + c * 128:qi * 1024 + (c + 1) * 128],
                                    xnT[:, c * T + ch * 512:c * T + (ch + 1) * 512],
                                    start=(c == 0), stop=(c == NCH - 1))
                            return last
                        p.t(mm, inc="s_t")
                        p.wait("vector", "s_t", p.c("s_t"))
                        def ev(e, ch=ch, half=half, dst=dst, bias_col=bias_col,
                               scale=scale):
                            dsl = dst[:, ch * 512:(ch + 1) * 512]
                            psl = pmm[:, half * 512:(half + 1) * 512]
                            if bias_col is None:
                                return e.tensor_copy(dsl, psl)
                            bap = bql[:, bias_col:bias_col + 1]
                            if scale is not None:
                                return e.tensor_scalar(
                                    dsl, psl, cvec[:, 2:3], bap,
                                    OP.mult, OP.add)
                            return e.tensor_scalar(dsl, psl, bap, None,
                                                   OP.add)
                        p.v(ev, inc="s_v")
                        evac_done["a"] = p.c("s_v")
                        evac_half[half] = p.c("s_v")

                if L == 0:
                    qkv_proj(0, qraw, None, None)
                    def rope(dstf, dlo):
                        for ch in range(4):
                            half = ch % 2
                            p.wait("tensor", "s_v", evac_half[half])
                            def mmr(e, ch=ch, half=half):
                                return e.matmul(
                                    pmm[:, half * 512:(half + 1) * 512],
                                    cb["c_perm"][:, :],
                                    qraw[:, ch * 512:(ch + 1) * 512],
                                    start=True, stop=True)
                            p.t(mmr, inc="s_t")
                            p.wait("vector", "s_t", p.c("s_t"))
                            def evr(e, ch=ch, half=half):
                                return e.tensor_copy(
                                    rot_sb[:, ch * 512:(ch + 1) * 512],
                                    pmm[:, half * 512:(half + 1) * 512])
                            p.v(evr, inc="s_v")
                            evac_done["a"] = p.c("s_v")
                            evac_half[half] = p.c("s_v")
                        p.wait("vector", "s_a")
                        def ropechain(e, dstf=dstf):
                            e.tensor_tensor(r1[:, :], qraw[:, :],
                                            cb["c_cos_hi"][:, :], OP.mult)
                            e.tensor_tensor(r2[:, :], rot_sb[:, :],
                                            cb["c_sin_hi"][:, :], OP.mult)
                            e.tensor_tensor(r1[:, :], r1[:, :], r2[:, :],
                                            OP.add)
                            e.tensor_tensor(r2[:, :], qraw[:, :],
                                            cb["c_cos_lo"][:, :], OP.mult)
                            e.tensor_tensor(r1[:, :], r1[:, :], r2[:, :],
                                            OP.add)
                            e.tensor_tensor(r2[:, :], rot_sb[:, :],
                                            cb["c_sin_lo"][:, :], OP.mult)
                            e.tensor_tensor(r1[:, :], r1[:, :], r2[:, :],
                                            OP.add)
                            return e.tensor_copy(dstf[:, :], r1[:, :])
                        p.v(ropechain, inc="s_v")
                        p.wait("vector", "s_v")
                        p.v(lambda e, dstf=dstf, dlo=dlo: e.scalar_tensor_tensor(
                            dlo[:, :], dstf[:, 0:256], -1.0, r1[:, 0:256],
                            OP.mult, OP.add), inc="s_v")
                        p.wait("vector", "s_v")
                    rope(qf, qlo)
                    qkv_proj(1, qraw, None, None)
                    rope(kf, klo)
                    qkv_proj(2, rot_sb, None, None)
                    vsrc = rot_sb
                else:
                    qkv_proj(0, qf, 0 + hp, 0.125)
                    qkv_proj(1, kf, 8 + hp, None)
                    qkv_proj(2, rot_sb, 16 + hp, None)
                    vsrc = rot_sb

                p.wait("tensor", "s_v", evac_done["a"])
                for st2 in range(0, 16, 2):
                    for j, pvt in ((0, pvt_a), (1, pvt_b)):
                        sc = st2 + j
                        if sc >= 16:
                            continue
                        p.wait("tensor", "s_v", evac_done["v"])
                        def trv(e, sc=sc, pvt=pvt, vsrc=vsrc):
                            return e.transpose(pvt[:, 0:128],
                                               vsrc[:, sc * 128:(sc + 1) * 128],
                                               cb["c_ident"][:, :])
                        p.t(trv, inc="s_t")
                        p.wait("vector", "s_t", p.c("s_t"))
                        def evv(e, sc=sc, pvt=pvt):
                            e.tensor_copy(vaa[:, sc * 65:sc * 65 + 64],
                                          pvt[:, 0:64])
                            return e.tensor_copy(vab[:, sc * 65:sc * 65 + 64],
                                                 pvt[:, 64:128])
                        p.v(evv, inc="s_v")
                        evac_done["v"] = p.c("s_v")

                p.wait("tensor", "s_v", evac_done["v"])
                for tq in range(8):
                    n_sc = min(16, 2 * tq + 2) if causal else 16
                    exp_guard = [p.c("s_a"), p.c("s_a")]
                    pT_guard = [0, 0]

                    def emit_mav(sc, n_sc=n_sc):
                        dbuf = sc % 2
                        p.wait("tensor", "s_a", exp_guard[dbuf])
                        def mav(e, sc=sc, dbuf=dbuf, first=(sc == 0),
                                last=(sc == n_sc - 1)):
                            last_i = None
                            for h2 in range(2):
                                last_i = e.matmul(
                                    pcs[h2][:, 0:65],
                                    pT[:, dbuf * 512 + h2 * 128:
                                       dbuf * 512 + (h2 + 1) * 128],
                                    vaa[:, sc * 65:(sc + 1) * 65],
                                    start=first, stop=last)
                                last_i = e.matmul(
                                    pcs[2 + h2][:, 0:65],
                                    pT[:, dbuf * 512 + 256 + h2 * 128:
                                       dbuf * 512 + 256 + (h2 + 1) * 128],
                                    vab[:, sc * 65:(sc + 1) * 65],
                                    start=first, stop=last)
                            return last_i
                        p.t(mav, inc="s_t")
                        pT_guard[dbuf] = p.c("s_t")

                    for sc in range(n_sc):
                        dbuf = sc % 2
                        pst = psts[dbuf]
                        p.wait("tensor", "s_a", exp_guard[dbuf])
                        hilo = causal and tq == 0
                        def mst(e, sc=sc, tq=tq, pst=pst, hilo=hilo):
                            terms = [(kf, qf, True, not hilo)]
                            if hilo:
                                terms += [(kf, qlo, False, False),
                                          (klo, qf, False, True)]
                            last = None
                            for (kk, qq, st0, sp0) in terms:
                                kap = (kk[:, sc * 128:(sc + 1) * 128]
                                       if kk is not klo else kk[:, :]
                                       [:, sc * 128:(sc + 1) * 128])
                                qap = (qq[:, tq * 256:(tq + 1) * 256]
                                       if qq is not qlo else qq[:, 0:256])
                                e.matmul(pst[:, 256:512], kap[0:64, :],
                                         qap[0:64, :], start=st0, stop=sp0,
                                         tile_position=(0, 0))
                                last = e.matmul(pst[:, 512:768], kap[64:128, :],
                                                qap[64:128, :], start=st0,
                                                stop=sp0,
                                                tile_position=(64, 0))
                            return last
                        p.t(mst, inc="s_t")
                        mst_done = p.c("s_t")
                        masked = causal and (sc * 128 >= tq * 256)
                        if masked:
                            mk = "c_mask%d" % (sc - 2 * tq)
                            p.wait("vector", "s_t", mst_done)
                            p.v(lambda e, pst=pst, mk=mk: e.tensor_tensor(
                                pst[:, 256:512], pst[:, 256:512], cb[mk][:, :],
                                OP.add), inc="s_v")
                            p.v(lambda e, pst=pst, mk=mk: e.tensor_tensor(
                                pst[:, 512:768], pst[:, 512:768], cb[mk][:, :],
                                OP.add), inc="s_v")
                            p.wait("scalar", "s_v")
                        p.wait("scalar", "s_t", mst_done)
                        p.wait("scalar", "s_t", pT_guard[dbuf])
                        def evex(e, pst=pst, dbuf=dbuf):
                            return e.activation(pT[:, dbuf * 512:(dbuf + 1) * 512],
                                                pst[:, 256:768], AF.Exp,
                                                scale=(0.125 if L == 0 else 1.0),
                                                bias=cvec[:, 0:1])
                        p.s(evex, inc="s_a")
                        exp_guard[dbuf] = p.c("s_a")
                        if sc > 0:
                            emit_mav(sc - 1)
                    emit_mav(n_sc - 1)
                    av_done = p.c("s_t")
                    p.wait("vector", "s_t", av_done)
                    for g in range(4):
                        head = 0 if g < 2 else 1
                        h2 = g % 2
                        tile = 2 * tq + h2
                        col = tile * C + (2 * hp + head) * D
                        p.v(lambda e, g=g: e.reciprocal(rc[:, g:g + 1],
                                                        pcs[g][:, 64:65]),
                            inc="s_v")
                        p.wait("vector", "s_v")
                        p.v(lambda e, g=g, col=col: e.tensor_scalar(
                            ctx_tm[:, col:col + D], pcs[g][:, 0:D],
                            rc[:, g:g + 1], None, OP.mult), inc="s_v")
                    p.wait("tensor", "s_v", p.c("s_v"))
                wsl_free = p.c("s_t")
            join_all()

        def emit_bcast(dst, dram_ap, n=C, off=0):
            p.g(lambda e: e.dma_start(
                out=dst[:, :],
                in_=bass.AP(dram_ap.tensor, off, [[0, 128], [1, n]])),
                inc="s_gp", n=16)

        def emit_attn_proj(L, xnT, slo=None):
            ptp, pproj = psum_phase((f"ptp{L}", 128, 4096, BF16),
                                    (f"pproj{L}", 128, 1024, F32))
            if L == 0:
                wpj, ctxT = sbuf_phase((f"wpj{L}", 128, NCH * C, BF16),
                                       (f"ctxT{L}", 128, C, BF16))
                yb = gf = bof = None
            else:
                wpj, ctxT, yb, xnf, bof, gaf, bef = sbuf_phase(
                    (f"wpj{L}", 128, NCH * C, BF16),
                    (f"ctxT{L}", 128, C, BF16),
                    (f"yb{L}", 128, C, F32),
                    (f"xnf{L}", 128, C, F32),
                    (f"bof{L}", 128, C, F32),
                    (f"gaf{L}", 128, C, F32),
                    (f"bef{L}", 128, C, F32))
            join_all()
            wsrc = l0_wproj if L == 0 else l1_wo
            p.dma("gpsimd",
                  lambda: bass.AP(wpj, 0, [[NCH * C, 128], [C, 8], [1, C]]),
                  lambda: bass.AP(wsrc.tensor, 0, [[C, 128], [128 * C, 8], [1, C]]),
                  "s_gp")
            if L == 1:
                emit_bcast(bof, l1_bo)
                emit_bcast(gaf, l1_gamma)
                emit_bcast(bef, l1_beta)
                p.wait("vector", "s_ld", p.c("s_ld"))
            p.wait("tensor", "s_gp", p.c("s_gp"))
            tp_free = 0
            import os as _os6
            _ks = _os6.environ.get("K_STAGE", "")
            _nt = 1 if _ks == "proj1" else (2 if _ks == "proj2" else NT)
            for i in range(_nt):
                srcs = [(ctx_tm, i * C, 128, ctxT, 0, 128, 8)]
                if _ks == "projnotp" and i > 0:
                    srcs = []
                if L == 1:
                    p.wait("vector", "s_a")
                    p.v(lambda e, i=i: e.tensor_scalar(
                        xnf[:, :], x_tm[:, i * C:(i + 1) * C],
                        stat[:, 2 * i:2 * i + 1], stat[:, 2 * i + 1:2 * i + 2],
                        OP.subtract, OP.mult), inc="s_v")
                p.wait("tensor", "s_a")
                p.wait("tensor", "s_v")
                for (stens, sc0, sstr, dtens, dc0, dstr, ntile) in srcs:
                    for g0 in (0, 4):
                        p.wait("tensor", "s_v", tp_free)
                        def trc(e, stens=stens, sc0=sc0, sstr=sstr, g0=g0):
                            for j in range(4):
                                e.transpose(ptp[:, j * 1024:j * 1024 + 128],
                                            stens[:, sc0 + (g0 + j) * sstr:
                                                  sc0 + (g0 + j) * sstr + 128],
                                            cb["c_ident"][:, :])
                            return e.nop()
                        p.t(trc, inc="s_t")
                        p.wait("vector", "s_t", p.c("s_t"))
                        def evc(e, dtens=dtens, dc0=dc0, dstr=dstr, g0=g0):
                            last = None
                            for j in range(4):
                                last = e.tensor_copy(
                                    dtens[:, dc0 + (g0 + j) * dstr:
                                          dc0 + (g0 + j) * dstr + 128],
                                    ptp[:, j * 1024:j * 1024 + 128])
                            return last
                        p.v(evc, inc="s_v")
                        tp_free = p.c("s_v")
                import os as _os4
                if _os4.environ.get("K_STAGE") == "projtp":
                    continue
                p.wait("tensor", "s_v", tp_free)
                p.wait("tensor", "s_a")
                p.wait("tensor", "s_v")
                for h in range(2):
                    def mmp(e, h=h):
                        last = None
                        for c in range(NCH):
                            last = e.matmul(pproj[:, h * 512:(h + 1) * 512],
                                            ctxT[:, c * 128:(c + 1) * 128],
                                            wpj[:, c * C + h * 512:
                                                c * C + (h + 1) * 512],
                                            start=(c == 0), stop=(c == NCH - 1))
                        return last
                    p.t(mmp, inc="s_t")
                mm_done = p.c("s_t")
                import os as _os5
                if _os5.environ.get("K_STAGE") == "projmm":
                    p.wait("scalar", "s_t", mm_done)
                    p.s(lambda e: e.activation(ctxT[:, 0:512], pproj[:, 0:512],
                                               AF.Identity), inc="s_a")
                    continue
                if _os5.environ.get("K_STAGE") == "projev":
                    p.wait("scalar", "s_t", mm_done)
                    p.s(lambda e: e.activation(ctxT[:, :], pproj[:, :],
                                               AF.Identity), inc="s_a")
                    continue
                if L == 0:
                    p.wait("scalar", "s_t", mm_done)
                    for h in range(2):
                        p.s(lambda e, i=i, h=h: e.activation(
                            ctx_tm[:, i * C + h * 512:i * C + (h + 1) * 512],
                            pproj[:, h * 512:(h + 1) * 512],
                            AF.Identity), inc="s_a")
                    if slo is not None and i < 4:
                        p.wait("vector", "s_a")
                        for h in range(2):
                            p.v(lambda e, i=i, h=h: e.scalar_tensor_tensor(
                                slo[:, i * C + h * 512:i * C + (h + 1) * 512],
                                ctx_tm[:, i * C + h * 512:
                                       i * C + (h + 1) * 512], -1.0,
                                pproj[:, h * 512:(h + 1) * 512],
                                OP.mult, OP.add), inc="s_v")
                else:
                    p.wait("vector", "s_t", mm_done)
                    for h in range(2):
                        p.v(lambda e, h=h: e.scalar_tensor_tensor(
                            yb[:, h * 512:(h + 1) * 512],
                            pproj[:, h * 512:(h + 1) * 512], 1.0,
                            bof[:, h * 512:(h + 1) * 512], OP.mult, OP.add),
                            inc="s_v")
                    p.wait("vector", "s_v")
                    p.v(lambda e: e.tensor_tensor(yb[:, :], yb[:, :],
                                                  xnf[:, :], OP.add),
                        inc="s_v")
                    p.wait("vector", "s_v")
                    mean, rstd = _ln_stats(p, yb[:, :], sc_sq, sums, sumsq,
                                           stat, 1e-3, i)
                    p.wait("vector", "s_a", p.c("s_a"))
                    p.v(lambda e, m=mean, r=rstd: e.tensor_scalar(
                        yb[:, :], yb[:, :], m, r, OP.subtract, OP.mult),
                        inc="s_v")
                    p.wait("vector", "s_v")
                    p.v(lambda e: e.tensor_tensor(yb[:, :], yb[:, :],
                                                  gaf[:, :], OP.mult),
                        inc="s_v")
                    p.wait("vector", "s_v")
                    p.v(lambda e, i=i: e.tensor_tensor(
                        ctx_tm[:, i * C:(i + 1) * C], yb[:, :], bef[:, :],
                        OP.add), inc="s_v")
            join_all()

        (xnT0,), close_xnT0 = sbuf_group(("xnT0", 128, NCH * T, BF16))
        close_l0c = load_consts(["c_cos_hi", "c_sin_hi", "c_cos_lo",
                                 "c_sin_lo", "c_perm", "c_mask0", "c_mask1"])
        emit_ln_xnT(1e-5, xnT0)
        emit_attention(0, xnT0)
        if debug_name == "ctxpre":
            join_all(("gpsimd",))
            p.dma("gpsimd", lambda: dbg.rearrange("(p a) m -> p (a m)", p=128),
                  lambda: ctx_tm[:, :], "s_gp")
            p.wait("gpsimd", "s_gp", p.c("s_gp"))
        close_l0c()
        close_xnT0()
        (slo,), close_slo = sbuf_group(("slo", 128, 4 * C, BF16))
        emit_attn_proj(0, None, slo=slo)

        def emit_quant(src_fn, dst_fn, qs, n_sec=1, veng="vector",
                       multi=False):
            amax, m_, inv, cmp, f_, G = (qs[k] for k in
                                         ("amax", "m", "inv", "cmp", "f", "G"))
            u, cbuf, d, m1, m2 = (qs[k] for k in ("u", "c", "d", "m1", "m2"))
            ef = qs.get("ef")
            nw = n_sec * 32
            def amaxes(e):
                last = None
                for s2 in range(n_sec):
                    src3 = src_fn(s2).rearrange("p (b k) -> p b k", k=32)
                    last = e.tensor_reduce(
                        amax[:, s2 * 32:(s2 + 1) * 32], src3, AXX, OP.max,
                        apply_absolute_value=True)
                return last
            p.v(amaxes, inc="s_v")
            p.wait("vector", "s_v")
            p.v(lambda e: e.tensor_scalar(m_[:, :nw], amax[:, :nw],
                                          float(2.0 ** -64), None, OP.max),
                inc="s_v")
            p.wait("vector", "s_v")
            p.v(lambda e: e.tensor_scalar(m_[:, :nw], m_[:, :nw],
                                          float(2.0 ** 64), None, OP.mult))
            p.v(lambda e: e.memset(inv[:, :nw], float(2.0 ** 67)),
                inc="s_v")
            if not multi:
                p.v(lambda e: e.memset(ef[:, :nw], -67.0), inc="s_v")
            p.wait("vector", "s_v")
            for bit in (64, 32, 16, 8, 4, 2, 1):
                k = min(bit, 16)
                rep = bit // k
                p.v(lambda e, bit=bit: e.tensor_scalar(
                    cmp[:, :nw], m_[:, :nw], float(2.0 ** bit), None,
                    OP.is_ge), inc="s_v")
                p.wait("vector", "s_v")
                p.v(lambda e, k=k: e.tensor_scalar(
                    f_[:, :nw], cmp[:, :nw], float(2.0 ** -k) - 1.0, 1.0,
                    OP.mult, OP.add), inc="s_v")
                if not multi:
                    p.v(lambda e, bit=bit: e.scalar_tensor_tensor(
                        ef[:, :nw], cmp[:, :nw], float(bit), ef[:, :nw],
                        OP.mult, OP.add), inc="s_v")
                p.wait("vector", "s_v")
                for _r in range(rep):
                    p.v(lambda e: e.tensor_tensor(m_[:, :nw], m_[:, :nw],
                                                  f_[:, :nw], OP.mult))
                    p.v(lambda e: e.tensor_tensor(inv[:, :nw], inv[:, :nw],
                                                  f_[:, :nw], OP.mult),
                        inc="s_v")
                    p.wait("vector", "s_v")
            p.v(lambda e: e.reciprocal(G[:, :nw], inv[:, :nw]), inc="s_v")
            p.wait("vector", "s_v")
            if not multi:
                def sop(fn):
                    p.v(fn, inc="s_v")
                    p.wait("vector", "s_v")
                sop(lambda e: e.tensor_scalar(cmp[:, :nw], ef[:, :nw],
                                              -16.5, None, OP.is_ge))
                sop(lambda e: e.tensor_scalar(m_[:, :nw], ef[:, :nw],
                                              -11.5, None, OP.is_le))
                sop(lambda e: e.tensor_tensor(cmp[:, :nw], cmp[:, :nw],
                                              m_[:, :nw], OP.mult))
                sop(lambda e: e.tensor_scalar(m_[:, :nw], ef[:, :nw],
                                              -8.5, None, OP.is_ge))
                sop(lambda e: e.tensor_scalar(amax[:, :nw], ef[:, :nw],
                                              -5.5, None, OP.is_le))
                sop(lambda e: e.tensor_tensor(m_[:, :nw], m_[:, :nw],
                                              amax[:, :nw], OP.mult))
                sop(lambda e: e.tensor_tensor(cmp[:, :nw], cmp[:, :nw],
                                              m_[:, :nw], OP.add))
                sop(lambda e: e.tensor_scalar(m_[:, :nw], ef[:, :nw],
                                              -3.0, None, OP.is_equal))
                sop(lambda e: e.tensor_tensor(cmp[:, :nw], cmp[:, :nw],
                                              m_[:, :nw], OP.add))
                sop(lambda e: e.tensor_scalar(m_[:, :nw], ef[:, :nw],
                                              -11.0, None, OP.is_equal))
                sop(lambda e: e.tensor_scalar(f_[:, :nw], cmp[:, :nw],
                                              -1.0, 1.0, OP.mult, OP.add))
                sop(lambda e: e.tensor_tensor(f_[:, :nw], f_[:, :nw],
                                              m_[:, :nw], OP.subtract))
            cX, nX, wA, w2, rA = (qs[k] for k in
                                  ("cx", "nx", "wa", "w2", "ra"))
            for s in range(n_sec):
                nb0 = s * 32
                src3 = src_fn(s).rearrange("p (b k) -> p b k", k=32)
                inv_b = inv[:, nb0:nb0 + 32].unsqueeze(2).broadcast_to(
                    [128, 32, 32])
                G_b = G[:, nb0:nb0 + 32].unsqueeze(2).broadcast_to(
                    [128, 32, 32])
                u3 = u[:, :].rearrange("p (b k) -> p b k", k=32)
                if not multi:
                    dirZ_b = f_[:, nb0:nb0 + 32].unsqueeze(2).broadcast_to(
                        [128, 32, 32])
                    dirP_b = m_[:, nb0:nb0 + 32].unsqueeze(2).broadcast_to(
                        [128, 32, 32])
                    def wchain(e, s=s, src3=src3, inv_b=inv_b, G_b=G_b,
                               dirZ_b=dirZ_b, dirP_b=dirP_b):
                        e.tensor_tensor(u3, src3, inv_b, OP.mult)
                        e.tensor_scalar(cbuf[:, :], u[:, :], 16.5, None,
                                        OP.add)
                        e.scalar_tensor_tensor(d[:, :], cbuf[:, :], -1.0,
                                               u[:, :], OP.mult, OP.add)
                        e.tensor_scalar(m1[:, :], d[:, :], -15.5, None,
                                        OP.is_ge)
                        e.tensor_scalar(m2[:, :], d[:, :], -16.5, None,
                                        OP.is_lt)
                        e.scalar_tensor_tensor(m1[:, :], m2[:, :], -1.0,
                                               m1[:, :], OP.mult, OP.add)
                        e.tensor_scalar(m2[:, :], d[:, :], -16.5, None,
                                        OP.is_equal)
                        e.tensor_scalar(wA[:, :], u[:, :], 0.0, None,
                                        OP.is_ge)
                        e.tensor_tensor(d[:, :], cbuf[:, :], m1[:, :],
                                        OP.add)
                        e.tensor_scalar(nX[:, :], d[:, :], 0.5, None,
                                        OP.mult)
                        e.tensor_scalar(cX[:, :], nX[:, :], 16.25, None,
                                        OP.add)
                        e.scalar_tensor_tensor(nX[:, :], cX[:, :], -1.0,
                                               nX[:, :], OP.mult, OP.add)
                        e.tensor_scalar(w2[:, :], nX[:, :], -16.0, None,
                                        OP.is_equal)
                        e.tensor_scalar(rA[:, :], w2[:, :], -1.0, 1.0,
                                        OP.mult, OP.add)
                        e.tensor_tensor(w2[:, :], wA[:, :], rA[:, :], OP.mult)
                        e.scalar_tensor_tensor(nX[:, :], w2[:, :], -2.0,
                                               wA[:, :], OP.mult, OP.add)
                        e.tensor_tensor(nX[:, :], nX[:, :], rA[:, :],
                                        OP.add)
                        nx3 = nX[:, :].rearrange("p (b k) -> p b k", k=32)
                        e.tensor_tensor(nx3, nx3, dirZ_b, OP.mult)
                        e.tensor_tensor(nx3, nx3, dirP_b, OP.add)
                        e.tensor_scalar(rA[:, :], wA[:, :], 2.0, -1.0,
                                        OP.mult, OP.add)
                        e.tensor_tensor(nX[:, :], nX[:, :], rA[:, :], OP.mult)
                        e.tensor_tensor(nX[:, :], nX[:, :], wA[:, :],
                                        OP.subtract)
                        e.tensor_tensor(nX[:, :], nX[:, :], m2[:, :],
                                        OP.mult)
                        e.tensor_tensor(d[:, :], d[:, :], nX[:, :], OP.add)
                        d3 = d[:, :].rearrange("p (b k) -> p b k", k=32)
                        return e.scalar_tensor_tensor(
                            dst_fn(s).rearrange("p (b k) -> p b k", k=32), d3,
                            -16.0, G_b, OP.add, OP.mult)
                    p.v(wchain, inc="s_v")
                    p.wait("vector", "s_v")
                else:
                    dstf = dst_fn(s)
                    dst3 = dstf.rearrange("p (b k) -> p b k", k=32)
                    def mchain(e, src3=src3, dstf=dstf, dst3=dst3,
                               inv_b=inv_b, G_b=G_b):
                        e.tensor_tensor(u3, src3, inv_b, OP.mult)
                        e.tensor_scalar(cbuf[:, :], u[:, :], 16.5, None,
                                        OP.add)
                        e.scalar_tensor_tensor(d[:, :], cbuf[:, :], -1.0,
                                               u[:, :], OP.mult, OP.add)
                        e.tensor_scalar(m1[:, :], d[:, :], -15.5, None,
                                        OP.is_ge)
                        e.tensor_scalar(m2[:, :], d[:, :], -16.5, None,
                                        OP.is_lt)
                        e.scalar_tensor_tensor(m1[:, :], m2[:, :], -1.0,
                                               m1[:, :], OP.mult, OP.add)
                        e.tensor_tensor(nX[:, :], cbuf[:, :], m1[:, :],
                                        OP.add)
                        e.tensor_scalar(wA[:, :], d[:, :], -17.5, None,
                                        OP.is_ge)
                        e.tensor_scalar(m2[:, :], d[:, :], -14.5, None,
                                        OP.is_lt)
                        e.tensor_tensor(wA[:, :], wA[:, :], m2[:, :], OP.mult)
                        e.scalar_tensor_tensor(dstf, nX[:, :], -16.0,
                                               wA[:, :], OP.add, OP.mult)
                        e.tensor_scalar(rA[:, :], wA[:, :], -1.0, 1.0,
                                        OP.mult, OP.add)
                        for ci, mulk in enumerate((2.0, 0.5)):
                            e.tensor_scalar(cX[:, :], u[:, :], mulk, 16.5,
                                            OP.mult, OP.add)
                            e.scalar_tensor_tensor(d[:, :], cX[:, :], -1.0,
                                                   u[:, :], OP.mult, OP.add)
                            e.tensor_scalar(m1[:, :], d[:, :], -15.5, None,
                                            OP.is_ge)
                            e.tensor_scalar(m2[:, :], d[:, :], -16.5, None,
                                            OP.is_lt)
                            e.scalar_tensor_tensor(m1[:, :], m2[:, :], -1.0,
                                                   m1[:, :], OP.mult, OP.add)
                            e.tensor_tensor(nX[:, :], cX[:, :], m1[:, :],
                                            OP.add)
                            e.tensor_scalar(w2[:, :], d[:, :], -17.5, None,
                                            OP.is_ge)
                            e.tensor_scalar(m2[:, :], d[:, :], -14.5, None,
                                            OP.is_lt)
                            e.tensor_tensor(w2[:, :], w2[:, :], m2[:, :],
                                            OP.mult)
                            e.tensor_tensor(w2[:, :], rA[:, :], w2[:, :],
                                            OP.mult)
                            e.scalar_tensor_tensor(nX[:, :], nX[:, :], -16.0,
                                                   w2[:, :], OP.add, OP.mult)
                            e.tensor_tensor(dstf, dstf, nX[:, :], OP.add)
                            if ci == 0:
                                e.tensor_tensor(rA[:, :], rA[:, :], w2[:, :],
                                                OP.subtract)
                        return e.tensor_tensor(dst3, dst3, G_b, OP.mult)
                    p.v(mchain, inc="s_v")
                    p.wait("vector", "s_v")

        def emit_ffn0(slo):
            SW = 4
            NSW = NT // SW
            qs_specs = [("q_amax", 128, 128, F32), ("q_m", 128, 128, F32),
                        ("q_inv", 128, 128, F32), ("q_cmp", 128, 128, F32),
                        ("q_f", 128, 128, F32), ("q_G", 128, 128, F32),
                        ("q_u", 128, C, F32), ("q_c", 128, C, I32),
                        ("q_m1", 128, C, BF16), ("q_m2", 128, C, BF16),
                        ("q_nx", 128, C, BF16),
                        ("q_wa", 128, C, BF16), ("q_w2", 128, C, BF16),
                        ("q_ra", 128, C, BF16), ("q_ef", 128, 128, F32)]
            names = ("amax", "m", "inv", "cmp", "f", "G", "u", "c",
                     "m1", "m2", "nx", "wa", "w2", "ra", "ef")
            hs, close_qs = sbuf_group(*qs_specs)
            qs = dict(zip(names, hs))
            qs["d"] = sc_sq
            qs["cx"] = qs["c"]
            (wbuf, wqbuf), close_prep = sbuf_group(
                ("wbuf", 128, 8 * C, F32), ("wqbuf", 128, 8 * C, BF16))
            join_all()
            for widx, (wsrc, whbm, rows, cols) in enumerate(
                    ((l0_wfc, wfcq_hbm, FFN0, C), (l0_wp, wprojq_hbm, C, FFN0))):
                ntile_r = rows // 128
                secs_per_tile = cols // C
                for q4 in range(4):
                    nt_q = ntile_r // 4
                    p.wait("gpsimd", "s_out")
                    def ldw(e, wsrc=wsrc, q4=q4, nt_q=nt_q, cols=cols):
                        return e.dma_start(
                            out=bass.AP(wbuf, 0, [[8 * C, 128], [cols, nt_q],
                                                  [1, cols]]),
                            in_=bass.AP(wsrc.tensor, q4 * nt_q * 128 * cols,
                                        [[cols, 128], [128 * cols, nt_q],
                                         [1, cols]]))
                    p.wait("sync", "s_v")
                    p.wait("sync", "s_g")
                    p.y(ldw, inc="s_ld", n=16)
                    p.wait("vector", "s_ld")
                    p.wait("gpsimd", "s_ld")
                    nsec = nt_q * secs_per_tile
                    for off in range(0, nsec, 4):
                        kk = min(4, nsec - off)
                        emit_quant(
                            lambda s, off=off:
                                wbuf[:, (off + s) * C:(off + s + 1) * C],
                            lambda s, off=off:
                                wqbuf[:, (off + s) * C:(off + s + 1) * C],
                            qs, n_sec=kk)
                    def stw(e, whbm=whbm, q4=q4, nt_q=nt_q, cols=cols):
                        return e.dma_start(
                            out=bass.AP(whbm.tensor, q4 * nt_q * 128 * cols,
                                        [[cols, 128], [128 * cols, nt_q],
                                         [1, cols]]),
                            in_=bass.AP(wqbuf, 0, [[8 * C, 128], [cols, nt_q],
                                                   [1, cols]]))
                    p.wait("gpsimd", "s_v")
                    p.g(stw, inc="s_gp", n=16)
            close_prep()
            (xq_sw, wfcqT, wpjT, g_acc, hbuf, hq, hqT,
             bfcb), close_sw = sbuf_group(
                ("xq_sw", 128, NCH * SW * 128, BF16),
                ("wfcqT", 128, 8 * C, BF16),
                ("wpjT", 128, 4 * C, BF16),
                ("g_acc", 128, SW * C, F32),
                ("hbuf", 128, 4 * C, F32),
                ("hq", 128, 4 * C, BF16),
                ("hqT", 128, 512, BF16),
                ("bfcb", 128, C, BF16))
            bpjb = bfcb
            pfc, ppj, ptq = psum_phase(("pfc", 128, 1024, F32),
                                       ("ppj", 128, 1024, F32),
                                       ("ptq", 128, 4096, BF16))
            join_all()
            emit_bcast(bpjb, l0_bp)
            st_guard = {"v": 0}

            def tq_transposes(stens, scol0, sstride, dtens, dcol0, dstride, n):
                for g0 in range(0, n, 4):
                    k = min(4, n - g0)
                    p.wait("tensor", "s_v", st_guard["v"])
                    p.wait("tensor", "s_a")
                    p.wait("tensor", "s_v")
                    def trq(e, g0=g0, k=k, stens=stens, scol0=scol0,
                            sstride=sstride):
                        for j in range(k):
                            e.transpose(ptq[:, j * 1024:j * 1024 + 128],
                                        stens[:, scol0 + (g0 + j) * sstride:
                                              scol0 + (g0 + j) * sstride + 128],
                                        cb["c_ident"][:, :])
                        return e.nop()
                    p.t(trq, inc="s_t")
                    p.wait("scalar", "s_t", p.c("s_t"))
                    def evq(e, g0=g0, k=k, dtens=dtens, dcol0=dcol0,
                            dstride=dstride):
                        last = None
                        for j in range(k):
                            last = e.copy(
                                dtens[:, dcol0 + (g0 + j) * dstride:
                                      dcol0 + (g0 + j) * dstride + 128],
                                ptq[:, j * 1024:j * 1024 + 128])
                        return last
                    p.s(evq, inc="s_v")
                    st_guard["v"] = p.c("s_v")

            for sw in range(NSW):
                tiles = list(range(sw * SW, (sw + 1) * SW))
                p.wait("vector", "s_a")
                def stage_xn(e, tiles=tiles):
                    last = None
                    for tloc, ti in enumerate(tiles):
                        last = e.tensor_scalar(
                            hbuf[:, tloc * C:(tloc + 1) * C],
                            x_tm[:, ti * C:(ti + 1) * C],
                            stat[:, 2 * ti:2 * ti + 1],
                            stat[:, 2 * ti + 1:2 * ti + 2],
                            OP.subtract, OP.mult)
                    return last
                p.v(stage_xn, inc="s_v")
                p.wait("vector", "s_v")
                emit_quant(lambda s: hbuf[:, s * C:(s + 1) * C],
                           lambda s: hq[:, s * C:(s + 1) * C],
                           qs, n_sec=4, multi=True)
                for tloc in range(4):
                    tq_transposes(hq, tloc * C, 128, xq_sw, tloc * 128,
                                  SW * 128, NCH)
                if debug_name == "xq0" and sw == 0:
                    join_all(("gpsimd",))
                    p.dma("gpsimd",
                          lambda: bass.AP(dbg.tensor, 0,
                                          [[4096, 128], [1, 4096]]),
                          lambda: xq_sw[:, :], "s_gp")
                    p.wait("gpsimd", "s_gp", p.c("s_gp"))
                for q4 in range(4):
                    p.wait("sync", "s_t")
                    for c in range(NCH):
                        p.y(lambda e, c=c, q4=q4: e.dma_start_transpose(
                            wfcqT[:, c * C:(c + 1) * C],
                            bass.AP(wfcq_hbm.tensor, q4 * 1024 * C + c * 128,
                                    [[C, 1024], [1, 128]])),
                            inc="s_sc", n=16)
                    p.wait("gpsimd", "s_v")
                    emit_bcast(bfcb, l0_bfc, off=q4 * C)
                    wload = p.c("s_sc")
                    bload = p.c("s_gp")
                    for tloc in range(4):
                        p.wait("tensor", "s_sc", wload)
                        p.wait("tensor", "s_v")
                        for h in range(2):
                            def mfc(e, h=h, tloc=tloc):
                                last = None
                                for c in range(NCH):
                                    last = e.matmul(
                                        pfc[:, h * 512:(h + 1) * 512],
                                        xq_sw[:, c * SW * 128 + tloc * 128:
                                              c * SW * 128 + (tloc + 1) * 128],
                                        wfcqT[:, c * C + h * 512:
                                              c * C + (h + 1) * 512],
                                        start=(c == 0), stop=(c == NCH - 1))
                                return last
                            p.t(mfc, inc="s_t")
                        fc_done = p.c("s_t")
                        p.wait("vector", "s_t", fc_done)
                        p.wait("vector", "s_gp", bload)
                        def evfc(e, tloc=tloc):
                            e.scalar_tensor_tensor(
                                hbuf[:, tloc * C:tloc * C + 512],
                                pfc[:, 0:512], 1.0, bfcb[:, 0:512],
                                OP.mult, OP.add)
                            return e.scalar_tensor_tensor(
                                hbuf[:, tloc * C + 512:(tloc + 1) * C],
                                pfc[:, 512:1024], 1.0, bfcb[:, 512:1024],
                                OP.mult, OP.add)
                        p.v(evfc, inc="s_v")
                    p.wait("scalar", "s_v")
                    def gelus(e):
                        last = None
                        for tloc in range(4):
                            last = e.activation(
                                hbuf[:, tloc * C:(tloc + 1) * C],
                                hbuf[:, tloc * C:(tloc + 1) * C], AF.Gelu)
                        return last
                    p.s(gelus, inc="s_a")
                    p.wait("vector", "s_a")
                    emit_quant(lambda s: hbuf[:, s * C:(s + 1) * C],
                               lambda s: hq[:, s * C:(s + 1) * C],
                               qs, n_sec=4, multi=True)
                    if debug_name == "h159" and sw == 0:
                        join_all(("gpsimd",))
                        p.dma("gpsimd",
                              lambda q4=q4: bass.AP(
                                  dbg.tensor, q4 * 128 * 1024,
                                  [[1024, 128], [1, 1024]]),
                              lambda: hbuf[:, C:2 * C], "s_gp")
                        p.dma("gpsimd",
                              lambda q4=q4: bass.AP(
                                  dbg.tensor, 512 * 1024 + q4 * 128 * 1024,
                                  [[1024, 128], [1, 1024]]),
                              lambda: hq[:, C:2 * C], "s_gp")
                        p.wait("gpsimd", "s_gp", p.c("s_gp"))
                        p.wait("vector", "s_gp", p.c("s_gp"))
                        p.wait("tensor", "s_gp", p.c("s_gp"))
                    for wh in range(2):
                        p.wait("sync", "s_t")
                        for j in range(4):
                            nglob = q4 * 8 + wh * 4 + j
                            p.y(lambda e, j=j, nglob=nglob:
                                e.dma_start_transpose(
                                    wpjT[:, j * C:(j + 1) * C],
                                    bass.AP(wprojq_hbm.tensor, nglob * 128,
                                            [[FFN0, C], [1, 128]])),
                                inc="s_sc", n=16)
                        wpl = p.c("s_sc")
                        for tloc in range(4):
                            tq_transposes(hq, tloc * C + wh * 512, 128,
                                          hqT, 0, 128, 4)
                            p.wait("tensor", "s_sc", wpl)
                            p.wait("tensor", "s_v")
                            for h in range(2):
                                def mpj(e, h=h, wh=wh):
                                    last = None
                                    for j in range(4):
                                        last = e.matmul(
                                            ppj[:, h * 512:(h + 1) * 512],
                                            hqT[:, j * 128:(j + 1) * 128],
                                            wpjT[:, j * C + h * 512:
                                                 j * C + (h + 1) * 512],
                                            start=(j == 0), stop=(j == 3))
                                    return last
                                p.t(mpj, inc="s_t")
                            pj_done = p.c("s_t")
                            p.wait("vector", "s_t", pj_done)
                            def evpj(e, tloc=tloc, first=(q4 == 0 and wh == 0)):
                                last = None
                                for h in range(2):
                                    ga = g_acc[:, tloc * C + h * 512:
                                               tloc * C + (h + 1) * 512]
                                    if first:
                                        last = e.tensor_copy(
                                            ga, ppj[:, h * 512:(h + 1) * 512])
                                    else:
                                        last = e.scalar_tensor_tensor(
                                            ga, ppj[:, h * 512:(h + 1) * 512],
                                            1.0, ga, OP.mult, OP.add)
                                return last
                            p.v(evpj, inc="s_v")
                            p.wait("tensor", "s_v")
                p.wait("gpsimd", "s_v")
                emit_bcast(bpjb, l0_bp)
                for tloc, ti in enumerate(tiles):
                    ga = lambda tloc=tloc: g_acc[:, tloc * C:(tloc + 1) * C]
                    p.wait("vector", "s_gp")
                    p.v(lambda e, tloc=tloc: e.tensor_tensor(
                        g_acc[:, tloc * C:(tloc + 1) * C],
                        g_acc[:, tloc * C:(tloc + 1) * C],
                        bpjb[:, :], OP.add), inc="s_v")
                    p.wait("scalar", "s_v")
                    p.s(lambda e, tloc=tloc: e.activation(
                        g_acc[:, tloc * C:(tloc + 1) * C],
                        g_acc[:, tloc * C:(tloc + 1) * C],
                        AF.Silu), inc="s_a")
                    p.wait("vector", "s_a")
                    hb = hbuf[:, tloc * C:(tloc + 1) * C]
                    if ti < 4:
                        u = qs["u"]
                        p.v(lambda e, ti=ti: e.tensor_tensor(
                            u[:, :], ctx_tm[:, ti * C:(ti + 1) * C],
                            slo[:, ti * C:(ti + 1) * C], OP.add), inc="s_v")
                        p.wait("vector", "s_v")
                        p.v(lambda e, ti=ti, tloc=tloc, u=u, hb=hb:
                            e.tensor_tensor(
                                hb, u[:, :],
                                g_acc[:, tloc * C:(tloc + 1) * C], OP.mult),
                            inc="s_v")
                    else:
                        p.v(lambda e, ti=ti, tloc=tloc, hb=hb: e.tensor_tensor(
                            hb, ctx_tm[:, ti * C:(ti + 1) * C],
                            g_acc[:, tloc * C:(tloc + 1) * C], OP.mult),
                            inc="s_v")
                    p.wait("vector", "s_v")
                    p.v(lambda e, ti=ti, hb=hb: e.tensor_tensor(
                        x_tm[:, ti * C:(ti + 1) * C],
                        x_tm[:, ti * C:(ti + 1) * C], hb, OP.add),
                        inc="s_v")
                    p.wait("vector", "s_v")
            join_all()
            close_sw()
            close_qs()

        emit_ffn0(slo)
        close_slo()
        if debug_name == "x1d":
            join_all(("gpsimd",))
            p.dma("gpsimd", lambda: dbg.rearrange("(p a) m -> p (a m)", p=128),
                  lambda: x_tm[:, :], "s_gp")
            p.wait("gpsimd", "s_gp", p.c("s_gp"))

        def emit_ffn1(xnT):
            HQ = 4
            HCQ = FFN1 // 128 // HQ
            TQ4 = 4
            (wtmp,), close_wt = sbuf_group(("wtmp", 128, 2 * FFN1, BF16))
            join_all()
            for wsrc, wdst, rows, cols in ((l1_wg, wg_bf_hbm, FFN1, C),
                                           (l1_wu, wu_bf_hbm, FFN1, C),
                                           (l1_wd, wd_bf_hbm, C, FFN1)):
                ntr = rows // 128
                per = (2 * FFN1) // cols
                for t0 in range(0, ntr, per):
                    k = min(per, ntr - t0)
                    p.wait("gpsimd", "s_gp")
                    p.g(lambda e, wsrc=wsrc, t0=t0, k=k, cols=cols:
                        e.dma_start(
                            out=bass.AP(wtmp, 0, [[2 * FFN1, 128], [cols, k],
                                                  [1, cols]]),
                            in_=bass.AP(wsrc.tensor, t0 * 128 * cols,
                                        [[cols, 128], [128 * cols, k],
                                         [1, cols]])), inc="s_gp", n=16)
                    p.wait("gpsimd", "s_gp")
                    p.g(lambda e, wdst=wdst, t0=t0, k=k, cols=cols:
                        e.dma_start(
                            out=bass.AP(wdst.tensor, t0 * 128 * cols,
                                        [[cols, 128], [128 * cols, k],
                                         [1, cols]]),
                            in_=bass.AP(wtmp, 0, [[2 * FFN1, 128], [cols, k],
                                                  [1, cols]])), inc="s_gp",
                        n=16)
            close_wt()
            (wgT, wuT, wdTq, hh, hl, hf, g_acc, sg, su), close_f1 = sbuf_group(
                ("wgT", 128, NCH * 768, BF16),
                ("wuT", 128, NCH * 768, BF16),
                ("wdTq", 128, HCQ * C, BF16),
                ("hh", 128, HCQ * 512, BF16),
                ("hl", 128, HCQ * 512, BF16),
                ("hf", 128, 512, F32),
                ("g1_acc", 128, 4 * C, F32),
                ("sg1", 128, 512, F32),
                ("su1", 128, 512, F32))
            pgu, pdn = psum_phase(("pgu", 128, 1024, F32),
                                  ("pdn", 128, 1024, F32))
            join_all()
            for tq4 in range(TQ4):
                tcol0 = tq4 * 512
                for hq in range(HQ):
                    p.wait("sync", "s_t")
                    p.wait("sync", "s_v")
                    for c in range(NCH):
                        p.y(lambda e, c=c, hq=hq: e.dma_start_transpose(
                            wgT[:, c * 768:(c + 1) * 768],
                            bass.AP(wg_bf_hbm.tensor, hq * 768 * C + c * 128,
                                    [[C, 768], [1, 128]])), inc="s_sc", n=16)
                        p.y(lambda e, c=c, hq=hq: e.dma_start_transpose(
                            wuT[:, c * 768:(c + 1) * 768],
                            bass.AP(wu_bf_hbm.tensor, hq * 768 * C + c * 128,
                                    [[C, 768], [1, 128]])), inc="s_sc", n=16)
                    for hcl in range(HCQ):
                        hc = hq * HCQ + hcl
                        p.y(lambda e, hcl=hcl, hc=hc: e.dma_start_transpose(
                            wdTq[:, hcl * C:(hcl + 1) * C],
                            bass.AP(wd_bf_hbm.tensor, hc * 128,
                                    [[FFN1, C], [1, 128]])), inc="s_sc", n=16)
                    wload = p.c("s_sc")
                    p.wait("tensor", "s_sc", wload)
                    gu_guard = [0, 0]
                    gu_done = {}

                    def emit_gu_evac(hcl):
                        buf = pgu if hcl % 2 == 0 else pdn
                        p.wait("vector", "s_t", gu_done[hcl])
                        def evgu(e, buf=buf):
                            e.tensor_scalar(hf[:, :], buf[:, 0:512], C127L,
                                            None, OP.mult)
                            e.scalar_tensor_tensor(
                                sg[:, :], buf[:, 0:512], C127H, hf[:, :],
                                OP.mult, OP.add)
                            e.tensor_scalar(hf[:, :], buf[:, 512:1024], C127L,
                                            None, OP.mult)
                            return e.scalar_tensor_tensor(
                                su[:, :], buf[:, 512:1024], C127H, hf[:, :],
                                OP.mult, OP.add)
                        p.v(evgu, inc="s_v")
                        gu_guard[hcl % 2] = p.c("s_v")
                        p.wait("scalar", "s_v")
                        p.s(lambda e: e.activation(sg[:, :], sg[:, :],
                                                   AF.Silu), inc="s_a")
                        p.wait("vector", "s_a")
                        def evhh(e, hcl=hcl):
                            e.tensor_tensor(hf[:, :], sg[:, :], su[:, :],
                                            OP.mult)
                            e.tensor_copy(hh[:, hcl * 512:(hcl + 1) * 512],
                                          hf[:, :])
                            return e.scalar_tensor_tensor(
                                hl[:, hcl * 512:(hcl + 1) * 512],
                                hh[:, hcl * 512:(hcl + 1) * 512], -1.0,
                                hf[:, :], OP.mult, OP.add)
                        p.v(evhh, inc="s_v")

                    for hcl in range(HCQ):
                        buf = pgu if hcl % 2 == 0 else pdn
                        p.wait("tensor", "s_v", gu_guard[hcl % 2])
                        for which, wT, ps0 in (("g", wgT, 0), ("u", wuT, 512)):
                            def mgu(e, wT=wT, ps0=ps0, hcl=hcl, tcol0=tcol0,
                                    buf=buf):
                                last = None
                                for c in range(NCH):
                                    last = e.matmul(
                                        buf[:, ps0:ps0 + 512],
                                        wT[:, c * 768 + hcl * 128:
                                           c * 768 + (hcl + 1) * 128],
                                        xnT[:, c * T + tcol0:
                                            c * T + tcol0 + 512],
                                        start=(c == 0), stop=(c == NCH - 1))
                                return last
                            p.t(mgu, inc="s_t")
                        gu_done[hcl] = p.c("s_t")
                        if hcl > 0:
                            emit_gu_evac(hcl - 1)
                    emit_gu_evac(HCQ - 1)
                    hh_done = p.c("s_v")
                    p.wait("tensor", "s_v", hh_done)
                    dn_guard = [0, 0]
                    dn_done = {}

                    def emit_dn_evac(tloc, hq=hq):
                        dbuf = pdn if tloc % 2 == 0 else pgu
                        p.wait("vector", "s_t", dn_done[tloc])
                        def evdn(e, tloc=tloc, dbuf=dbuf, first=(hq == 0)):
                            last = None
                            for h in range(2):
                                ga = g_acc[:, tloc * C + h * 512:
                                           tloc * C + (h + 1) * 512]
                                if first:
                                    last = e.tensor_copy(
                                        ga, dbuf[:, h * 512:(h + 1) * 512])
                                else:
                                    last = e.scalar_tensor_tensor(
                                        ga, dbuf[:, h * 512:(h + 1) * 512],
                                        1.0, ga, OP.mult, OP.add)
                            return last
                        p.v(evdn, inc="s_v")
                        dn_guard[tloc % 2] = p.c("s_v")

                    for tloc in range(4):
                        dbuf = pdn if tloc % 2 == 0 else pgu
                        p.wait("tensor", "s_v", dn_guard[tloc % 2])
                        for h in range(2):
                            def mdn(e, tloc=tloc, h=h, dbuf=dbuf):
                                last = None
                                for part in (hh, hl):
                                    for hcl in range(HCQ):
                                        last = e.matmul(
                                            dbuf[:, h * 512:(h + 1) * 512],
                                            part[:, hcl * 512 + tloc * 128:
                                                 hcl * 512 + (tloc + 1) * 128],
                                            wdTq[:, hcl * C + h * 512:
                                                 hcl * C + (h + 1) * 512],
                                            start=(part is hh and hcl == 0),
                                            stop=(part is hl and
                                                  hcl == HCQ - 1))
                                return last
                            p.t(mdn, inc="s_t")
                        dn_done[tloc] = p.c("s_t")
                        if tloc > 0:
                            emit_dn_evac(tloc - 1)
                    emit_dn_evac(3)
                    p.wait("tensor", "s_v")
                if debug_name == "f1d":
                    join_all(("gpsimd",))
                    p.dma("gpsimd",
                          lambda tq4=tq4: bass.AP(
                              dbg.tensor, tq4 * 128 * 4096,
                              [[4096, 128], [1, 4096]]),
                          lambda: g_acc[:, :], "s_gp")
                    p.wait("gpsimd", "s_gp", p.c("s_gp"))
                    p.wait("vector", "s_gp", p.c("s_gp"))
                    p.wait("scalar", "s_gp", p.c("s_gp"))
                for tloc in range(4):
                    ti = tq4 * 4 + tloc
                    ga = g_acc[:, tloc * C:(tloc + 1) * C]
                    def ev127(e, ga=ga):
                        e.tensor_scalar(sc_sq[:, :], ga, C127L, None, OP.mult)
                        return e.scalar_tensor_tensor(
                            ga, ga, C127H, sc_sq[:, :], OP.mult, OP.add)
                    p.v(ev127, inc="s_v")
                    p.wait("scalar", "s_v")
                    p.s(lambda e, ga=ga: e.activation(
                        ga, ga, AF.Silu), inc="s_a")
                    p.wait("vector", "s_a")
                    p.v(lambda e, ti=ti, ga=ga: e.tensor_tensor(
                        sc_sq[:, :], ctx_tm[:, ti * C:(ti + 1) * C],
                        ga, OP.mult), inc="s_v")
                    p.wait("vector", "s_v")
                    p.v(lambda e, ti=ti: e.tensor_tensor(
                        x_tm[:, ti * C:(ti + 1) * C],
                        x_tm[:, ti * C:(ti + 1) * C], sc_sq[:, :],
                        OP.add), inc="s_v")
                    p.wait("vector", "s_v")
            join_all()
            close_f1()

        (xnT1,), close_xnT1 = sbuf_group(("xnT1", 128, NCH * T, BF16))
        emit_ln_xnT(1e-5, xnT1)
        emit_attention(1, xnT1)
        if debug_name == "ctx1":
            join_all(("gpsimd",))
            p.dma("gpsimd", lambda: dbg.rearrange("(p a) m -> p (a m)", p=128),
                  lambda: ctx_tm[:, :], "s_gp")
            p.wait("gpsimd", "s_gp", p.c("s_gp"))
        emit_attn_proj(1, xnT1)
        if debug_name == "s1":
            join_all(("gpsimd",))
            p.dma("gpsimd", lambda: dbg.rearrange("(p a) m -> p (a m)", p=128),
                  lambda: ctx_tm[:, :], "s_gp")
            p.wait("gpsimd", "s_gp", p.c("s_gp"))
        emit_ffn1(xnT1)
        close_xnT1()

        if debug_name == "xnt0":
            dump_dbg(lambda: xnT[:, :])
        if debug_name == "ctx0":
            dump_dbg(lambda: ctx_tm[:, :])

        join_all(("sync",))
        p.dma("sync",
              lambda: out.rearrange("(i p) m -> p i m", p=128),
              lambda: x_tm[:, :].rearrange("p (i m) -> p i m", m=C),
              "s_out")
        out_done = p.c("s_out")
        p.wait("gpsimd", "s_out", out_done)

        with contextlib.ExitStack() as es2:
            sems = {s: es2.enter_context(nc.semaphore(s)) for s in SEMS}
            p.sems = sems
            with nc.Block() as block:
                @block.tensor
                def _(e):
                    for f in p.ops["tensor"]:
                        f(e)

                @block.vector
                def _(e):
                    for f in p.ops["vector"]:
                        f(e)

                @block.scalar
                def _(e):
                    for f in p.ops["scalar"]:
                        f(e)

                @block.gpsimd
                def _(e):
                    for f in p.ops["gpsimd"]:
                        f(e)

                @block.sync
                def _(e):
                    for f in p.ops["sync"]:
                        f(e)
    return nc


def _prep_inputs(inputs):
    d = dict(inputs)
    feed = {}
    feed["L0_attn_w_qkv"] = np.ascontiguousarray(d["L0_attn_w_qkv"], np.float32)
    feed["L0_attn_w_proj"] = np.ascontiguousarray(d["L0_attn_w_proj"], np.float32)
    feed["L0_ffn_w_fc"] = np.ascontiguousarray(d["L0_ffn_w_fc"], np.float32)
    feed["L0_ffn_b_fc"] = np.ascontiguousarray(d["L0_ffn_b_fc"], np.float32)
    feed["L0_ffn_w_proj"] = np.ascontiguousarray(d["L0_ffn_w_proj"], np.float32)
    feed["L0_ffn_b_proj"] = np.ascontiguousarray(d["L0_ffn_b_proj"], np.float32)
    feed["L1_attn_Wq"] = np.ascontiguousarray(d["L1_attn_Wq"], np.float32).reshape(C, C)
    feed["L1_attn_bq"] = np.ascontiguousarray(d["L1_attn_bq"], np.float32).reshape(C)
    feed["L1_attn_Wk"] = np.ascontiguousarray(d["L1_attn_Wk"], np.float32).reshape(C, C)
    feed["L1_attn_bk"] = np.ascontiguousarray(d["L1_attn_bk"], np.float32).reshape(C)
    feed["L1_attn_Wv"] = np.ascontiguousarray(d["L1_attn_Wv"], np.float32).reshape(C, C)
    feed["L1_attn_bv"] = np.ascontiguousarray(d["L1_attn_bv"], np.float32).reshape(C)
    feed["L1_attn_Wo"] = np.ascontiguousarray(d["L1_attn_Wo"], np.float32).reshape(C, C)
    feed["L1_attn_bo"] = np.ascontiguousarray(d["L1_attn_bo"], np.float32)
    feed["L1_attn_gamma"] = np.ascontiguousarray(d["L1_attn_gamma"], np.float32)
    feed["L1_attn_beta"] = np.ascontiguousarray(d["L1_attn_beta"], np.float32)
    feed["L1_ffn_w_gate"] = np.ascontiguousarray(d["L1_ffn_w_gate"], np.int32)
    feed["L1_ffn_w_up"] = np.ascontiguousarray(d["L1_ffn_w_up"], np.int32)
    feed["L1_ffn_w_down"] = np.ascontiguousarray(d["L1_ffn_w_down"], np.int32)
    feed.update(_host_consts())
    xs = np.ascontiguousarray(np.asarray(d["L0_attn_x"], np.float32))
    return feed, xs


def kernel(**inputs):
    feed, xs = _prep_inputs(inputs)
    nc = build()
    in_maps = []
    for b in range(8):
        m = dict(feed)
        m["x"] = xs[b]
        in_maps.append(m)
    res = run_bass_kernel_spmd(nc, in_maps, core_ids=list(range(8)))
    return np.stack([np.asarray(res.results[b]["out"]) for b in range(8)], axis=0)


# version 23
# speedup vs baseline: 1.4030x; 1.0113x over previous
import contextlib
import numpy as np
import ml_dtypes
import concourse.bass as bass
import concourse.mybir as mybir
from concourse.bass_utils import run_bass_kernel_spmd

F32 = mybir.dt.float32
BF16 = mybir.dt.bfloat16
I32 = mybir.dt.int32
OP = mybir.AluOpType
AF = mybir.ActivationFunctionType
AXX = mybir.AxisListType.X
BF = ml_dtypes.bfloat16

T = 2048
C = 1024
NT = T // 128
NCH = C // 128
NHP = 8
D = 64
U0 = 12.0
FFN0 = 4096
FFN1 = 3072
NEG = -1.0e30
C127H = float(np.float32(1.0 / 127.0))
C127L = float(np.float32(1.0 / 127.0 - np.float64(np.float32(1.0 / 127.0))))

SEMS = ("s_ld", "s_gp", "s_sc", "s_t", "s_v", "s_a", "s_g", "s_out")


class Prog:

    def __init__(self):
        self.ops = {k: [] for k in ("tensor", "vector", "scalar", "gpsimd", "sync")}
        self.cnt = {s: 0 for s in SEMS}
        self.sems = None

    def _add(self, eng, fn, inc=None, n=1):
        tgt = None
        if inc is not None:
            self.cnt[inc] += n
            tgt = self.cnt[inc]
        sem = inc

        def emit(e):
            ins = fn(e)
            if sem is not None:
                ins.then_inc(self.sems[sem], n)
        self.ops[eng].append(emit)
        return tgt

    def t(self, fn, inc=None, n=1):
        return self._add("tensor", fn, inc, n)

    def v(self, fn, inc=None, n=1):
        return self._add("vector", fn, inc, n)

    def s(self, fn, inc=None, n=1):
        return self._add("scalar", fn, inc, n)

    def g(self, fn, inc=None, n=1):
        return self._add("gpsimd", fn, inc, n)

    def y(self, fn, inc=None, n=1):
        return self._add("sync", fn, inc, n)

    def dma(self, eng, out_ap_fn, in_ap_fn, sem):
        return self._add(eng, lambda e: e.dma_start(out=out_ap_fn(), in_=in_ap_fn()),
                         sem, 16)

    def wait(self, eng, sem, val=None):
        v = self.cnt[sem] if val is None else val
        self.ops[eng].append(lambda e: e.wait_ge(self.sems[sem], v))

    def c(self, sem):
        return self.cnt[sem]


def _host_consts():
    half = D // 2
    theta = 1.0 / (10000.0 ** (np.arange(half, dtype=np.float64) / half))
    pos = np.arange(T, dtype=np.float64)
    freqs = pos[:, None] * theta[None, :]
    cosv = np.cos(freqs)
    sinv = np.sin(freqs)
    idx = np.arange(D) // 2
    cos_fm = cosv[:, idx].T
    sin_fm = sinv[:, idx].T
    cos_fm = np.concatenate([cos_fm, cos_fm], 0)
    sin_fm = np.concatenate([sin_fm, sin_fm], 0)
    cos_hi = cos_fm.astype(np.float32)
    sin_hi = sin_fm.astype(np.float32)
    cos_lo = (cos_fm - cos_hi).astype(np.float32)
    sin_lo = (sin_fm - sin_hi).astype(np.float32)
    perm = np.zeros((128, 128), dtype=np.float32)
    for m in range(128):
        base = (m // 64) * 64
        j = m % 64
        if j < 32:
            perm[base + j + 32, m] = -1.0
        else:
            perm[base + j - 32, m] = 1.0
    ident = np.eye(128, dtype=np.float32)
    out = {
        "c_cos_hi": cos_hi, "c_sin_hi": sin_hi,
        "c_cos_lo": np.asarray(cos_lo), "c_sin_lo": np.asarray(sin_lo),
        "c_perm": perm.astype(BF), "c_ident": ident.astype(BF),
    }
    r = np.arange(128)[:, None]
    cc = np.arange(256)[None, :]
    for o in range(2):
        m = np.zeros((128, 256), dtype=np.float32)
        m[(o * 128 + r > cc)] = NEG
        out[f"c_mask{o}"] = m.astype(BF)
    return out


def _ln_stats(p, xt, sc_sq, sums, sumsq, stat, eps, i):
    p.wait("scalar", "s_v")
    p.v(lambda e: e.tensor_reduce(sums[:, i:i + 1], xt, AXX, OP.add), inc="s_v")
    p.s(lambda e: e.activation(sc_sq[:, :], xt, AF.Square,
                               accum_out=sumsq[:, i:i + 1]), inc="s_a")
    p.wait("vector", "s_a")
    p.wait("vector", "s_v")
    mean = stat[:, 2 * i:2 * i + 1]
    rstd = stat[:, 2 * i + 1:2 * i + 2]
    p.v(lambda e: e.tensor_scalar(mean, sums[:, i:i + 1], 1.0 / C, None, OP.mult))
    p.v(lambda e: e.tensor_scalar(rstd, sumsq[:, i:i + 1], 1.0 / C, None,
                                  OP.mult), inc="s_v")
    p.wait("vector", "s_v")
    p.v(lambda e: e.tensor_tensor(sums[:, i:i + 1], mean, mean, OP.mult),
        inc="s_v")
    p.wait("vector", "s_v")
    p.v(lambda e: e.tensor_tensor(rstd, rstd, sums[:, i:i + 1], OP.subtract),
        inc="s_v")
    p.wait("vector", "s_v")
    p.v(lambda e: e.tensor_scalar(rstd, rstd, float(eps), None, OP.add),
        inc="s_v")
    p.wait("scalar", "s_v")
    p.s(lambda e: e.activation(rstd, rstd, AF.Ln), inc="s_a")
    p.wait("scalar", "s_a")
    p.s(lambda e: e.activation(rstd, rstd, AF.Exp, scale=-0.5), inc="s_a")
    return mean, rstd


def build(n_layer_stop=None, debug_name=None):
    nc = bass.Bass("TRN2", target_bir_lowering=False,
                   detect_race_conditions=False)

    x_in = nc.dram_tensor("x", [T, C], F32, kind="ExternalInput").ap()
    out = nc.dram_tensor("out", [T, C], F32, kind="ExternalOutput").ap()
    dbg = None
    if debug_name is not None:
        dbg = nc.dram_tensor("dbg", [T, C], F32, kind="ExternalOutput").ap()

    w_qkv = nc.dram_tensor("L0_attn_w_qkv", [C, 3 * C], F32, kind="ExternalInput").ap()
    l0_wproj = nc.dram_tensor("L0_attn_w_proj", [C, C], F32, kind="ExternalInput").ap()
    l0_wfc = nc.dram_tensor("L0_ffn_w_fc", [FFN0, C], F32, kind="ExternalInput").ap()
    l0_bfc = nc.dram_tensor("L0_ffn_b_fc", [FFN0], F32, kind="ExternalInput").ap()
    l0_wp = nc.dram_tensor("L0_ffn_w_proj", [C, FFN0], F32, kind="ExternalInput").ap()
    l0_bp = nc.dram_tensor("L0_ffn_b_proj", [C], F32, kind="ExternalInput").ap()
    l1_wq = nc.dram_tensor("L1_attn_Wq", [C, C], F32, kind="ExternalInput").ap()
    l1_bq = nc.dram_tensor("L1_attn_bq", [C], F32, kind="ExternalInput").ap()
    l1_wk = nc.dram_tensor("L1_attn_Wk", [C, C], F32, kind="ExternalInput").ap()
    l1_bk = nc.dram_tensor("L1_attn_bk", [C], F32, kind="ExternalInput").ap()
    l1_wv = nc.dram_tensor("L1_attn_Wv", [C, C], F32, kind="ExternalInput").ap()
    l1_bv = nc.dram_tensor("L1_attn_bv", [C], F32, kind="ExternalInput").ap()
    l1_wo = nc.dram_tensor("L1_attn_Wo", [C, C], F32, kind="ExternalInput").ap()
    l1_bo = nc.dram_tensor("L1_attn_bo", [C], F32, kind="ExternalInput").ap()
    l1_gamma = nc.dram_tensor("L1_attn_gamma", [C], F32, kind="ExternalInput").ap()
    l1_beta = nc.dram_tensor("L1_attn_beta", [C], F32, kind="ExternalInput").ap()
    l1_wg = nc.dram_tensor("L1_ffn_w_gate", [FFN1, C], I32, kind="ExternalInput").ap()
    l1_wu = nc.dram_tensor("L1_ffn_w_up", [FFN1, C], I32, kind="ExternalInput").ap()
    l1_wd = nc.dram_tensor("L1_ffn_w_down", [C, FFN1], I32, kind="ExternalInput").ap()

    cmaps = _host_consts()
    cdram = {k: nc.dram_tensor(k, list(v.shape),
                               BF16 if v.dtype == BF else F32,
                               kind="ExternalInput").ap()
             for k, v in cmaps.items()}
    wfcq_hbm = nc.dram_tensor("wfcq_scratch", [FFN0, C], BF16).ap()
    wprojq_hbm = nc.dram_tensor("wprojq_scratch", [C, FFN0], BF16).ap()
    wg_bf_hbm = nc.dram_tensor("wg_bf_scratch", [FFN1, C], BF16).ap()
    wu_bf_hbm = nc.dram_tensor("wu_bf_scratch", [FFN1, C], BF16).ap()
    wd_bf_hbm = nc.dram_tensor("wd_bf_scratch", [C, FFN1], BF16).ap()

    p = Prog()
    es = contextlib.ExitStack()
    with es:
        def sb(name, part, cols, dt):
            return es.enter_context(nc.sbuf_tensor(name, [part, cols], dt))

        def psum(name, part, cols, dt=F32):
            return es.enter_context(nc.psum_tensor(name, [part, cols], dt))

        def sbuf_group(*specs):
            hs, cms = [], []
            for (name, part, cols, dt) in specs:
                cm = nc.sbuf_tensor(name, [part, cols], dt)
                hs.append(cm.__enter__())
                cms.append(cm)

            def close():
                for cm in reversed(cms):
                    cm.__exit__(None, None, None)
            return hs, close

        x_tm = sb("x_tm", 128, NT * C, F32)
        ctx_tm = sb("ctx_tm", 128, NT * C, BF16)
        cb = {"c_ident": sb("c_ident_t", 128, 128, BF16)}
        sums = sb("sums", 128, NT, F32)
        sumsq = sb("sumsq", 128, NT, F32)
        stat = sb("stat", 128, 2 * NT, F32)
        sc_sq = sb("sc_sq", 128, C, F32)
        cvec = sb("cvec", 128, 4, F32)

        def psum_phase(*specs):
            hs, ctxs = [], []
            for (name, part, cols, dt) in specs:
                cm = nc.psum_tensor(name, [part, cols], dt)
                hs.append(cm.__enter__())
                ctxs.append(cm)
            for cm in reversed(ctxs):
                cm.__exit__(None, None, None)
            return hs

        def sbuf_phase(*specs):
            hs, ctxs = [], []
            for (name, part, cols, dt) in specs:
                cm = nc.sbuf_tensor(name, [part, cols], dt)
                hs.append(cm.__enter__())
                ctxs.append(cm)
            for cm in reversed(ctxs):
                cm.__exit__(None, None, None)
            return hs

        pt_bf, pt_bf2 = psum_phase(("pt_bf", 128, 4096, BF16),
                                   ("pt_bf2", 128, 4096, BF16))

        p.dma("sync", lambda: cb["c_ident"][:, :], lambda: cdram["c_ident"],
              "s_ld")
        p.dma("sync",
              lambda: x_tm[:, :].rearrange("p (i m) -> p i m", m=C),
              lambda: x_in.rearrange("(i p) m -> p i m", p=128),
              "s_ld")
        ld_done = p.c("s_ld")

        def load_consts(keys):
            for k in keys:
                v = cmaps[k]
                dt = BF16 if v.dtype == BF else F32
                cb[k] = None
            specs = [(k + "_t", cmaps[k].shape[0], cmaps[k].shape[1],
                      BF16 if cmaps[k].dtype == BF else F32) for k in keys]
            hs, close = sbuf_group(*specs)
            for k, h in zip(keys, hs):
                cb[k] = h
                p.dma("sync", lambda h=h: h[:, :], lambda dk=cdram[k]: dk,
                      "s_ld")
            p.wait("vector", "s_ld")
            p.wait("gpsimd", "s_ld")
            p.wait("tensor", "s_ld")
            p.wait("scalar", "s_ld")
            return close

        tp_bank_free = {0: 0, 1: 0}

        def emit_T_transposes(dst, dst_col0, dst_stride, src, src_col0,
                              src_stride, ntile, evac_eng="vector"):
            dcols = dst.shape[1] if hasattr(dst, "shape") else None
            for g0 in range(0, ntile, 4):
                n = min(4, ntile - g0)
                bank = (g0 // 4) % 2
                pt = pt_bf if bank == 0 else pt_bf2
                p.wait("tensor", "s_v", tp_bank_free[bank])

                def tr(e, g0=g0, n=n, pt=pt):
                    for j in range(n):
                        e.transpose(
                            pt[:, j * 1024:j * 1024 + 128],
                            src[:, src_col0 + (g0 + j) * src_stride:
                                src_col0 + (g0 + j) * src_stride + 128],
                            cb["c_ident"][:, :])
                    return e.nop()
                p.t(tr, inc="s_t")
                p.wait(evac_eng, "s_t", p.c("s_t"))

                def ev(e, g0=g0, n=n, pt=pt):
                    last = None
                    for j in range(n):
                        c0 = dst_col0 + (g0 + j) * dst_stride
                        last = (e.tensor_copy if evac_eng == "vector"
                                else e.copy)(dst[:, c0:c0 + 128],
                                             pt[:, j * 1024:j * 1024 + 128])
                    return last
                if evac_eng == "vector":
                    p.v(ev, inc="s_v")
                else:
                    p.s(ev, inc="s_v")
                tp_bank_free[bank] = p.c("s_v")

        def emit_xnT_tile(xnT, i, src_bf_tile):
            emit_T_transposes(xnT, i * 128, T, src_bf_tile, 0, 128, NCH)

        def emit_ln_xnT(eps, xnT):
            join_all()
            (xn_bf,), close_xb = sbuf_group(
                (f"xn_bf{p.c('s_v')}", 128, 2 * C, BF16))
            tp_bank_free[0] = p.c("s_v")
            tp_bank_free[1] = p.c("s_v")
            xb_free = {0: 0, 1: 0}
            for i in range(NT):
                xt = x_tm[:, i * C:(i + 1) * C]
                mean, rstd = _ln_stats(p, xt, sc_sq, sums, sumsq, stat, eps, i)
                p.wait("vector", "s_a", p.c("s_a"))
                slot = i % 2
                p.wait("vector", "s_t", xb_free[slot])
                p.wait("vector", "s_a")
                xb = xn_bf[:, slot * C:(slot + 1) * C]
                p.v(lambda e, xt=xt, xb=xb, m=mean, r=rstd:
                    e.tensor_scalar(xb, xt, m, r, OP.subtract, OP.mult),
                    inc="s_v")
                p.wait("tensor", "s_v", p.c("s_v"))
                import os as _os2
                _st = _os2.environ.get("K_STAGE", "")
                if _st != "ln" and not (_st == "ln1" and i > 0):
                    emit_xnT_tile(xnT, i, xb)
                xb_free[slot] = p.c("s_t")
            close_xb()

        def join_all(engines=("tensor", "vector", "scalar", "gpsimd", "sync")):
            for eng in engines:
                for s in SEMS:
                    if p.cnt[s]:
                        p.wait(eng, s)

        def dump_dbg(src_fn, cast=False):
            join_all(("gpsimd",))
            p.dma("gpsimd",
                  lambda: dbg.rearrange("(p a) m -> p (a m)", p=128),
                  src_fn, "s_gp")
            p.wait("gpsimd", "s_gp", p.c("s_gp"))

        p.v(lambda e: e.memset(cvec[:, 0:1], -U0))
        p.v(lambda e: e.memset(cvec[:, 2:3], 0.125))
        p.v(lambda e: e.memset(cvec[:, 1:2], 1.0), inc="s_v")

        p.wait("vector", "s_ld", ld_done)
        p.wait("scalar", "s_ld", ld_done)
        p.wait("tensor", "s_ld", ld_done)

        def emit_attention(L, xnT):
            causal = (L == 0)
            pstAB, pstAB2, pc0, pc1, pc2, pc3 = psum_phase(
                (f"pstAB{L}", 128, 1024, F32), (f"pstAB2{L}", 128, 1024, F32),
                (f"pc0_{L}", 128, 512, F32), (f"pc1_{L}", 128, 512, F32),
                (f"pc2_{L}", 128, 512, F32), (f"pc3_{L}", 128, 512, F32))
            pmm, = psum_phase((f"pmm{L}", 128, 1024, F32))
            pvt_a, pvt_b = psum_phase((f"pvt_a{L}", 128, 1024, BF16),
                                      (f"pvt_b{L}", 128, 1024, BF16))
            psts = [pstAB, pstAB2]
            pcs = [pc0, pc1, pc2, pc3]
            (wsl, qraw, rot_sb, r1, r2, qf, kf, vaa, vab, pT, rc, bql,
             qlo, klo) = sbuf_phase(
                (f"wsl{L}", 128, 3072, BF16),
                (f"qraw{L}", 128, T, BF16),
                (f"rot{L}", 128, T, BF16),
                (f"r1_{L}", 128, T, F32),
                (f"r2_{L}", 128, T, BF16),
                (f"qf{L}", 128, T, BF16),
                (f"kf{L}", 128, T, BF16),
                (f"vaa{L}", 128, 16 * 65, BF16),
                (f"vab{L}", 128, 16 * 65, BF16),
                (f"pT{L}", 128, 2 * 512, BF16),
                (f"rc{L}", 128, 8, F32),
                (f"bql{L}", 128, 24, F32),
                (f"qlo{L}", 128, 256, BF16),
                (f"klo{L}", 128, 256, BF16))
            join_all()
            p.v(lambda e: e.memset(vaa[:, :], 1.0))
            p.v(lambda e: e.memset(vab[:, :], 1.0), inc="s_v")
            if L == 1:
                for name, dram, col0 in (("q", l1_bq, 0), ("k", l1_bk, 8),
                                         ("v", l1_bv, 16)):
                    p.y(lambda e, c0=col0, d=dram: e.dma_start(
                        out=bass.AP(bql, c0, [[24, 128], [1, 8]]),
                        in_=d.rearrange("(j q) -> q j", q=128),
                        allow_slow_non_contiguous=True), inc="s_ld", n=16)
                p.wait("vector", "s_ld", p.c("s_ld"))
                p.v(lambda e: e.tensor_scalar(bql[:, 0:8], bql[:, 0:8], 0.125,
                                              None, OP.mult), inc="s_v")
                p.wait("scalar", "s_v")

            last_av = {"t": 0}
            evac_done = {"v": 0, "a": 0}
            evac_half = {0: 0, 1: 0}
            wsl_free = 0

            for hp in range(NHP):
                p.wait("gpsimd", "s_t", wsl_free)
                for i3, (dram, off) in enumerate(
                        [(w_qkv if L == 0 else l1_wq, 0),
                         (w_qkv if L == 0 else l1_wk, C if L == 0 else 0),
                         (w_qkv if L == 0 else l1_wv, 2 * C if L == 0 else 0)]):
                    wid = 3 * C if L == 0 else C
                    p.dma("gpsimd",
                          lambda i3=i3: bass.AP(wsl, i3 * 1024,
                                                [[3072, 128], [128, 8], [1, 128]]),
                          lambda dram=dram, off=off, wid=wid, hp=hp:
                          bass.AP(dram.tensor, off + hp * 128,
                                  [[wid, 128], [128 * wid, 8], [1, 128]]),
                          "s_gp")
                wload = p.c("s_gp")

                def qkv_proj(qi, dst, bias_col, scale):
                    p.wait("tensor", "s_gp", wload)
                    p.wait("tensor", "s_v", evac_done["v"])
                    p.wait("tensor", "s_a")
                    p.wait("scalar", "s_v")
                    p.wait("scalar", "s_g")
                    for ch in range(4):
                        half = ch % 2
                        p.wait("tensor", "s_v", evac_half[half])
                        def mm(e, qi=qi, ch=ch, half=half):
                            last = None
                            for c in range(NCH):
                                last = e.matmul(
                                    pmm[:, half * 512:(half + 1) * 512],
                                    wsl[:, qi * 1024 + c * 128:qi * 1024 + (c + 1) * 128],
                                    xnT[:, c * T + ch * 512:c * T + (ch + 1) * 512],
                                    start=(c == 0), stop=(c == NCH - 1))
                            return last
                        p.t(mm, inc="s_t")
                        p.wait("vector", "s_t", p.c("s_t"))
                        def ev(e, ch=ch, half=half, dst=dst, bias_col=bias_col,
                               scale=scale):
                            dsl = dst[:, ch * 512:(ch + 1) * 512]
                            psl = pmm[:, half * 512:(half + 1) * 512]
                            if bias_col is None:
                                return e.tensor_copy(dsl, psl)
                            bap = bql[:, bias_col:bias_col + 1]
                            if scale is not None:
                                return e.tensor_scalar(
                                    dsl, psl, cvec[:, 2:3], bap,
                                    OP.mult, OP.add)
                            return e.tensor_scalar(dsl, psl, bap, None,
                                                   OP.add)
                        p.v(ev, inc="s_v")
                        evac_done["a"] = p.c("s_v")
                        evac_half[half] = p.c("s_v")

                if L == 0:
                    qkv_proj(0, qraw, None, None)
                    def rope(dstf, dlo):
                        for ch in range(4):
                            half = ch % 2
                            p.wait("tensor", "s_v", evac_half[half])
                            def mmr(e, ch=ch, half=half):
                                return e.matmul(
                                    pmm[:, half * 512:(half + 1) * 512],
                                    cb["c_perm"][:, :],
                                    qraw[:, ch * 512:(ch + 1) * 512],
                                    start=True, stop=True)
                            p.t(mmr, inc="s_t")
                            p.wait("vector", "s_t", p.c("s_t"))
                            def evr(e, ch=ch, half=half):
                                return e.tensor_copy(
                                    rot_sb[:, ch * 512:(ch + 1) * 512],
                                    pmm[:, half * 512:(half + 1) * 512])
                            p.v(evr, inc="s_v")
                            evac_done["a"] = p.c("s_v")
                            evac_half[half] = p.c("s_v")
                        p.wait("vector", "s_a")
                        def ropechain(e, dstf=dstf):
                            e.tensor_tensor(r1[:, :], qraw[:, :],
                                            cb["c_cos_hi"][:, :], OP.mult)
                            e.tensor_tensor(r2[:, :], rot_sb[:, :],
                                            cb["c_sin_hi"][:, :], OP.mult)
                            e.tensor_tensor(r1[:, :], r1[:, :], r2[:, :],
                                            OP.add)
                            e.tensor_tensor(r2[:, :], qraw[:, :],
                                            cb["c_cos_lo"][:, :], OP.mult)
                            e.tensor_tensor(r1[:, :], r1[:, :], r2[:, :],
                                            OP.add)
                            e.tensor_tensor(r2[:, :], rot_sb[:, :],
                                            cb["c_sin_lo"][:, :], OP.mult)
                            e.tensor_tensor(r1[:, :], r1[:, :], r2[:, :],
                                            OP.add)
                            return e.tensor_copy(dstf[:, :], r1[:, :])
                        p.v(ropechain, inc="s_v")
                        p.wait("vector", "s_v")
                        p.v(lambda e, dstf=dstf, dlo=dlo: e.scalar_tensor_tensor(
                            dlo[:, :], dstf[:, 0:256], -1.0, r1[:, 0:256],
                            OP.mult, OP.add), inc="s_v")
                        p.wait("vector", "s_v")
                    rope(qf, qlo)
                    qkv_proj(1, qraw, None, None)
                    rope(kf, klo)
                    qkv_proj(2, rot_sb, None, None)
                    vsrc = rot_sb
                else:
                    qkv_proj(0, qf, 0 + hp, 0.125)
                    qkv_proj(1, kf, 8 + hp, None)
                    qkv_proj(2, rot_sb, 16 + hp, None)
                    vsrc = rot_sb
                wsl_free = p.c("s_t")

                p.wait("tensor", "s_v", evac_done["a"])
                for st2 in range(0, 16, 2):
                    for j, pvt in ((0, pvt_a), (1, pvt_b)):
                        sc = st2 + j
                        if sc >= 16:
                            continue
                        p.wait("tensor", "s_v", evac_done["v"])
                        def trv(e, sc=sc, pvt=pvt, vsrc=vsrc):
                            return e.transpose(pvt[:, 0:128],
                                               vsrc[:, sc * 128:(sc + 1) * 128],
                                               cb["c_ident"][:, :])
                        p.t(trv, inc="s_t")
                        p.wait("vector", "s_t", p.c("s_t"))
                        def evv(e, sc=sc, pvt=pvt):
                            e.tensor_copy(vaa[:, sc * 65:sc * 65 + 64],
                                          pvt[:, 0:64])
                            return e.tensor_copy(vab[:, sc * 65:sc * 65 + 64],
                                                 pvt[:, 64:128])
                        p.v(evv, inc="s_v")
                        evac_done["v"] = p.c("s_v")

                p.wait("tensor", "s_v", evac_done["v"])
                for tq in range(8):
                    n_sc = min(16, 2 * tq + 2) if causal else 16
                    exp_guard = [p.c("s_a"), p.c("s_a")]
                    pT_guard = [0, 0]

                    def emit_mav(sc, n_sc=n_sc):
                        dbuf = sc % 2
                        p.wait("tensor", "s_a", exp_guard[dbuf])
                        def mav(e, sc=sc, dbuf=dbuf, first=(sc == 0),
                                last=(sc == n_sc - 1)):
                            last_i = None
                            for h2 in range(2):
                                last_i = e.matmul(
                                    pcs[h2][:, 0:65],
                                    pT[:, dbuf * 512 + h2 * 128:
                                       dbuf * 512 + (h2 + 1) * 128],
                                    vaa[:, sc * 65:(sc + 1) * 65],
                                    start=first, stop=last)
                                last_i = e.matmul(
                                    pcs[2 + h2][:, 0:65],
                                    pT[:, dbuf * 512 + 256 + h2 * 128:
                                       dbuf * 512 + 256 + (h2 + 1) * 128],
                                    vab[:, sc * 65:(sc + 1) * 65],
                                    start=first, stop=last)
                            return last_i
                        p.t(mav, inc="s_t")
                        pT_guard[dbuf] = p.c("s_t")

                    for sc in range(n_sc):
                        dbuf = sc % 2
                        pst = psts[dbuf]
                        p.wait("tensor", "s_a", exp_guard[dbuf])
                        hilo = causal and tq == 0
                        def mst(e, sc=sc, tq=tq, pst=pst, hilo=hilo):
                            terms = [(kf, qf, True, not hilo)]
                            if hilo:
                                terms += [(kf, qlo, False, False),
                                          (klo, qf, False, True)]
                            last = None
                            for (kk, qq, st0, sp0) in terms:
                                kap = (kk[:, sc * 128:(sc + 1) * 128]
                                       if kk is not klo else kk[:, :]
                                       [:, sc * 128:(sc + 1) * 128])
                                qap = (qq[:, tq * 256:(tq + 1) * 256]
                                       if qq is not qlo else qq[:, 0:256])
                                e.matmul(pst[:, 256:512], kap[0:64, :],
                                         qap[0:64, :], start=st0, stop=sp0,
                                         tile_position=(0, 0))
                                last = e.matmul(pst[:, 512:768], kap[64:128, :],
                                                qap[64:128, :], start=st0,
                                                stop=sp0,
                                                tile_position=(64, 0))
                            return last
                        p.t(mst, inc="s_t")
                        mst_done = p.c("s_t")
                        masked = causal and (sc * 128 >= tq * 256)
                        if masked:
                            mk = "c_mask%d" % (sc - 2 * tq)
                            p.wait("vector", "s_t", mst_done)
                            p.v(lambda e, pst=pst, mk=mk: e.tensor_tensor(
                                pst[:, 256:512], pst[:, 256:512], cb[mk][:, :],
                                OP.add), inc="s_v")
                            p.v(lambda e, pst=pst, mk=mk: e.tensor_tensor(
                                pst[:, 512:768], pst[:, 512:768], cb[mk][:, :],
                                OP.add), inc="s_v")
                            p.wait("scalar", "s_v")
                        p.wait("scalar", "s_t", mst_done)
                        p.wait("scalar", "s_t", pT_guard[dbuf])
                        def evex(e, pst=pst, dbuf=dbuf):
                            return e.activation(pT[:, dbuf * 512:(dbuf + 1) * 512],
                                                pst[:, 256:768], AF.Exp,
                                                scale=(0.125 if L == 0 else 1.0),
                                                bias=cvec[:, 0:1])
                        p.s(evex, inc="s_a")
                        exp_guard[dbuf] = p.c("s_a")
                        if sc > 0:
                            emit_mav(sc - 1)
                    emit_mav(n_sc - 1)
                    av_done = p.c("s_t")
                    p.wait("vector", "s_t", av_done)
                    for g in range(4):
                        head = 0 if g < 2 else 1
                        h2 = g % 2
                        tile = 2 * tq + h2
                        col = tile * C + (2 * hp + head) * D
                        p.v(lambda e, g=g: e.reciprocal(rc[:, g:g + 1],
                                                        pcs[g][:, 64:65]),
                            inc="s_v")
                        p.wait("vector", "s_v")
                        p.v(lambda e, g=g, col=col: e.tensor_scalar(
                            ctx_tm[:, col:col + D], pcs[g][:, 0:D],
                            rc[:, g:g + 1], None, OP.mult), inc="s_v")
                    p.wait("tensor", "s_v", p.c("s_v"))
            join_all()

        def emit_bcast(dst, dram_ap, n=C, off=0):
            p.g(lambda e: e.dma_start(
                out=dst[:, :],
                in_=bass.AP(dram_ap.tensor, off, [[0, 128], [1, n]])),
                inc="s_gp", n=16)

        def emit_attn_proj(L, xnT, slo=None):
            ptp, pproj = psum_phase((f"ptp{L}", 128, 4096, BF16),
                                    (f"pproj{L}", 128, 1024, F32))
            if L == 0:
                wpj, ctxT = sbuf_phase((f"wpj{L}", 128, NCH * C, BF16),
                                       (f"ctxT{L}", 128, C, BF16))
                yb = gf = bof = None
            else:
                wpj, ctxT, yb, xnf, bof, gaf, bef = sbuf_phase(
                    (f"wpj{L}", 128, NCH * C, BF16),
                    (f"ctxT{L}", 128, C, BF16),
                    (f"yb{L}", 128, C, F32),
                    (f"xnf{L}", 128, C, F32),
                    (f"bof{L}", 128, C, F32),
                    (f"gaf{L}", 128, C, F32),
                    (f"bef{L}", 128, C, F32))
            join_all()
            wsrc = l0_wproj if L == 0 else l1_wo
            p.dma("gpsimd",
                  lambda: bass.AP(wpj, 0, [[NCH * C, 128], [C, 8], [1, C]]),
                  lambda: bass.AP(wsrc.tensor, 0, [[C, 128], [128 * C, 8], [1, C]]),
                  "s_gp")
            if L == 1:
                emit_bcast(bof, l1_bo)
                emit_bcast(gaf, l1_gamma)
                emit_bcast(bef, l1_beta)
                p.wait("vector", "s_ld", p.c("s_ld"))
            p.wait("tensor", "s_gp", p.c("s_gp"))
            tp_free = 0
            import os as _os6
            _ks = _os6.environ.get("K_STAGE", "")
            _nt = 1 if _ks == "proj1" else (2 if _ks == "proj2" else NT)
            for i in range(_nt):
                srcs = [(ctx_tm, i * C, 128, ctxT, 0, 128, 8)]
                if _ks == "projnotp" and i > 0:
                    srcs = []
                if L == 1:
                    p.wait("vector", "s_a")
                    p.v(lambda e, i=i: e.tensor_scalar(
                        xnf[:, :], x_tm[:, i * C:(i + 1) * C],
                        stat[:, 2 * i:2 * i + 1], stat[:, 2 * i + 1:2 * i + 2],
                        OP.subtract, OP.mult), inc="s_v")
                p.wait("tensor", "s_a")
                p.wait("tensor", "s_v")
                for (stens, sc0, sstr, dtens, dc0, dstr, ntile) in srcs:
                    for g0 in (0, 4):
                        p.wait("tensor", "s_v", tp_free)
                        def trc(e, stens=stens, sc0=sc0, sstr=sstr, g0=g0):
                            for j in range(4):
                                e.transpose(ptp[:, j * 1024:j * 1024 + 128],
                                            stens[:, sc0 + (g0 + j) * sstr:
                                                  sc0 + (g0 + j) * sstr + 128],
                                            cb["c_ident"][:, :])
                            return e.nop()
                        p.t(trc, inc="s_t")
                        p.wait("vector", "s_t", p.c("s_t"))
                        def evc(e, dtens=dtens, dc0=dc0, dstr=dstr, g0=g0):
                            last = None
                            for j in range(4):
                                last = e.tensor_copy(
                                    dtens[:, dc0 + (g0 + j) * dstr:
                                          dc0 + (g0 + j) * dstr + 128],
                                    ptp[:, j * 1024:j * 1024 + 128])
                            return last
                        p.v(evc, inc="s_v")
                        tp_free = p.c("s_v")
                import os as _os4
                if _os4.environ.get("K_STAGE") == "projtp":
                    continue
                p.wait("tensor", "s_v", tp_free)
                p.wait("tensor", "s_a")
                p.wait("tensor", "s_v")
                for h in range(2):
                    def mmp(e, h=h):
                        last = None
                        for c in range(NCH):
                            last = e.matmul(pproj[:, h * 512:(h + 1) * 512],
                                            ctxT[:, c * 128:(c + 1) * 128],
                                            wpj[:, c * C + h * 512:
                                                c * C + (h + 1) * 512],
                                            start=(c == 0), stop=(c == NCH - 1))
                        return last
                    p.t(mmp, inc="s_t")
                mm_done = p.c("s_t")
                import os as _os5
                if _os5.environ.get("K_STAGE") == "projmm":
                    p.wait("scalar", "s_t", mm_done)
                    p.s(lambda e: e.activation(ctxT[:, 0:512], pproj[:, 0:512],
                                               AF.Identity), inc="s_a")
                    continue
                if _os5.environ.get("K_STAGE") == "projev":
                    p.wait("scalar", "s_t", mm_done)
                    p.s(lambda e: e.activation(ctxT[:, :], pproj[:, :],
                                               AF.Identity), inc="s_a")
                    continue
                if L == 0:
                    p.wait("scalar", "s_t", mm_done)
                    for h in range(2):
                        p.s(lambda e, i=i, h=h: e.activation(
                            ctx_tm[:, i * C + h * 512:i * C + (h + 1) * 512],
                            pproj[:, h * 512:(h + 1) * 512],
                            AF.Identity), inc="s_a")
                    if slo is not None and i < 4:
                        p.wait("vector", "s_a")
                        for h in range(2):
                            p.v(lambda e, i=i, h=h: e.scalar_tensor_tensor(
                                slo[:, i * C + h * 512:i * C + (h + 1) * 512],
                                ctx_tm[:, i * C + h * 512:
                                       i * C + (h + 1) * 512], -1.0,
                                pproj[:, h * 512:(h + 1) * 512],
                                OP.mult, OP.add), inc="s_v")
                else:
                    p.wait("vector", "s_t", mm_done)
                    for h in range(2):
                        p.v(lambda e, h=h: e.scalar_tensor_tensor(
                            yb[:, h * 512:(h + 1) * 512],
                            pproj[:, h * 512:(h + 1) * 512], 1.0,
                            bof[:, h * 512:(h + 1) * 512], OP.mult, OP.add),
                            inc="s_v")
                    p.wait("vector", "s_v")
                    p.v(lambda e: e.tensor_tensor(yb[:, :], yb[:, :],
                                                  xnf[:, :], OP.add),
                        inc="s_v")
                    p.wait("vector", "s_v")
                    mean, rstd = _ln_stats(p, yb[:, :], sc_sq, sums, sumsq,
                                           stat, 1e-3, i)
                    p.wait("vector", "s_a", p.c("s_a"))
                    p.v(lambda e, m=mean, r=rstd: e.tensor_scalar(
                        yb[:, :], yb[:, :], m, r, OP.subtract, OP.mult),
                        inc="s_v")
                    p.wait("vector", "s_v")
                    p.v(lambda e: e.tensor_tensor(yb[:, :], yb[:, :],
                                                  gaf[:, :], OP.mult),
                        inc="s_v")
                    p.wait("vector", "s_v")
                    p.v(lambda e, i=i: e.tensor_tensor(
                        ctx_tm[:, i * C:(i + 1) * C], yb[:, :], bef[:, :],
                        OP.add), inc="s_v")
            join_all()

        (xnT0,), close_xnT0 = sbuf_group(("xnT0", 128, NCH * T, BF16))
        close_l0c = load_consts(["c_cos_hi", "c_sin_hi", "c_cos_lo",
                                 "c_sin_lo", "c_perm", "c_mask0", "c_mask1"])
        emit_ln_xnT(1e-5, xnT0)
        emit_attention(0, xnT0)
        if debug_name == "ctxpre":
            join_all(("gpsimd",))
            p.dma("gpsimd", lambda: dbg.rearrange("(p a) m -> p (a m)", p=128),
                  lambda: ctx_tm[:, :], "s_gp")
            p.wait("gpsimd", "s_gp", p.c("s_gp"))
        close_l0c()
        close_xnT0()
        (slo,), close_slo = sbuf_group(("slo", 128, 4 * C, BF16))
        emit_attn_proj(0, None, slo=slo)

        def emit_quant(src_fn, dst_fn, qs, n_sec=1, veng="vector",
                       multi=False):
            amax, m_, inv, cmp, f_, G = (qs[k] for k in
                                         ("amax", "m", "inv", "cmp", "f", "G"))
            u, cbuf, d, m1, m2 = (qs[k] for k in ("u", "c", "d", "m1", "m2"))
            ef = qs.get("ef")
            nw = n_sec * 32
            def amaxes(e):
                last = None
                for s2 in range(n_sec):
                    src3 = src_fn(s2).rearrange("p (b k) -> p b k", k=32)
                    last = e.tensor_reduce(
                        amax[:, s2 * 32:(s2 + 1) * 32], src3, AXX, OP.max,
                        apply_absolute_value=True)
                return last
            p.v(amaxes, inc="s_v")
            p.wait("vector", "s_v")
            p.v(lambda e: e.tensor_scalar(m_[:, :nw], amax[:, :nw],
                                          float(2.0 ** -64), None, OP.max),
                inc="s_v")
            p.wait("vector", "s_v")
            p.v(lambda e: e.tensor_scalar(m_[:, :nw], m_[:, :nw],
                                          float(2.0 ** 64), None, OP.mult))
            p.v(lambda e: e.memset(inv[:, :nw], float(2.0 ** 67)),
                inc="s_v")
            if not multi:
                p.v(lambda e: e.memset(ef[:, :nw], -67.0), inc="s_v")
            p.wait("vector", "s_v")
            for bit in (64, 32, 16, 8, 4, 2, 1):
                k = min(bit, 16)
                rep = bit // k
                p.v(lambda e, bit=bit: e.tensor_scalar(
                    cmp[:, :nw], m_[:, :nw], float(2.0 ** bit), None,
                    OP.is_ge), inc="s_v")
                p.wait("vector", "s_v")
                p.v(lambda e, k=k: e.tensor_scalar(
                    f_[:, :nw], cmp[:, :nw], float(2.0 ** -k) - 1.0, 1.0,
                    OP.mult, OP.add), inc="s_v")
                if not multi:
                    p.v(lambda e, bit=bit: e.scalar_tensor_tensor(
                        ef[:, :nw], cmp[:, :nw], float(bit), ef[:, :nw],
                        OP.mult, OP.add), inc="s_v")
                p.wait("vector", "s_v")
                for _r in range(rep):
                    p.v(lambda e: e.tensor_tensor(m_[:, :nw], m_[:, :nw],
                                                  f_[:, :nw], OP.mult))
                    p.v(lambda e: e.tensor_tensor(inv[:, :nw], inv[:, :nw],
                                                  f_[:, :nw], OP.mult),
                        inc="s_v")
                    p.wait("vector", "s_v")
            p.v(lambda e: e.reciprocal(G[:, :nw], inv[:, :nw]), inc="s_v")
            p.wait("vector", "s_v")
            if not multi:
                def sop(fn):
                    p.v(fn, inc="s_v")
                    p.wait("vector", "s_v")
                sop(lambda e: e.tensor_scalar(cmp[:, :nw], ef[:, :nw],
                                              -16.5, None, OP.is_ge))
                sop(lambda e: e.tensor_scalar(m_[:, :nw], ef[:, :nw],
                                              -11.5, None, OP.is_le))
                sop(lambda e: e.tensor_tensor(cmp[:, :nw], cmp[:, :nw],
                                              m_[:, :nw], OP.mult))
                sop(lambda e: e.tensor_scalar(m_[:, :nw], ef[:, :nw],
                                              -8.5, None, OP.is_ge))
                sop(lambda e: e.tensor_scalar(amax[:, :nw], ef[:, :nw],
                                              -5.5, None, OP.is_le))
                sop(lambda e: e.tensor_tensor(m_[:, :nw], m_[:, :nw],
                                              amax[:, :nw], OP.mult))
                sop(lambda e: e.tensor_tensor(cmp[:, :nw], cmp[:, :nw],
                                              m_[:, :nw], OP.add))
                sop(lambda e: e.tensor_scalar(m_[:, :nw], ef[:, :nw],
                                              -3.0, None, OP.is_equal))
                sop(lambda e: e.tensor_tensor(cmp[:, :nw], cmp[:, :nw],
                                              m_[:, :nw], OP.add))
                sop(lambda e: e.tensor_scalar(m_[:, :nw], ef[:, :nw],
                                              -11.0, None, OP.is_equal))
                sop(lambda e: e.tensor_scalar(f_[:, :nw], cmp[:, :nw],
                                              -1.0, 1.0, OP.mult, OP.add))
                sop(lambda e: e.tensor_tensor(f_[:, :nw], f_[:, :nw],
                                              m_[:, :nw], OP.subtract))
            cX, nX, wA, w2, rA = (qs[k] for k in
                                  ("cx", "nx", "wa", "w2", "ra"))
            for s in range(n_sec):
                nb0 = s * 32
                src3 = src_fn(s).rearrange("p (b k) -> p b k", k=32)
                inv_b = inv[:, nb0:nb0 + 32].unsqueeze(2).broadcast_to(
                    [128, 32, 32])
                G_b = G[:, nb0:nb0 + 32].unsqueeze(2).broadcast_to(
                    [128, 32, 32])
                u3 = u[:, :].rearrange("p (b k) -> p b k", k=32)
                if not multi:
                    dirZ_b = f_[:, nb0:nb0 + 32].unsqueeze(2).broadcast_to(
                        [128, 32, 32])
                    dirP_b = m_[:, nb0:nb0 + 32].unsqueeze(2).broadcast_to(
                        [128, 32, 32])
                    def wchain(e, s=s, src3=src3, inv_b=inv_b, G_b=G_b,
                               dirZ_b=dirZ_b, dirP_b=dirP_b):
                        e.tensor_tensor(u3, src3, inv_b, OP.mult)
                        e.tensor_scalar(cbuf[:, :], u[:, :], 16.5, None,
                                        OP.add)
                        e.scalar_tensor_tensor(d[:, :], cbuf[:, :], -1.0,
                                               u[:, :], OP.mult, OP.add)
                        e.tensor_scalar(m1[:, :], d[:, :], -15.5, None,
                                        OP.is_ge)
                        e.tensor_scalar(m2[:, :], d[:, :], -16.5, None,
                                        OP.is_lt)
                        e.scalar_tensor_tensor(m1[:, :], m2[:, :], -1.0,
                                               m1[:, :], OP.mult, OP.add)
                        e.tensor_scalar(m2[:, :], d[:, :], -16.5, None,
                                        OP.is_equal)
                        e.tensor_scalar(wA[:, :], u[:, :], 0.0, None,
                                        OP.is_ge)
                        e.tensor_tensor(d[:, :], cbuf[:, :], m1[:, :],
                                        OP.add)
                        e.tensor_scalar(nX[:, :], d[:, :], 0.5, None,
                                        OP.mult)
                        e.tensor_scalar(cX[:, :], nX[:, :], 16.25, None,
                                        OP.add)
                        e.scalar_tensor_tensor(nX[:, :], cX[:, :], -1.0,
                                               nX[:, :], OP.mult, OP.add)
                        e.tensor_scalar(w2[:, :], nX[:, :], -16.0, None,
                                        OP.is_equal)
                        e.tensor_scalar(rA[:, :], w2[:, :], -1.0, 1.0,
                                        OP.mult, OP.add)
                        e.tensor_tensor(w2[:, :], wA[:, :], rA[:, :], OP.mult)
                        e.scalar_tensor_tensor(nX[:, :], w2[:, :], -2.0,
                                               wA[:, :], OP.mult, OP.add)
                        e.tensor_tensor(nX[:, :], nX[:, :], rA[:, :],
                                        OP.add)
                        nx3 = nX[:, :].rearrange("p (b k) -> p b k", k=32)
                        e.tensor_tensor(nx3, nx3, dirZ_b, OP.mult)
                        e.tensor_tensor(nx3, nx3, dirP_b, OP.add)
                        e.tensor_scalar(rA[:, :], wA[:, :], 2.0, -1.0,
                                        OP.mult, OP.add)
                        e.tensor_tensor(nX[:, :], nX[:, :], rA[:, :], OP.mult)
                        e.tensor_tensor(nX[:, :], nX[:, :], wA[:, :],
                                        OP.subtract)
                        e.tensor_tensor(nX[:, :], nX[:, :], m2[:, :],
                                        OP.mult)
                        e.tensor_tensor(d[:, :], d[:, :], nX[:, :], OP.add)
                        d3 = d[:, :].rearrange("p (b k) -> p b k", k=32)
                        return e.scalar_tensor_tensor(
                            dst_fn(s).rearrange("p (b k) -> p b k", k=32), d3,
                            -16.0, G_b, OP.add, OP.mult)
                    p.v(wchain, inc="s_v")
                    p.wait("vector", "s_v")
                else:
                    dstf = dst_fn(s)
                    dst3 = dstf.rearrange("p (b k) -> p b k", k=32)
                    def mchain(e, src3=src3, dstf=dstf, dst3=dst3,
                               inv_b=inv_b, G_b=G_b):
                        e.tensor_tensor(u3, src3, inv_b, OP.mult)
                        e.tensor_scalar(cbuf[:, :], u[:, :], 16.5, None,
                                        OP.add)
                        e.scalar_tensor_tensor(d[:, :], cbuf[:, :], -1.0,
                                               u[:, :], OP.mult, OP.add)
                        e.tensor_scalar(m1[:, :], d[:, :], -15.5, None,
                                        OP.is_ge)
                        e.tensor_scalar(m2[:, :], d[:, :], -16.5, None,
                                        OP.is_lt)
                        e.scalar_tensor_tensor(m1[:, :], m2[:, :], -1.0,
                                               m1[:, :], OP.mult, OP.add)
                        e.tensor_tensor(nX[:, :], cbuf[:, :], m1[:, :],
                                        OP.add)
                        e.tensor_scalar(wA[:, :], d[:, :], -17.5, None,
                                        OP.is_ge)
                        e.tensor_scalar(m2[:, :], d[:, :], -14.5, None,
                                        OP.is_lt)
                        e.tensor_tensor(wA[:, :], wA[:, :], m2[:, :], OP.mult)
                        e.scalar_tensor_tensor(dstf, nX[:, :], -16.0,
                                               wA[:, :], OP.add, OP.mult)
                        e.tensor_scalar(rA[:, :], wA[:, :], -1.0, 1.0,
                                        OP.mult, OP.add)
                        for ci, mulk in enumerate((2.0, 0.5)):
                            e.tensor_scalar(cX[:, :], u[:, :], mulk, 16.5,
                                            OP.mult, OP.add)
                            e.scalar_tensor_tensor(d[:, :], cX[:, :], -1.0,
                                                   u[:, :], OP.mult, OP.add)
                            e.tensor_scalar(m1[:, :], d[:, :], -15.5, None,
                                            OP.is_ge)
                            e.tensor_scalar(m2[:, :], d[:, :], -16.5, None,
                                            OP.is_lt)
                            e.scalar_tensor_tensor(m1[:, :], m2[:, :], -1.0,
                                                   m1[:, :], OP.mult, OP.add)
                            e.tensor_tensor(nX[:, :], cX[:, :], m1[:, :],
                                            OP.add)
                            e.tensor_scalar(w2[:, :], d[:, :], -17.5, None,
                                            OP.is_ge)
                            e.tensor_scalar(m2[:, :], d[:, :], -14.5, None,
                                            OP.is_lt)
                            e.tensor_tensor(w2[:, :], w2[:, :], m2[:, :],
                                            OP.mult)
                            e.tensor_tensor(w2[:, :], rA[:, :], w2[:, :],
                                            OP.mult)
                            e.scalar_tensor_tensor(nX[:, :], nX[:, :], -16.0,
                                                   w2[:, :], OP.add, OP.mult)
                            e.tensor_tensor(dstf, dstf, nX[:, :], OP.add)
                            if ci == 0:
                                e.tensor_tensor(rA[:, :], rA[:, :], w2[:, :],
                                                OP.subtract)
                        return e.tensor_tensor(dst3, dst3, G_b, OP.mult)
                    p.v(mchain, inc="s_v")
                    p.wait("vector", "s_v")

        def emit_ffn0(slo):
            SW = 4
            NSW = NT // SW
            qs_specs = [("q_amax", 128, 128, F32), ("q_m", 128, 128, F32),
                        ("q_inv", 128, 128, F32), ("q_cmp", 128, 128, F32),
                        ("q_f", 128, 128, F32), ("q_G", 128, 128, F32),
                        ("q_u", 128, C, F32), ("q_c", 128, C, I32),
                        ("q_m1", 128, C, BF16), ("q_m2", 128, C, BF16),
                        ("q_nx", 128, C, BF16),
                        ("q_wa", 128, C, BF16), ("q_w2", 128, C, BF16),
                        ("q_ra", 128, C, BF16), ("q_ef", 128, 128, F32)]
            names = ("amax", "m", "inv", "cmp", "f", "G", "u", "c",
                     "m1", "m2", "nx", "wa", "w2", "ra", "ef")
            hs, close_qs = sbuf_group(*qs_specs)
            qs = dict(zip(names, hs))
            qs["d"] = sc_sq
            qs["cx"] = qs["c"]
            (wbuf, wqbuf), close_prep = sbuf_group(
                ("wbuf", 128, 8 * C, F32), ("wqbuf", 128, 8 * C, BF16))
            join_all()
            for widx, (wsrc, whbm, rows, cols) in enumerate(
                    ((l0_wfc, wfcq_hbm, FFN0, C), (l0_wp, wprojq_hbm, C, FFN0))):
                ntile_r = rows // 128
                secs_per_tile = cols // C
                for q4 in range(4):
                    nt_q = ntile_r // 4
                    p.wait("gpsimd", "s_out")
                    def ldw(e, wsrc=wsrc, q4=q4, nt_q=nt_q, cols=cols):
                        return e.dma_start(
                            out=bass.AP(wbuf, 0, [[8 * C, 128], [cols, nt_q],
                                                  [1, cols]]),
                            in_=bass.AP(wsrc.tensor, q4 * nt_q * 128 * cols,
                                        [[cols, 128], [128 * cols, nt_q],
                                         [1, cols]]))
                    p.wait("sync", "s_v")
                    p.wait("sync", "s_g")
                    p.y(ldw, inc="s_ld", n=16)
                    p.wait("vector", "s_ld")
                    p.wait("gpsimd", "s_ld")
                    nsec = nt_q * secs_per_tile
                    for off in range(0, nsec, 4):
                        kk = min(4, nsec - off)
                        emit_quant(
                            lambda s, off=off:
                                wbuf[:, (off + s) * C:(off + s + 1) * C],
                            lambda s, off=off:
                                wqbuf[:, (off + s) * C:(off + s + 1) * C],
                            qs, n_sec=kk)
                    def stw(e, whbm=whbm, q4=q4, nt_q=nt_q, cols=cols):
                        return e.dma_start(
                            out=bass.AP(whbm.tensor, q4 * nt_q * 128 * cols,
                                        [[cols, 128], [128 * cols, nt_q],
                                         [1, cols]]),
                            in_=bass.AP(wqbuf, 0, [[8 * C, 128], [cols, nt_q],
                                                   [1, cols]]))
                    p.wait("gpsimd", "s_v")
                    p.g(stw, inc="s_gp", n=16)
            close_prep()
            (xq_sw, wfcqT, wpjT, g_acc, hbuf, hq, hqT,
             bfcb), close_sw = sbuf_group(
                ("xq_sw", 128, NCH * SW * 128, BF16),
                ("wfcqT", 128, 8 * C, BF16),
                ("wpjT", 128, 4 * C, BF16),
                ("g_acc", 128, SW * C, F32),
                ("hbuf", 128, 4 * C, F32),
                ("hq", 128, 4 * C, BF16),
                ("hqT", 128, 512, BF16),
                ("bfcb", 128, C, BF16))
            bpjb = bfcb
            pfc, ppj, ptq = psum_phase(("pfc", 128, 1024, F32),
                                       ("ppj", 128, 1024, F32),
                                       ("ptq", 128, 4096, BF16))
            join_all()
            emit_bcast(bpjb, l0_bp)
            st_guard = {"v": 0}

            def tq_transposes(stens, scol0, sstride, dtens, dcol0, dstride, n):
                for g0 in range(0, n, 4):
                    k = min(4, n - g0)
                    p.wait("tensor", "s_v", st_guard["v"])
                    p.wait("tensor", "s_a")
                    p.wait("tensor", "s_v")
                    def trq(e, g0=g0, k=k, stens=stens, scol0=scol0,
                            sstride=sstride):
                        for j in range(k):
                            e.transpose(ptq[:, j * 1024:j * 1024 + 128],
                                        stens[:, scol0 + (g0 + j) * sstride:
                                              scol0 + (g0 + j) * sstride + 128],
                                        cb["c_ident"][:, :])
                        return e.nop()
                    p.t(trq, inc="s_t")
                    p.wait("scalar", "s_t", p.c("s_t"))
                    def evq(e, g0=g0, k=k, dtens=dtens, dcol0=dcol0,
                            dstride=dstride):
                        last = None
                        for j in range(k):
                            last = e.copy(
                                dtens[:, dcol0 + (g0 + j) * dstride:
                                      dcol0 + (g0 + j) * dstride + 128],
                                ptq[:, j * 1024:j * 1024 + 128])
                        return last
                    p.s(evq, inc="s_v")
                    st_guard["v"] = p.c("s_v")

            for sw in range(NSW):
                tiles = list(range(sw * SW, (sw + 1) * SW))
                p.wait("vector", "s_a")
                def stage_xn(e, tiles=tiles):
                    last = None
                    for tloc, ti in enumerate(tiles):
                        last = e.tensor_scalar(
                            hbuf[:, tloc * C:(tloc + 1) * C],
                            x_tm[:, ti * C:(ti + 1) * C],
                            stat[:, 2 * ti:2 * ti + 1],
                            stat[:, 2 * ti + 1:2 * ti + 2],
                            OP.subtract, OP.mult)
                    return last
                p.v(stage_xn, inc="s_v")
                p.wait("vector", "s_v")
                emit_quant(lambda s: hbuf[:, s * C:(s + 1) * C],
                           lambda s: hq[:, s * C:(s + 1) * C],
                           qs, n_sec=4, multi=True)
                for tloc in range(4):
                    tq_transposes(hq, tloc * C, 128, xq_sw, tloc * 128,
                                  SW * 128, NCH)
                if debug_name == "xq0" and sw == 0:
                    join_all(("gpsimd",))
                    p.dma("gpsimd",
                          lambda: bass.AP(dbg.tensor, 0,
                                          [[4096, 128], [1, 4096]]),
                          lambda: xq_sw[:, :], "s_gp")
                    p.wait("gpsimd", "s_gp", p.c("s_gp"))
                for q4 in range(4):
                    p.wait("sync", "s_t")
                    for c in range(NCH):
                        p.y(lambda e, c=c, q4=q4: e.dma_start_transpose(
                            wfcqT[:, c * C:(c + 1) * C],
                            bass.AP(wfcq_hbm.tensor, q4 * 1024 * C + c * 128,
                                    [[C, 1024], [1, 128]])),
                            inc="s_sc", n=16)
                    p.wait("gpsimd", "s_v")
                    emit_bcast(bfcb, l0_bfc, off=q4 * C)
                    wload = p.c("s_sc")
                    bload = p.c("s_gp")
                    for tloc in range(4):
                        p.wait("tensor", "s_sc", wload)
                        p.wait("tensor", "s_v")
                        for h in range(2):
                            def mfc(e, h=h, tloc=tloc):
                                last = None
                                for c in range(NCH):
                                    last = e.matmul(
                                        pfc[:, h * 512:(h + 1) * 512],
                                        xq_sw[:, c * SW * 128 + tloc * 128:
                                              c * SW * 128 + (tloc + 1) * 128],
                                        wfcqT[:, c * C + h * 512:
                                              c * C + (h + 1) * 512],
                                        start=(c == 0), stop=(c == NCH - 1))
                                return last
                            p.t(mfc, inc="s_t")
                        fc_done = p.c("s_t")
                        p.wait("vector", "s_t", fc_done)
                        p.wait("vector", "s_gp", bload)
                        def evfc(e, tloc=tloc):
                            e.scalar_tensor_tensor(
                                hbuf[:, tloc * C:tloc * C + 512],
                                pfc[:, 0:512], 1.0, bfcb[:, 0:512],
                                OP.mult, OP.add)
                            return e.scalar_tensor_tensor(
                                hbuf[:, tloc * C + 512:(tloc + 1) * C],
                                pfc[:, 512:1024], 1.0, bfcb[:, 512:1024],
                                OP.mult, OP.add)
                        p.v(evfc, inc="s_v")
                    p.wait("scalar", "s_v")
                    def gelus(e):
                        last = None
                        for tloc in range(4):
                            last = e.activation(
                                hbuf[:, tloc * C:(tloc + 1) * C],
                                hbuf[:, tloc * C:(tloc + 1) * C], AF.Gelu)
                        return last
                    p.s(gelus, inc="s_a")
                    p.wait("vector", "s_a")
                    emit_quant(lambda s: hbuf[:, s * C:(s + 1) * C],
                               lambda s: hq[:, s * C:(s + 1) * C],
                               qs, n_sec=4, multi=True)
                    if debug_name == "h159" and sw == 0:
                        join_all(("gpsimd",))
                        p.dma("gpsimd",
                              lambda q4=q4: bass.AP(
                                  dbg.tensor, q4 * 128 * 1024,
                                  [[1024, 128], [1, 1024]]),
                              lambda: hbuf[:, C:2 * C], "s_gp")
                        p.dma("gpsimd",
                              lambda q4=q4: bass.AP(
                                  dbg.tensor, 512 * 1024 + q4 * 128 * 1024,
                                  [[1024, 128], [1, 1024]]),
                              lambda: hq[:, C:2 * C], "s_gp")
                        p.wait("gpsimd", "s_gp", p.c("s_gp"))
                        p.wait("vector", "s_gp", p.c("s_gp"))
                        p.wait("tensor", "s_gp", p.c("s_gp"))
                    for wh in range(2):
                        p.wait("sync", "s_t")
                        for j in range(4):
                            nglob = q4 * 8 + wh * 4 + j
                            p.y(lambda e, j=j, nglob=nglob:
                                e.dma_start_transpose(
                                    wpjT[:, j * C:(j + 1) * C],
                                    bass.AP(wprojq_hbm.tensor, nglob * 128,
                                            [[FFN0, C], [1, 128]])),
                                inc="s_sc", n=16)
                        wpl = p.c("s_sc")
                        for tloc in range(4):
                            tq_transposes(hq, tloc * C + wh * 512, 128,
                                          hqT, 0, 128, 4)
                            p.wait("tensor", "s_sc", wpl)
                            p.wait("tensor", "s_v")
                            for h in range(2):
                                def mpj(e, h=h, wh=wh):
                                    last = None
                                    for j in range(4):
                                        last = e.matmul(
                                            ppj[:, h * 512:(h + 1) * 512],
                                            hqT[:, j * 128:(j + 1) * 128],
                                            wpjT[:, j * C + h * 512:
                                                 j * C + (h + 1) * 512],
                                            start=(j == 0), stop=(j == 3))
                                    return last
                                p.t(mpj, inc="s_t")
                            pj_done = p.c("s_t")
                            p.wait("vector", "s_t", pj_done)
                            def evpj(e, tloc=tloc, first=(q4 == 0 and wh == 0)):
                                last = None
                                for h in range(2):
                                    ga = g_acc[:, tloc * C + h * 512:
                                               tloc * C + (h + 1) * 512]
                                    if first:
                                        last = e.tensor_copy(
                                            ga, ppj[:, h * 512:(h + 1) * 512])
                                    else:
                                        last = e.scalar_tensor_tensor(
                                            ga, ppj[:, h * 512:(h + 1) * 512],
                                            1.0, ga, OP.mult, OP.add)
                                return last
                            p.v(evpj, inc="s_v")
                            p.wait("tensor", "s_v")
                p.wait("gpsimd", "s_v")
                emit_bcast(bpjb, l0_bp)
                for tloc, ti in enumerate(tiles):
                    ga = lambda tloc=tloc: g_acc[:, tloc * C:(tloc + 1) * C]
                    p.wait("vector", "s_gp")
                    p.v(lambda e, tloc=tloc: e.tensor_tensor(
                        g_acc[:, tloc * C:(tloc + 1) * C],
                        g_acc[:, tloc * C:(tloc + 1) * C],
                        bpjb[:, :], OP.add), inc="s_v")
                    p.wait("scalar", "s_v")
                    p.s(lambda e, tloc=tloc: e.activation(
                        g_acc[:, tloc * C:(tloc + 1) * C],
                        g_acc[:, tloc * C:(tloc + 1) * C],
                        AF.Silu), inc="s_a")
                    p.wait("vector", "s_a")
                    hb = hbuf[:, tloc * C:(tloc + 1) * C]
                    if ti < 4:
                        u = qs["u"]
                        p.v(lambda e, ti=ti: e.tensor_tensor(
                            u[:, :], ctx_tm[:, ti * C:(ti + 1) * C],
                            slo[:, ti * C:(ti + 1) * C], OP.add), inc="s_v")
                        p.wait("vector", "s_v")
                        p.v(lambda e, ti=ti, tloc=tloc, u=u, hb=hb:
                            e.tensor_tensor(
                                hb, u[:, :],
                                g_acc[:, tloc * C:(tloc + 1) * C], OP.mult),
                            inc="s_v")
                    else:
                        p.v(lambda e, ti=ti, tloc=tloc, hb=hb: e.tensor_tensor(
                            hb, ctx_tm[:, ti * C:(ti + 1) * C],
                            g_acc[:, tloc * C:(tloc + 1) * C], OP.mult),
                            inc="s_v")
                    p.wait("vector", "s_v")
                    p.v(lambda e, ti=ti, hb=hb: e.tensor_tensor(
                        x_tm[:, ti * C:(ti + 1) * C],
                        x_tm[:, ti * C:(ti + 1) * C], hb, OP.add),
                        inc="s_v")
                    p.wait("vector", "s_v")
            join_all()
            close_sw()
            close_qs()

        emit_ffn0(slo)
        close_slo()
        if debug_name == "x1d":
            join_all(("gpsimd",))
            p.dma("gpsimd", lambda: dbg.rearrange("(p a) m -> p (a m)", p=128),
                  lambda: x_tm[:, :], "s_gp")
            p.wait("gpsimd", "s_gp", p.c("s_gp"))

        def emit_ffn1(xnT):
            HQ = 4
            HCQ = FFN1 // 128 // HQ
            TQ4 = 4
            (wtmp,), close_wt = sbuf_group(("wtmp", 128, 2 * FFN1, BF16))
            join_all()
            for wsrc, wdst, rows, cols in ((l1_wg, wg_bf_hbm, FFN1, C),
                                           (l1_wu, wu_bf_hbm, FFN1, C),
                                           (l1_wd, wd_bf_hbm, C, FFN1)):
                ntr = rows // 128
                per = (2 * FFN1) // cols
                for t0 in range(0, ntr, per):
                    k = min(per, ntr - t0)
                    p.wait("gpsimd", "s_gp")
                    p.g(lambda e, wsrc=wsrc, t0=t0, k=k, cols=cols:
                        e.dma_start(
                            out=bass.AP(wtmp, 0, [[2 * FFN1, 128], [cols, k],
                                                  [1, cols]]),
                            in_=bass.AP(wsrc.tensor, t0 * 128 * cols,
                                        [[cols, 128], [128 * cols, k],
                                         [1, cols]])), inc="s_gp", n=16)
                    p.wait("gpsimd", "s_gp")
                    p.g(lambda e, wdst=wdst, t0=t0, k=k, cols=cols:
                        e.dma_start(
                            out=bass.AP(wdst.tensor, t0 * 128 * cols,
                                        [[cols, 128], [128 * cols, k],
                                         [1, cols]]),
                            in_=bass.AP(wtmp, 0, [[2 * FFN1, 128], [cols, k],
                                                  [1, cols]])), inc="s_gp",
                        n=16)
            close_wt()
            (wgT, wuT, wdTq, hh, hl, hf, g_acc, sg, su), close_f1 = sbuf_group(
                ("wgT", 128, NCH * 768, BF16),
                ("wuT", 128, NCH * 768, BF16),
                ("wdTq", 128, HCQ * C, BF16),
                ("hh", 128, HCQ * 512, BF16),
                ("hl", 128, HCQ * 512, BF16),
                ("hf", 128, 512, F32),
                ("g1_acc", 128, 4 * C, F32),
                ("sg1", 128, 512, F32),
                ("su1", 128, 512, F32))
            pgu, pdn = psum_phase(("pgu", 128, 1024, F32),
                                  ("pdn", 128, 1024, F32))
            join_all()
            for tq4 in range(TQ4):
                tcol0 = tq4 * 512
                for hq in range(HQ):
                    p.wait("sync", "s_t")
                    p.wait("sync", "s_v")
                    for c in range(NCH):
                        p.y(lambda e, c=c, hq=hq: e.dma_start_transpose(
                            wgT[:, c * 768:(c + 1) * 768],
                            bass.AP(wg_bf_hbm.tensor, hq * 768 * C + c * 128,
                                    [[C, 768], [1, 128]])), inc="s_sc", n=16)
                        p.y(lambda e, c=c, hq=hq: e.dma_start_transpose(
                            wuT[:, c * 768:(c + 1) * 768],
                            bass.AP(wu_bf_hbm.tensor, hq * 768 * C + c * 128,
                                    [[C, 768], [1, 128]])), inc="s_sc", n=16)
                    for hcl in range(HCQ):
                        hc = hq * HCQ + hcl
                        p.y(lambda e, hcl=hcl, hc=hc: e.dma_start_transpose(
                            wdTq[:, hcl * C:(hcl + 1) * C],
                            bass.AP(wd_bf_hbm.tensor, hc * 128,
                                    [[FFN1, C], [1, 128]])), inc="s_sc", n=16)
                    wload = p.c("s_sc")
                    p.wait("tensor", "s_sc", wload)
                    gu_guard = [0, 0]
                    gu_done = {}

                    def emit_gu_evac(hcl):
                        buf = pgu if hcl % 2 == 0 else pdn
                        p.wait("vector", "s_t", gu_done[hcl])
                        def evgu(e, buf=buf):
                            e.tensor_scalar(hf[:, :], buf[:, 0:512], C127L,
                                            None, OP.mult)
                            e.scalar_tensor_tensor(
                                sg[:, :], buf[:, 0:512], C127H, hf[:, :],
                                OP.mult, OP.add)
                            e.tensor_scalar(hf[:, :], buf[:, 512:1024], C127L,
                                            None, OP.mult)
                            return e.scalar_tensor_tensor(
                                su[:, :], buf[:, 512:1024], C127H, hf[:, :],
                                OP.mult, OP.add)
                        p.v(evgu, inc="s_v")
                        gu_guard[hcl % 2] = p.c("s_v")
                        p.wait("scalar", "s_v")
                        p.s(lambda e: e.activation(sg[:, :], sg[:, :],
                                                   AF.Silu), inc="s_a")
                        p.wait("vector", "s_a")
                        def evhh(e, hcl=hcl):
                            e.tensor_tensor(hf[:, :], sg[:, :], su[:, :],
                                            OP.mult)
                            e.tensor_copy(hh[:, hcl * 512:(hcl + 1) * 512],
                                          hf[:, :])
                            return e.scalar_tensor_tensor(
                                hl[:, hcl * 512:(hcl + 1) * 512],
                                hh[:, hcl * 512:(hcl + 1) * 512], -1.0,
                                hf[:, :], OP.mult, OP.add)
                        p.v(evhh, inc="s_v")

                    for hcl in range(HCQ):
                        buf = pgu if hcl % 2 == 0 else pdn
                        p.wait("tensor", "s_v", gu_guard[hcl % 2])
                        for which, wT, ps0 in (("g", wgT, 0), ("u", wuT, 512)):
                            def mgu(e, wT=wT, ps0=ps0, hcl=hcl, tcol0=tcol0,
                                    buf=buf):
                                last = None
                                for c in range(NCH):
                                    last = e.matmul(
                                        buf[:, ps0:ps0 + 512],
                                        wT[:, c * 768 + hcl * 128:
                                           c * 768 + (hcl + 1) * 128],
                                        xnT[:, c * T + tcol0:
                                            c * T + tcol0 + 512],
                                        start=(c == 0), stop=(c == NCH - 1))
                                return last
                            p.t(mgu, inc="s_t")
                        gu_done[hcl] = p.c("s_t")
                        if hcl > 0:
                            emit_gu_evac(hcl - 1)
                    emit_gu_evac(HCQ - 1)
                    hh_done = p.c("s_v")
                    p.wait("tensor", "s_v", hh_done)
                    dn_guard = [0, 0]
                    dn_done = {}

                    def emit_dn_evac(tloc, hq=hq):
                        dbuf = pdn if tloc % 2 == 0 else pgu
                        p.wait("vector", "s_t", dn_done[tloc])
                        def evdn(e, tloc=tloc, dbuf=dbuf, first=(hq == 0)):
                            last = None
                            for h in range(2):
                                ga = g_acc[:, tloc * C + h * 512:
                                           tloc * C + (h + 1) * 512]
                                if first:
                                    last = e.tensor_copy(
                                        ga, dbuf[:, h * 512:(h + 1) * 512])
                                else:
                                    last = e.scalar_tensor_tensor(
                                        ga, dbuf[:, h * 512:(h + 1) * 512],
                                        1.0, ga, OP.mult, OP.add)
                            return last
                        p.v(evdn, inc="s_v")
                        dn_guard[tloc % 2] = p.c("s_v")

                    for tloc in range(4):
                        dbuf = pdn if tloc % 2 == 0 else pgu
                        p.wait("tensor", "s_v", dn_guard[tloc % 2])
                        for h in range(2):
                            def mdn(e, tloc=tloc, h=h, dbuf=dbuf):
                                last = None
                                for part in (hh, hl):
                                    for hcl in range(HCQ):
                                        last = e.matmul(
                                            dbuf[:, h * 512:(h + 1) * 512],
                                            part[:, hcl * 512 + tloc * 128:
                                                 hcl * 512 + (tloc + 1) * 128],
                                            wdTq[:, hcl * C + h * 512:
                                                 hcl * C + (h + 1) * 512],
                                            start=(part is hh and hcl == 0),
                                            stop=(part is hl and
                                                  hcl == HCQ - 1))
                                return last
                            p.t(mdn, inc="s_t")
                        dn_done[tloc] = p.c("s_t")
                        if tloc > 0:
                            emit_dn_evac(tloc - 1)
                    emit_dn_evac(3)
                    p.wait("tensor", "s_v")
                if debug_name == "f1d":
                    join_all(("gpsimd",))
                    p.dma("gpsimd",
                          lambda tq4=tq4: bass.AP(
                              dbg.tensor, tq4 * 128 * 4096,
                              [[4096, 128], [1, 4096]]),
                          lambda: g_acc[:, :], "s_gp")
                    p.wait("gpsimd", "s_gp", p.c("s_gp"))
                    p.wait("vector", "s_gp", p.c("s_gp"))
                    p.wait("scalar", "s_gp", p.c("s_gp"))
                for tloc in range(4):
                    ti = tq4 * 4 + tloc
                    ga = g_acc[:, tloc * C:(tloc + 1) * C]
                    def ev127(e, ga=ga):
                        e.tensor_scalar(sc_sq[:, :], ga, C127L, None, OP.mult)
                        return e.scalar_tensor_tensor(
                            ga, ga, C127H, sc_sq[:, :], OP.mult, OP.add)
                    p.v(ev127, inc="s_v")
                    p.wait("scalar", "s_v")
                    p.s(lambda e, ga=ga: e.activation(
                        ga, ga, AF.Silu), inc="s_a")
                    p.wait("vector", "s_a")
                    p.v(lambda e, ti=ti, ga=ga: e.tensor_tensor(
                        sc_sq[:, :], ctx_tm[:, ti * C:(ti + 1) * C],
                        ga, OP.mult), inc="s_v")
                    p.wait("vector", "s_v")
                    p.v(lambda e, ti=ti: e.tensor_tensor(
                        x_tm[:, ti * C:(ti + 1) * C],
                        x_tm[:, ti * C:(ti + 1) * C], sc_sq[:, :],
                        OP.add), inc="s_v")
                    p.wait("vector", "s_v")
            join_all()
            close_f1()

        (xnT1,), close_xnT1 = sbuf_group(("xnT1", 128, NCH * T, BF16))
        emit_ln_xnT(1e-5, xnT1)
        emit_attention(1, xnT1)
        if debug_name == "ctx1":
            join_all(("gpsimd",))
            p.dma("gpsimd", lambda: dbg.rearrange("(p a) m -> p (a m)", p=128),
                  lambda: ctx_tm[:, :], "s_gp")
            p.wait("gpsimd", "s_gp", p.c("s_gp"))
        emit_attn_proj(1, xnT1)
        if debug_name == "s1":
            join_all(("gpsimd",))
            p.dma("gpsimd", lambda: dbg.rearrange("(p a) m -> p (a m)", p=128),
                  lambda: ctx_tm[:, :], "s_gp")
            p.wait("gpsimd", "s_gp", p.c("s_gp"))
        emit_ffn1(xnT1)
        close_xnT1()

        if debug_name == "xnt0":
            dump_dbg(lambda: xnT[:, :])
        if debug_name == "ctx0":
            dump_dbg(lambda: ctx_tm[:, :])

        join_all(("sync",))
        p.dma("sync",
              lambda: out.rearrange("(i p) m -> p i m", p=128),
              lambda: x_tm[:, :].rearrange("p (i m) -> p i m", m=C),
              "s_out")
        out_done = p.c("s_out")
        p.wait("gpsimd", "s_out", out_done)

        with contextlib.ExitStack() as es2:
            sems = {s: es2.enter_context(nc.semaphore(s)) for s in SEMS}
            p.sems = sems
            with nc.Block() as block:
                @block.tensor
                def _(e):
                    for f in p.ops["tensor"]:
                        f(e)

                @block.vector
                def _(e):
                    for f in p.ops["vector"]:
                        f(e)

                @block.scalar
                def _(e):
                    for f in p.ops["scalar"]:
                        f(e)

                @block.gpsimd
                def _(e):
                    for f in p.ops["gpsimd"]:
                        f(e)

                @block.sync
                def _(e):
                    for f in p.ops["sync"]:
                        f(e)
    return nc


def _prep_inputs(inputs):
    d = dict(inputs)
    feed = {}
    feed["L0_attn_w_qkv"] = np.ascontiguousarray(d["L0_attn_w_qkv"], np.float32)
    feed["L0_attn_w_proj"] = np.ascontiguousarray(d["L0_attn_w_proj"], np.float32)
    feed["L0_ffn_w_fc"] = np.ascontiguousarray(d["L0_ffn_w_fc"], np.float32)
    feed["L0_ffn_b_fc"] = np.ascontiguousarray(d["L0_ffn_b_fc"], np.float32)
    feed["L0_ffn_w_proj"] = np.ascontiguousarray(d["L0_ffn_w_proj"], np.float32)
    feed["L0_ffn_b_proj"] = np.ascontiguousarray(d["L0_ffn_b_proj"], np.float32)
    feed["L1_attn_Wq"] = np.ascontiguousarray(d["L1_attn_Wq"], np.float32).reshape(C, C)
    feed["L1_attn_bq"] = np.ascontiguousarray(d["L1_attn_bq"], np.float32).reshape(C)
    feed["L1_attn_Wk"] = np.ascontiguousarray(d["L1_attn_Wk"], np.float32).reshape(C, C)
    feed["L1_attn_bk"] = np.ascontiguousarray(d["L1_attn_bk"], np.float32).reshape(C)
    feed["L1_attn_Wv"] = np.ascontiguousarray(d["L1_attn_Wv"], np.float32).reshape(C, C)
    feed["L1_attn_bv"] = np.ascontiguousarray(d["L1_attn_bv"], np.float32).reshape(C)
    feed["L1_attn_Wo"] = np.ascontiguousarray(d["L1_attn_Wo"], np.float32).reshape(C, C)
    feed["L1_attn_bo"] = np.ascontiguousarray(d["L1_attn_bo"], np.float32)
    feed["L1_attn_gamma"] = np.ascontiguousarray(d["L1_attn_gamma"], np.float32)
    feed["L1_attn_beta"] = np.ascontiguousarray(d["L1_attn_beta"], np.float32)
    feed["L1_ffn_w_gate"] = np.ascontiguousarray(d["L1_ffn_w_gate"], np.int32)
    feed["L1_ffn_w_up"] = np.ascontiguousarray(d["L1_ffn_w_up"], np.int32)
    feed["L1_ffn_w_down"] = np.ascontiguousarray(d["L1_ffn_w_down"], np.int32)
    feed.update(_host_consts())
    xs = np.ascontiguousarray(np.asarray(d["L0_attn_x"], np.float32))
    return feed, xs


def kernel(**inputs):
    feed, xs = _prep_inputs(inputs)
    nc = build()
    in_maps = []
    for b in range(8):
        m = dict(feed)
        m["x"] = xs[b]
        in_maps.append(m)
    res = run_bass_kernel_spmd(nc, in_maps, core_ids=list(range(8)))
    return np.stack([np.asarray(res.results[b]["out"]) for b in range(8)], axis=0)


# version 24 (final)
# speedup vs baseline: 1.4224x; 1.0138x over previous
import contextlib
import numpy as np
import ml_dtypes
import concourse.bass as bass
import concourse.mybir as mybir
from concourse.bass_utils import run_bass_kernel_spmd

F32 = mybir.dt.float32
BF16 = mybir.dt.bfloat16
I32 = mybir.dt.int32
OP = mybir.AluOpType
AF = mybir.ActivationFunctionType
AXX = mybir.AxisListType.X
BF = ml_dtypes.bfloat16

T = 2048
C = 1024
NT = T // 128
NCH = C // 128
NHP = 8
D = 64
U0 = 12.0
FFN0 = 4096
FFN1 = 3072
NEG = -1.0e30
C127H = float(np.float32(1.0 / 127.0))
C127L = float(np.float32(1.0 / 127.0 - np.float64(np.float32(1.0 / 127.0))))

SEMS = ("s_ld", "s_gp", "s_sc", "s_t", "s_v", "s_a", "s_g", "s_out")


class Prog:

    def __init__(self):
        self.ops = {k: [] for k in ("tensor", "vector", "scalar", "gpsimd", "sync")}
        self.cnt = {s: 0 for s in SEMS}
        self.sems = None

    def _add(self, eng, fn, inc=None, n=1):
        tgt = None
        if inc is not None:
            self.cnt[inc] += n
            tgt = self.cnt[inc]
        sem = inc

        def emit(e):
            ins = fn(e)
            if sem is not None:
                ins.then_inc(self.sems[sem], n)
        self.ops[eng].append(emit)
        return tgt

    def t(self, fn, inc=None, n=1):
        return self._add("tensor", fn, inc, n)

    def v(self, fn, inc=None, n=1):
        return self._add("vector", fn, inc, n)

    def s(self, fn, inc=None, n=1):
        return self._add("scalar", fn, inc, n)

    def g(self, fn, inc=None, n=1):
        return self._add("gpsimd", fn, inc, n)

    def y(self, fn, inc=None, n=1):
        return self._add("sync", fn, inc, n)

    def dma(self, eng, out_ap_fn, in_ap_fn, sem):
        return self._add(eng, lambda e: e.dma_start(out=out_ap_fn(), in_=in_ap_fn()),
                         sem, 16)

    def wait(self, eng, sem, val=None):
        v = self.cnt[sem] if val is None else val
        self.ops[eng].append(lambda e: e.wait_ge(self.sems[sem], v))

    def c(self, sem):
        return self.cnt[sem]


def _host_consts():
    half = D // 2
    theta = 1.0 / (10000.0 ** (np.arange(half, dtype=np.float64) / half))
    pos = np.arange(T, dtype=np.float64)
    freqs = pos[:, None] * theta[None, :]
    cosv = np.cos(freqs)
    sinv = np.sin(freqs)
    idx = np.arange(D) // 2
    cos_fm = cosv[:, idx].T
    sin_fm = sinv[:, idx].T
    cos_fm = np.concatenate([cos_fm, cos_fm], 0)
    sin_fm = np.concatenate([sin_fm, sin_fm], 0)
    cos_hi = cos_fm.astype(np.float32)
    sin_hi = sin_fm.astype(np.float32)
    cos_lo = (cos_fm - cos_hi).astype(np.float32)
    sin_lo = (sin_fm - sin_hi).astype(np.float32)
    perm = np.zeros((128, 128), dtype=np.float32)
    for m in range(128):
        base = (m // 64) * 64
        j = m % 64
        if j < 32:
            perm[base + j + 32, m] = -1.0
        else:
            perm[base + j - 32, m] = 1.0
    ident = np.eye(128, dtype=np.float32)
    out = {
        "c_cos_hi": cos_hi, "c_sin_hi": sin_hi,
        "c_cos_lo": np.asarray(cos_lo), "c_sin_lo": np.asarray(sin_lo),
        "c_perm": perm.astype(BF), "c_ident": ident.astype(BF),
    }
    r = np.arange(128)[:, None]
    cc = np.arange(256)[None, :]
    for o in range(2):
        m = np.zeros((128, 256), dtype=np.float32)
        m[(o * 128 + r > cc)] = NEG
        out[f"c_mask{o}"] = m.astype(BF)
    return out


def _ln_stats(p, xt, sc_sq, sums, sumsq, stat, eps, i):
    p.wait("scalar", "s_v")
    p.v(lambda e: e.tensor_reduce(sums[:, i:i + 1], xt, AXX, OP.add), inc="s_v")
    p.s(lambda e: e.activation(sc_sq[:, :], xt, AF.Square,
                               accum_out=sumsq[:, i:i + 1]), inc="s_a")
    p.wait("vector", "s_a")
    p.wait("vector", "s_v")
    mean = stat[:, 2 * i:2 * i + 1]
    rstd = stat[:, 2 * i + 1:2 * i + 2]
    p.v(lambda e: e.tensor_scalar(mean, sums[:, i:i + 1], 1.0 / C, None, OP.mult))
    p.v(lambda e: e.tensor_scalar(rstd, sumsq[:, i:i + 1], 1.0 / C, None,
                                  OP.mult), inc="s_v")
    p.wait("vector", "s_v")
    p.v(lambda e: e.tensor_tensor(sums[:, i:i + 1], mean, mean, OP.mult),
        inc="s_v")
    p.wait("vector", "s_v")
    p.v(lambda e: e.tensor_tensor(rstd, rstd, sums[:, i:i + 1], OP.subtract),
        inc="s_v")
    p.wait("vector", "s_v")
    p.v(lambda e: e.tensor_scalar(rstd, rstd, float(eps), None, OP.add),
        inc="s_v")
    p.wait("scalar", "s_v")
    p.s(lambda e: e.activation(rstd, rstd, AF.Ln), inc="s_a")
    p.wait("scalar", "s_a")
    p.s(lambda e: e.activation(rstd, rstd, AF.Exp, scale=-0.5), inc="s_a")
    return mean, rstd


def build(n_layer_stop=None, debug_name=None):
    nc = bass.Bass("TRN2", target_bir_lowering=False,
                   detect_race_conditions=False)

    x_in = nc.dram_tensor("x", [T, C], F32, kind="ExternalInput").ap()
    out = nc.dram_tensor("out", [T, C], F32, kind="ExternalOutput").ap()
    dbg = None
    if debug_name is not None:
        dbg = nc.dram_tensor("dbg", [T, C], F32, kind="ExternalOutput").ap()

    w_qkv = nc.dram_tensor("L0_attn_w_qkv", [C, 3 * C], F32, kind="ExternalInput").ap()
    l0_wproj = nc.dram_tensor("L0_attn_w_proj", [C, C], F32, kind="ExternalInput").ap()
    l0_wfc = nc.dram_tensor("L0_ffn_w_fc", [FFN0, C], F32, kind="ExternalInput").ap()
    l0_bfc = nc.dram_tensor("L0_ffn_b_fc", [FFN0], F32, kind="ExternalInput").ap()
    l0_wp = nc.dram_tensor("L0_ffn_w_proj", [C, FFN0], F32, kind="ExternalInput").ap()
    l0_bp = nc.dram_tensor("L0_ffn_b_proj", [C], F32, kind="ExternalInput").ap()
    l1_wq = nc.dram_tensor("L1_attn_Wq", [C, C], F32, kind="ExternalInput").ap()
    l1_bq = nc.dram_tensor("L1_attn_bq", [C], F32, kind="ExternalInput").ap()
    l1_wk = nc.dram_tensor("L1_attn_Wk", [C, C], F32, kind="ExternalInput").ap()
    l1_bk = nc.dram_tensor("L1_attn_bk", [C], F32, kind="ExternalInput").ap()
    l1_wv = nc.dram_tensor("L1_attn_Wv", [C, C], F32, kind="ExternalInput").ap()
    l1_bv = nc.dram_tensor("L1_attn_bv", [C], F32, kind="ExternalInput").ap()
    l1_wo = nc.dram_tensor("L1_attn_Wo", [C, C], F32, kind="ExternalInput").ap()
    l1_bo = nc.dram_tensor("L1_attn_bo", [C], F32, kind="ExternalInput").ap()
    l1_gamma = nc.dram_tensor("L1_attn_gamma", [C], F32, kind="ExternalInput").ap()
    l1_beta = nc.dram_tensor("L1_attn_beta", [C], F32, kind="ExternalInput").ap()
    l1_wg = nc.dram_tensor("L1_ffn_w_gate", [FFN1, C], I32, kind="ExternalInput").ap()
    l1_wu = nc.dram_tensor("L1_ffn_w_up", [FFN1, C], I32, kind="ExternalInput").ap()
    l1_wd = nc.dram_tensor("L1_ffn_w_down", [C, FFN1], I32, kind="ExternalInput").ap()

    cmaps = _host_consts()
    cdram = {k: nc.dram_tensor(k, list(v.shape),
                               BF16 if v.dtype == BF else F32,
                               kind="ExternalInput").ap()
             for k, v in cmaps.items()}
    wfcq_hbm = nc.dram_tensor("wfcq_scratch", [FFN0, C], BF16).ap()
    wprojq_hbm = nc.dram_tensor("wprojq_scratch", [C, FFN0], BF16).ap()
    wg_bf_hbm = nc.dram_tensor("wg_bf_scratch", [FFN1, C], BF16).ap()
    wu_bf_hbm = nc.dram_tensor("wu_bf_scratch", [FFN1, C], BF16).ap()
    wd_bf_hbm = nc.dram_tensor("wd_bf_scratch", [C, FFN1], BF16).ap()

    p = Prog()
    es = contextlib.ExitStack()
    with es:
        def sb(name, part, cols, dt):
            return es.enter_context(nc.sbuf_tensor(name, [part, cols], dt))

        def psum(name, part, cols, dt=F32):
            return es.enter_context(nc.psum_tensor(name, [part, cols], dt))

        def sbuf_group(*specs):
            hs, cms = [], []
            for (name, part, cols, dt) in specs:
                cm = nc.sbuf_tensor(name, [part, cols], dt)
                hs.append(cm.__enter__())
                cms.append(cm)

            def close():
                for cm in reversed(cms):
                    cm.__exit__(None, None, None)
            return hs, close

        x_tm = sb("x_tm", 128, NT * C, F32)
        ctx_tm = sb("ctx_tm", 128, NT * C, BF16)
        cb = {"c_ident": sb("c_ident_t", 128, 128, BF16)}
        sums = sb("sums", 128, NT, F32)
        sumsq = sb("sumsq", 128, NT, F32)
        stat = sb("stat", 128, 2 * NT, F32)
        sc_sq = sb("sc_sq", 128, C, F32)
        cvec = sb("cvec", 128, 4, F32)

        def psum_phase(*specs):
            hs, ctxs = [], []
            for (name, part, cols, dt) in specs:
                cm = nc.psum_tensor(name, [part, cols], dt)
                hs.append(cm.__enter__())
                ctxs.append(cm)
            for cm in reversed(ctxs):
                cm.__exit__(None, None, None)
            return hs

        def sbuf_phase(*specs):
            hs, ctxs = [], []
            for (name, part, cols, dt) in specs:
                cm = nc.sbuf_tensor(name, [part, cols], dt)
                hs.append(cm.__enter__())
                ctxs.append(cm)
            for cm in reversed(ctxs):
                cm.__exit__(None, None, None)
            return hs

        pt_bf, pt_bf2 = psum_phase(("pt_bf", 128, 4096, BF16),
                                   ("pt_bf2", 128, 4096, BF16))

        p.dma("sync", lambda: cb["c_ident"][:, :], lambda: cdram["c_ident"],
              "s_ld")
        p.dma("sync",
              lambda: x_tm[:, :].rearrange("p (i m) -> p i m", m=C),
              lambda: x_in.rearrange("(i p) m -> p i m", p=128),
              "s_ld")
        ld_done = p.c("s_ld")

        def load_consts(keys):
            for k in keys:
                v = cmaps[k]
                dt = BF16 if v.dtype == BF else F32
                cb[k] = None
            specs = [(k + "_t", cmaps[k].shape[0], cmaps[k].shape[1],
                      BF16 if cmaps[k].dtype == BF else F32) for k in keys]
            hs, close = sbuf_group(*specs)
            for k, h in zip(keys, hs):
                cb[k] = h
                p.dma("sync", lambda h=h: h[:, :], lambda dk=cdram[k]: dk,
                      "s_ld")
            p.wait("vector", "s_ld")
            p.wait("gpsimd", "s_ld")
            p.wait("tensor", "s_ld")
            p.wait("scalar", "s_ld")
            return close

        tp_bank_free = {0: 0, 1: 0}

        def emit_T_transposes(dst, dst_col0, dst_stride, src, src_col0,
                              src_stride, ntile, evac_eng="vector"):
            dcols = dst.shape[1] if hasattr(dst, "shape") else None
            for g0 in range(0, ntile, 4):
                n = min(4, ntile - g0)
                bank = (g0 // 4) % 2
                pt = pt_bf if bank == 0 else pt_bf2
                p.wait("tensor", "s_v", tp_bank_free[bank])

                def tr(e, g0=g0, n=n, pt=pt):
                    for j in range(n):
                        e.transpose(
                            pt[:, j * 1024:j * 1024 + 128],
                            src[:, src_col0 + (g0 + j) * src_stride:
                                src_col0 + (g0 + j) * src_stride + 128],
                            cb["c_ident"][:, :])
                    return e.nop()
                p.t(tr, inc="s_t")
                p.wait(evac_eng, "s_t", p.c("s_t"))

                def ev(e, g0=g0, n=n, pt=pt):
                    last = None
                    for j in range(n):
                        c0 = dst_col0 + (g0 + j) * dst_stride
                        last = (e.tensor_copy if evac_eng == "vector"
                                else e.copy)(dst[:, c0:c0 + 128],
                                             pt[:, j * 1024:j * 1024 + 128])
                    return last
                if evac_eng == "vector":
                    p.v(ev, inc="s_v")
                else:
                    p.s(ev, inc="s_v")
                tp_bank_free[bank] = p.c("s_v")

        def emit_xnT_tile(xnT, i, src_bf_tile):
            emit_T_transposes(xnT, i * 128, T, src_bf_tile, 0, 128, NCH)

        def emit_ln_xnT(eps, xnT):
            join_all()
            (xn_bf,), close_xb = sbuf_group(
                (f"xn_bf{p.c('s_v')}", 128, 2 * C, BF16))
            tp_bank_free[0] = p.c("s_v")
            tp_bank_free[1] = p.c("s_v")
            xb_free = {0: 0, 1: 0}
            for i in range(NT):
                xt = x_tm[:, i * C:(i + 1) * C]
                mean, rstd = _ln_stats(p, xt, sc_sq, sums, sumsq, stat, eps, i)
                p.wait("vector", "s_a", p.c("s_a"))
                slot = i % 2
                p.wait("vector", "s_t", xb_free[slot])
                p.wait("vector", "s_a")
                xb = xn_bf[:, slot * C:(slot + 1) * C]
                p.v(lambda e, xt=xt, xb=xb, m=mean, r=rstd:
                    e.tensor_scalar(xb, xt, m, r, OP.subtract, OP.mult),
                    inc="s_v")
                p.wait("tensor", "s_v", p.c("s_v"))
                import os as _os2
                _st = _os2.environ.get("K_STAGE", "")
                if _st != "ln" and not (_st == "ln1" and i > 0):
                    emit_xnT_tile(xnT, i, xb)
                xb_free[slot] = p.c("s_t")
            close_xb()

        def join_all(engines=("tensor", "vector", "scalar", "gpsimd", "sync")):
            for eng in engines:
                for s in SEMS:
                    if p.cnt[s]:
                        p.wait(eng, s)

        def dump_dbg(src_fn, cast=False):
            join_all(("gpsimd",))
            p.dma("gpsimd",
                  lambda: dbg.rearrange("(p a) m -> p (a m)", p=128),
                  src_fn, "s_gp")
            p.wait("gpsimd", "s_gp", p.c("s_gp"))

        p.v(lambda e: e.memset(cvec[:, 0:1], -U0))
        p.v(lambda e: e.memset(cvec[:, 2:3], 0.125))
        p.v(lambda e: e.memset(cvec[:, 1:2], 1.0), inc="s_v")

        p.wait("vector", "s_ld", ld_done)
        p.wait("scalar", "s_ld", ld_done)
        p.wait("tensor", "s_ld", ld_done)

        def emit_attention(L, xnT):
            causal = (L == 0)
            pstAB, pstAB2, pc0, pc1, pc2, pc3 = psum_phase(
                (f"pstAB{L}", 128, 1024, F32), (f"pstAB2{L}", 128, 1024, F32),
                (f"pc0_{L}", 128, 512, F32), (f"pc1_{L}", 128, 512, F32),
                (f"pc2_{L}", 128, 512, F32), (f"pc3_{L}", 128, 512, F32))
            pmm, = psum_phase((f"pmm{L}", 128, 1024, F32))
            pvt_a, pvt_b = psum_phase((f"pvt_a{L}", 128, 1024, BF16),
                                      (f"pvt_b{L}", 128, 1024, BF16))
            psts = [pstAB, pstAB2]
            pcs = [pc0, pc1, pc2, pc3]
            (wsl, qraw, rot_sb, r1, r2, qf, kf, vaa, vab, pT, rc, bql,
             qlo, klo) = sbuf_phase(
                (f"wsl{L}", 128, 3072, BF16),
                (f"qraw{L}", 128, T, BF16),
                (f"rot{L}", 128, T, BF16),
                (f"r1_{L}", 128, T, F32),
                (f"r2_{L}", 128, T, BF16),
                (f"qf{L}", 128, T, BF16),
                (f"kf{L}", 128, T, BF16),
                (f"vaa{L}", 128, 16 * 65, BF16),
                (f"vab{L}", 128, 16 * 65, BF16),
                (f"pT{L}", 128, 2 * 512, BF16),
                (f"rc{L}", 128, 8, F32),
                (f"bql{L}", 128, 24, F32),
                (f"qlo{L}", 128, 256, BF16),
                (f"klo{L}", 128, 256, BF16))
            join_all()
            p.v(lambda e: e.memset(vaa[:, :], 1.0))
            p.v(lambda e: e.memset(vab[:, :], 1.0), inc="s_v")
            if L == 1:
                for name, dram, col0 in (("q", l1_bq, 0), ("k", l1_bk, 8),
                                         ("v", l1_bv, 16)):
                    p.y(lambda e, c0=col0, d=dram: e.dma_start(
                        out=bass.AP(bql, c0, [[24, 128], [1, 8]]),
                        in_=d.rearrange("(j q) -> q j", q=128),
                        allow_slow_non_contiguous=True), inc="s_ld", n=16)
                p.wait("vector", "s_ld", p.c("s_ld"))
                p.v(lambda e: e.tensor_scalar(bql[:, 0:8], bql[:, 0:8], 0.125,
                                              None, OP.mult), inc="s_v")
                p.wait("scalar", "s_v")

            last_av = {"t": 0}
            evac_done = {"v": 0, "a": 0}
            evac_half = {0: 0, 1: 0}
            wsl_free = 0

            for hp in range(NHP):
                p.wait("gpsimd", "s_t", wsl_free)
                for i3, (dram, off) in enumerate(
                        [(w_qkv if L == 0 else l1_wq, 0),
                         (w_qkv if L == 0 else l1_wk, C if L == 0 else 0),
                         (w_qkv if L == 0 else l1_wv, 2 * C if L == 0 else 0)]):
                    wid = 3 * C if L == 0 else C
                    p.dma("gpsimd",
                          lambda i3=i3: bass.AP(wsl, i3 * 1024,
                                                [[3072, 128], [128, 8], [1, 128]]),
                          lambda dram=dram, off=off, wid=wid, hp=hp:
                          bass.AP(dram.tensor, off + hp * 128,
                                  [[wid, 128], [128 * wid, 8], [1, 128]]),
                          "s_gp")
                wload = p.c("s_gp")

                def qkv_proj(qi, dst, bias_col, scale):
                    p.wait("tensor", "s_gp", wload)
                    p.wait("tensor", "s_v", evac_done["v"])
                    p.wait("tensor", "s_a")
                    p.wait("scalar", "s_v")
                    p.wait("scalar", "s_g")
                    for ch in range(4):
                        half = ch % 2
                        p.wait("tensor", "s_v", evac_half[half])
                        def mm(e, qi=qi, ch=ch, half=half):
                            last = None
                            for c in range(NCH):
                                last = e.matmul(
                                    pmm[:, half * 512:(half + 1) * 512],
                                    wsl[:, qi * 1024 + c * 128:qi * 1024 + (c + 1) * 128],
                                    xnT[:, c * T + ch * 512:c * T + (ch + 1) * 512],
                                    start=(c == 0), stop=(c == NCH - 1))
                            return last
                        p.t(mm, inc="s_t")
                        p.wait("vector", "s_t", p.c("s_t"))
                        def ev(e, ch=ch, half=half, dst=dst, bias_col=bias_col,
                               scale=scale):
                            dsl = dst[:, ch * 512:(ch + 1) * 512]
                            psl = pmm[:, half * 512:(half + 1) * 512]
                            if bias_col is None:
                                return e.tensor_copy(dsl, psl)
                            bap = bql[:, bias_col:bias_col + 1]
                            if scale is not None:
                                return e.tensor_scalar(
                                    dsl, psl, cvec[:, 2:3], bap,
                                    OP.mult, OP.add)
                            return e.tensor_scalar(dsl, psl, bap, None,
                                                   OP.add)
                        p.v(ev, inc="s_v")
                        evac_done["a"] = p.c("s_v")
                        evac_half[half] = p.c("s_v")

                if L == 0:
                    qkv_proj(0, qraw, None, None)
                    def rope(dstf, dlo):
                        for ch in range(4):
                            half = ch % 2
                            p.wait("tensor", "s_v", evac_half[half])
                            def mmr(e, ch=ch, half=half):
                                return e.matmul(
                                    pmm[:, half * 512:(half + 1) * 512],
                                    cb["c_perm"][:, :],
                                    qraw[:, ch * 512:(ch + 1) * 512],
                                    start=True, stop=True)
                            p.t(mmr, inc="s_t")
                            p.wait("vector", "s_t", p.c("s_t"))
                            def evr(e, ch=ch, half=half):
                                return e.tensor_copy(
                                    rot_sb[:, ch * 512:(ch + 1) * 512],
                                    pmm[:, half * 512:(half + 1) * 512])
                            p.v(evr, inc="s_v")
                            evac_done["a"] = p.c("s_v")
                            evac_half[half] = p.c("s_v")
                        p.wait("vector", "s_a")
                        def ropechain(e, dstf=dstf):
                            e.tensor_tensor(r1[:, :], qraw[:, :],
                                            cb["c_cos_hi"][:, :], OP.mult)
                            e.tensor_tensor(r2[:, :], rot_sb[:, :],
                                            cb["c_sin_hi"][:, :], OP.mult)
                            e.tensor_tensor(r1[:, :], r1[:, :], r2[:, :],
                                            OP.add)
                            e.tensor_tensor(r2[:, :], qraw[:, :],
                                            cb["c_cos_lo"][:, :], OP.mult)
                            e.tensor_tensor(r1[:, :], r1[:, :], r2[:, :],
                                            OP.add)
                            e.tensor_tensor(r2[:, :], rot_sb[:, :],
                                            cb["c_sin_lo"][:, :], OP.mult)
                            e.tensor_tensor(r1[:, :], r1[:, :], r2[:, :],
                                            OP.add)
                            return e.tensor_copy(dstf[:, :], r1[:, :])
                        p.v(ropechain, inc="s_v")
                        p.wait("vector", "s_v")
                        p.v(lambda e, dstf=dstf, dlo=dlo: e.scalar_tensor_tensor(
                            dlo[:, :], dstf[:, 0:256], -1.0, r1[:, 0:256],
                            OP.mult, OP.add), inc="s_v")
                        p.wait("vector", "s_v")
                    rope(qf, qlo)
                    qkv_proj(1, qraw, None, None)
                    rope(kf, klo)
                    qkv_proj(2, rot_sb, None, None)
                    vsrc = rot_sb
                else:
                    qkv_proj(0, qf, 0 + hp, 0.125)
                    qkv_proj(1, kf, 8 + hp, None)
                    qkv_proj(2, rot_sb, 16 + hp, None)
                    vsrc = rot_sb
                wsl_free = p.c("s_t")

                p.wait("tensor", "s_v", evac_done["a"])
                for st2 in range(0, 16, 2):
                    for j, pvt in ((0, pvt_a), (1, pvt_b)):
                        sc = st2 + j
                        if sc >= 16:
                            continue
                        p.wait("tensor", "s_v", evac_done["v"])
                        def trv(e, sc=sc, pvt=pvt, vsrc=vsrc):
                            return e.transpose(pvt[:, 0:128],
                                               vsrc[:, sc * 128:(sc + 1) * 128],
                                               cb["c_ident"][:, :])
                        p.t(trv, inc="s_t")
                        p.wait("vector", "s_t", p.c("s_t"))
                        def evv(e, sc=sc, pvt=pvt):
                            e.tensor_copy(vaa[:, sc * 65:sc * 65 + 64],
                                          pvt[:, 0:64])
                            return e.tensor_copy(vab[:, sc * 65:sc * 65 + 64],
                                                 pvt[:, 64:128])
                        p.v(evv, inc="s_v")
                        evac_done["v"] = p.c("s_v")

                p.wait("tensor", "s_v", evac_done["v"])
                ctx_guard = {"v": 0}
                for tq in range(8):
                    n_sc = min(16, 2 * tq + 2) if causal else 16
                    exp_guard = [p.c("s_a"), p.c("s_a")]
                    pT_guard = [0, 0]

                    def emit_mav(sc, n_sc=n_sc):
                        dbuf = sc % 2
                        if sc == 0:
                            p.wait("tensor", "s_v", ctx_guard["v"])
                        p.wait("tensor", "s_a", exp_guard[dbuf])
                        def mav(e, sc=sc, dbuf=dbuf, first=(sc == 0),
                                last=(sc == n_sc - 1)):
                            last_i = None
                            for h2 in range(2):
                                last_i = e.matmul(
                                    pcs[h2][:, 0:65],
                                    pT[:, dbuf * 512 + h2 * 128:
                                       dbuf * 512 + (h2 + 1) * 128],
                                    vaa[:, sc * 65:(sc + 1) * 65],
                                    start=first, stop=last)
                                last_i = e.matmul(
                                    pcs[2 + h2][:, 0:65],
                                    pT[:, dbuf * 512 + 256 + h2 * 128:
                                       dbuf * 512 + 256 + (h2 + 1) * 128],
                                    vab[:, sc * 65:(sc + 1) * 65],
                                    start=first, stop=last)
                            return last_i
                        p.t(mav, inc="s_t")
                        pT_guard[dbuf] = p.c("s_t")

                    for sc in range(n_sc):
                        dbuf = sc % 2
                        pst = psts[dbuf]
                        p.wait("tensor", "s_a", exp_guard[dbuf])
                        hilo = causal and tq == 0
                        def mst(e, sc=sc, tq=tq, pst=pst, hilo=hilo):
                            terms = [(kf, qf, True, not hilo)]
                            if hilo:
                                terms += [(kf, qlo, False, False),
                                          (klo, qf, False, True)]
                            last = None
                            for (kk, qq, st0, sp0) in terms:
                                kap = (kk[:, sc * 128:(sc + 1) * 128]
                                       if kk is not klo else kk[:, :]
                                       [:, sc * 128:(sc + 1) * 128])
                                qap = (qq[:, tq * 256:(tq + 1) * 256]
                                       if qq is not qlo else qq[:, 0:256])
                                e.matmul(pst[:, 256:512], kap[0:64, :],
                                         qap[0:64, :], start=st0, stop=sp0,
                                         tile_position=(0, 0))
                                last = e.matmul(pst[:, 512:768], kap[64:128, :],
                                                qap[64:128, :], start=st0,
                                                stop=sp0,
                                                tile_position=(64, 0))
                            return last
                        p.t(mst, inc="s_t")
                        mst_done = p.c("s_t")
                        masked = causal and (sc * 128 >= tq * 256)
                        if masked:
                            mk = "c_mask%d" % (sc - 2 * tq)
                            p.wait("vector", "s_t", mst_done)
                            p.v(lambda e, pst=pst, mk=mk: e.tensor_tensor(
                                pst[:, 256:512], pst[:, 256:512], cb[mk][:, :],
                                OP.add), inc="s_v")
                            p.v(lambda e, pst=pst, mk=mk: e.tensor_tensor(
                                pst[:, 512:768], pst[:, 512:768], cb[mk][:, :],
                                OP.add), inc="s_v")
                            p.wait("scalar", "s_v")
                        p.wait("scalar", "s_t", mst_done)
                        p.wait("scalar", "s_t", pT_guard[dbuf])
                        def evex(e, pst=pst, dbuf=dbuf):
                            return e.activation(pT[:, dbuf * 512:(dbuf + 1) * 512],
                                                pst[:, 256:768], AF.Exp,
                                                scale=(0.125 if L == 0 else 1.0),
                                                bias=cvec[:, 0:1])
                        p.s(evex, inc="s_a")
                        exp_guard[dbuf] = p.c("s_a")
                        if sc > 0:
                            emit_mav(sc - 1)
                    emit_mav(n_sc - 1)
                    av_done = p.c("s_t")
                    p.wait("vector", "s_t", av_done)
                    for g in range(4):
                        head = 0 if g < 2 else 1
                        h2 = g % 2
                        tile = 2 * tq + h2
                        col = tile * C + (2 * hp + head) * D
                        p.v(lambda e, g=g: e.reciprocal(rc[:, g:g + 1],
                                                        pcs[g][:, 64:65]),
                            inc="s_v")
                        p.wait("vector", "s_v")
                        p.v(lambda e, g=g, col=col: e.tensor_scalar(
                            ctx_tm[:, col:col + D], pcs[g][:, 0:D],
                            rc[:, g:g + 1], None, OP.mult), inc="s_v")
                    ctx_guard["v"] = p.c("s_v")
                    if tq == 7:
                        p.wait("tensor", "s_v", p.c("s_v"))
            join_all()

        def emit_bcast(dst, dram_ap, n=C, off=0):
            p.g(lambda e: e.dma_start(
                out=dst[:, :],
                in_=bass.AP(dram_ap.tensor, off, [[0, 128], [1, n]])),
                inc="s_gp", n=16)

        def emit_attn_proj(L, xnT, slo=None):
            ptp, pproj = psum_phase((f"ptp{L}", 128, 4096, BF16),
                                    (f"pproj{L}", 128, 1024, F32))
            if L == 0:
                wpj, ctxT = sbuf_phase((f"wpj{L}", 128, NCH * C, BF16),
                                       (f"ctxT{L}", 128, C, BF16))
                yb = gf = bof = None
            else:
                wpj, ctxT, yb, xnf, bof, gaf, bef = sbuf_phase(
                    (f"wpj{L}", 128, NCH * C, BF16),
                    (f"ctxT{L}", 128, C, BF16),
                    (f"yb{L}", 128, C, F32),
                    (f"xnf{L}", 128, C, F32),
                    (f"bof{L}", 128, C, F32),
                    (f"gaf{L}", 128, C, F32),
                    (f"bef{L}", 128, C, F32))
            join_all()
            wsrc = l0_wproj if L == 0 else l1_wo
            p.dma("gpsimd",
                  lambda: bass.AP(wpj, 0, [[NCH * C, 128], [C, 8], [1, C]]),
                  lambda: bass.AP(wsrc.tensor, 0, [[C, 128], [128 * C, 8], [1, C]]),
                  "s_gp")
            if L == 1:
                emit_bcast(bof, l1_bo)
                emit_bcast(gaf, l1_gamma)
                emit_bcast(bef, l1_beta)
                p.wait("vector", "s_ld", p.c("s_ld"))
            p.wait("tensor", "s_gp", p.c("s_gp"))
            tp_free = 0
            import os as _os6
            _ks = _os6.environ.get("K_STAGE", "")
            _nt = 1 if _ks == "proj1" else (2 if _ks == "proj2" else NT)
            for i in range(_nt):
                srcs = [(ctx_tm, i * C, 128, ctxT, 0, 128, 8)]
                if _ks == "projnotp" and i > 0:
                    srcs = []
                if L == 1:
                    p.wait("vector", "s_a")
                    p.v(lambda e, i=i: e.tensor_scalar(
                        xnf[:, :], x_tm[:, i * C:(i + 1) * C],
                        stat[:, 2 * i:2 * i + 1], stat[:, 2 * i + 1:2 * i + 2],
                        OP.subtract, OP.mult), inc="s_v")
                p.wait("tensor", "s_a")
                p.wait("tensor", "s_v")
                for (stens, sc0, sstr, dtens, dc0, dstr, ntile) in srcs:
                    for g0 in (0, 4):
                        p.wait("tensor", "s_v", tp_free)
                        def trc(e, stens=stens, sc0=sc0, sstr=sstr, g0=g0):
                            for j in range(4):
                                e.transpose(ptp[:, j * 1024:j * 1024 + 128],
                                            stens[:, sc0 + (g0 + j) * sstr:
                                                  sc0 + (g0 + j) * sstr + 128],
                                            cb["c_ident"][:, :])
                            return e.nop()
                        p.t(trc, inc="s_t")
                        p.wait("vector", "s_t", p.c("s_t"))
                        def evc(e, dtens=dtens, dc0=dc0, dstr=dstr, g0=g0):
                            last = None
                            for j in range(4):
                                last = e.tensor_copy(
                                    dtens[:, dc0 + (g0 + j) * dstr:
                                          dc0 + (g0 + j) * dstr + 128],
                                    ptp[:, j * 1024:j * 1024 + 128])
                            return last
                        p.v(evc, inc="s_v")
                        tp_free = p.c("s_v")
                import os as _os4
                if _os4.environ.get("K_STAGE") == "projtp":
                    continue
                p.wait("tensor", "s_v", tp_free)
                p.wait("tensor", "s_a")
                p.wait("tensor", "s_v")
                for h in range(2):
                    def mmp(e, h=h):
                        last = None
                        for c in range(NCH):
                            last = e.matmul(pproj[:, h * 512:(h + 1) * 512],
                                            ctxT[:, c * 128:(c + 1) * 128],
                                            wpj[:, c * C + h * 512:
                                                c * C + (h + 1) * 512],
                                            start=(c == 0), stop=(c == NCH - 1))
                        return last
                    p.t(mmp, inc="s_t")
                mm_done = p.c("s_t")
                import os as _os5
                if _os5.environ.get("K_STAGE") == "projmm":
                    p.wait("scalar", "s_t", mm_done)
                    p.s(lambda e: e.activation(ctxT[:, 0:512], pproj[:, 0:512],
                                               AF.Identity), inc="s_a")
                    continue
                if _os5.environ.get("K_STAGE") == "projev":
                    p.wait("scalar", "s_t", mm_done)
                    p.s(lambda e: e.activation(ctxT[:, :], pproj[:, :],
                                               AF.Identity), inc="s_a")
                    continue
                if L == 0:
                    p.wait("scalar", "s_t", mm_done)
                    for h in range(2):
                        p.s(lambda e, i=i, h=h: e.activation(
                            ctx_tm[:, i * C + h * 512:i * C + (h + 1) * 512],
                            pproj[:, h * 512:(h + 1) * 512],
                            AF.Identity), inc="s_a")
                    if slo is not None and i < 4:
                        p.wait("vector", "s_a")
                        for h in range(2):
                            p.v(lambda e, i=i, h=h: e.scalar_tensor_tensor(
                                slo[:, i * C + h * 512:i * C + (h + 1) * 512],
                                ctx_tm[:, i * C + h * 512:
                                       i * C + (h + 1) * 512], -1.0,
                                pproj[:, h * 512:(h + 1) * 512],
                                OP.mult, OP.add), inc="s_v")
                else:
                    p.wait("vector", "s_t", mm_done)
                    for h in range(2):
                        p.v(lambda e, h=h: e.scalar_tensor_tensor(
                            yb[:, h * 512:(h + 1) * 512],
                            pproj[:, h * 512:(h + 1) * 512], 1.0,
                            bof[:, h * 512:(h + 1) * 512], OP.mult, OP.add),
                            inc="s_v")
                    p.wait("vector", "s_v")
                    p.v(lambda e: e.tensor_tensor(yb[:, :], yb[:, :],
                                                  xnf[:, :], OP.add),
                        inc="s_v")
                    p.wait("vector", "s_v")
                    mean, rstd = _ln_stats(p, yb[:, :], sc_sq, sums, sumsq,
                                           stat, 1e-3, i)
                    p.wait("vector", "s_a", p.c("s_a"))
                    p.v(lambda e, m=mean, r=rstd: e.tensor_scalar(
                        yb[:, :], yb[:, :], m, r, OP.subtract, OP.mult),
                        inc="s_v")
                    p.wait("vector", "s_v")
                    p.v(lambda e: e.tensor_tensor(yb[:, :], yb[:, :],
                                                  gaf[:, :], OP.mult),
                        inc="s_v")
                    p.wait("vector", "s_v")
                    p.v(lambda e, i=i: e.tensor_tensor(
                        ctx_tm[:, i * C:(i + 1) * C], yb[:, :], bef[:, :],
                        OP.add), inc="s_v")
            join_all()

        (xnT0,), close_xnT0 = sbuf_group(("xnT0", 128, NCH * T, BF16))
        close_l0c = load_consts(["c_cos_hi", "c_sin_hi", "c_cos_lo",
                                 "c_sin_lo", "c_perm", "c_mask0", "c_mask1"])
        emit_ln_xnT(1e-5, xnT0)
        emit_attention(0, xnT0)
        if debug_name == "ctxpre":
            join_all(("gpsimd",))
            p.dma("gpsimd", lambda: dbg.rearrange("(p a) m -> p (a m)", p=128),
                  lambda: ctx_tm[:, :], "s_gp")
            p.wait("gpsimd", "s_gp", p.c("s_gp"))
        close_l0c()
        close_xnT0()
        (slo,), close_slo = sbuf_group(("slo", 128, 4 * C, BF16))
        emit_attn_proj(0, None, slo=slo)

        def emit_quant(src_fn, dst_fn, qs, n_sec=1, veng="vector",
                       multi=False):
            amax, m_, inv, cmp, f_, G = (qs[k] for k in
                                         ("amax", "m", "inv", "cmp", "f", "G"))
            u, cbuf, d, m1, m2 = (qs[k] for k in ("u", "c", "d", "m1", "m2"))
            ef = qs.get("ef")
            nw = n_sec * 32
            def amaxes(e):
                last = None
                for s2 in range(n_sec):
                    src3 = src_fn(s2).rearrange("p (b k) -> p b k", k=32)
                    last = e.tensor_reduce(
                        amax[:, s2 * 32:(s2 + 1) * 32], src3, AXX, OP.max,
                        apply_absolute_value=True)
                return last
            p.v(amaxes, inc="s_v")
            p.wait("vector", "s_v")
            p.v(lambda e: e.tensor_scalar(m_[:, :nw], amax[:, :nw],
                                          float(2.0 ** -64), None, OP.max),
                inc="s_v")
            p.wait("vector", "s_v")
            p.v(lambda e: e.tensor_scalar(m_[:, :nw], m_[:, :nw],
                                          float(2.0 ** 64), None, OP.mult))
            p.v(lambda e: e.memset(inv[:, :nw], float(2.0 ** 67)),
                inc="s_v")
            if not multi:
                p.v(lambda e: e.memset(ef[:, :nw], -67.0), inc="s_v")
            p.wait("vector", "s_v")
            for bit in (64, 32, 16, 8, 4, 2, 1):
                k = min(bit, 16)
                rep = bit // k
                p.v(lambda e, bit=bit: e.tensor_scalar(
                    cmp[:, :nw], m_[:, :nw], float(2.0 ** bit), None,
                    OP.is_ge), inc="s_v")
                p.wait("vector", "s_v")
                p.v(lambda e, k=k: e.tensor_scalar(
                    f_[:, :nw], cmp[:, :nw], float(2.0 ** -k) - 1.0, 1.0,
                    OP.mult, OP.add), inc="s_v")
                if not multi:
                    p.v(lambda e, bit=bit: e.scalar_tensor_tensor(
                        ef[:, :nw], cmp[:, :nw], float(bit), ef[:, :nw],
                        OP.mult, OP.add), inc="s_v")
                p.wait("vector", "s_v")
                for _r in range(rep):
                    p.v(lambda e: e.tensor_tensor(m_[:, :nw], m_[:, :nw],
                                                  f_[:, :nw], OP.mult))
                    p.v(lambda e: e.tensor_tensor(inv[:, :nw], inv[:, :nw],
                                                  f_[:, :nw], OP.mult),
                        inc="s_v")
                    p.wait("vector", "s_v")
            p.v(lambda e: e.reciprocal(G[:, :nw], inv[:, :nw]), inc="s_v")
            p.wait("vector", "s_v")
            if not multi:
                def sop(fn):
                    p.v(fn, inc="s_v")
                    p.wait("vector", "s_v")
                sop(lambda e: e.tensor_scalar(cmp[:, :nw], ef[:, :nw],
                                              -16.5, None, OP.is_ge))
                sop(lambda e: e.tensor_scalar(m_[:, :nw], ef[:, :nw],
                                              -11.5, None, OP.is_le))
                sop(lambda e: e.tensor_tensor(cmp[:, :nw], cmp[:, :nw],
                                              m_[:, :nw], OP.mult))
                sop(lambda e: e.tensor_scalar(m_[:, :nw], ef[:, :nw],
                                              -8.5, None, OP.is_ge))
                sop(lambda e: e.tensor_scalar(amax[:, :nw], ef[:, :nw],
                                              -5.5, None, OP.is_le))
                sop(lambda e: e.tensor_tensor(m_[:, :nw], m_[:, :nw],
                                              amax[:, :nw], OP.mult))
                sop(lambda e: e.tensor_tensor(cmp[:, :nw], cmp[:, :nw],
                                              m_[:, :nw], OP.add))
                sop(lambda e: e.tensor_scalar(m_[:, :nw], ef[:, :nw],
                                              -3.0, None, OP.is_equal))
                sop(lambda e: e.tensor_tensor(cmp[:, :nw], cmp[:, :nw],
                                              m_[:, :nw], OP.add))
                sop(lambda e: e.tensor_scalar(m_[:, :nw], ef[:, :nw],
                                              -11.0, None, OP.is_equal))
                sop(lambda e: e.tensor_scalar(f_[:, :nw], cmp[:, :nw],
                                              -1.0, 1.0, OP.mult, OP.add))
                sop(lambda e: e.tensor_tensor(f_[:, :nw], f_[:, :nw],
                                              m_[:, :nw], OP.subtract))
            cX, nX, wA, w2, rA = (qs[k] for k in
                                  ("cx", "nx", "wa", "w2", "ra"))
            for s in range(n_sec):
                nb0 = s * 32
                src3 = src_fn(s).rearrange("p (b k) -> p b k", k=32)
                inv_b = inv[:, nb0:nb0 + 32].unsqueeze(2).broadcast_to(
                    [128, 32, 32])
                G_b = G[:, nb0:nb0 + 32].unsqueeze(2).broadcast_to(
                    [128, 32, 32])
                u3 = u[:, :].rearrange("p (b k) -> p b k", k=32)
                if not multi:
                    dirZ_b = f_[:, nb0:nb0 + 32].unsqueeze(2).broadcast_to(
                        [128, 32, 32])
                    dirP_b = m_[:, nb0:nb0 + 32].unsqueeze(2).broadcast_to(
                        [128, 32, 32])
                    def wchain(e, s=s, src3=src3, inv_b=inv_b, G_b=G_b,
                               dirZ_b=dirZ_b, dirP_b=dirP_b):
                        e.tensor_tensor(u3, src3, inv_b, OP.mult)
                        e.tensor_scalar(cbuf[:, :], u[:, :], 16.5, None,
                                        OP.add)
                        e.scalar_tensor_tensor(d[:, :], cbuf[:, :], -1.0,
                                               u[:, :], OP.mult, OP.add)
                        e.tensor_scalar(m1[:, :], d[:, :], -15.5, None,
                                        OP.is_ge)
                        e.tensor_scalar(m2[:, :], d[:, :], -16.5, None,
                                        OP.is_lt)
                        e.scalar_tensor_tensor(m1[:, :], m2[:, :], -1.0,
                                               m1[:, :], OP.mult, OP.add)
                        e.tensor_scalar(m2[:, :], d[:, :], -16.5, None,
                                        OP.is_equal)
                        e.tensor_scalar(wA[:, :], u[:, :], 0.0, None,
                                        OP.is_ge)
                        e.tensor_tensor(d[:, :], cbuf[:, :], m1[:, :],
                                        OP.add)
                        e.tensor_scalar(nX[:, :], d[:, :], 0.5, None,
                                        OP.mult)
                        e.tensor_scalar(cX[:, :], nX[:, :], 16.25, None,
                                        OP.add)
                        e.scalar_tensor_tensor(nX[:, :], cX[:, :], -1.0,
                                               nX[:, :], OP.mult, OP.add)
                        e.tensor_scalar(w2[:, :], nX[:, :], -16.0, None,
                                        OP.is_equal)
                        e.tensor_scalar(rA[:, :], w2[:, :], -1.0, 1.0,
                                        OP.mult, OP.add)
                        e.tensor_tensor(w2[:, :], wA[:, :], rA[:, :], OP.mult)
                        e.scalar_tensor_tensor(nX[:, :], w2[:, :], -2.0,
                                               wA[:, :], OP.mult, OP.add)
                        e.tensor_tensor(nX[:, :], nX[:, :], rA[:, :],
                                        OP.add)
                        nx3 = nX[:, :].rearrange("p (b k) -> p b k", k=32)
                        e.tensor_tensor(nx3, nx3, dirZ_b, OP.mult)
                        e.tensor_tensor(nx3, nx3, dirP_b, OP.add)
                        e.tensor_scalar(rA[:, :], wA[:, :], 2.0, -1.0,
                                        OP.mult, OP.add)
                        e.tensor_tensor(nX[:, :], nX[:, :], rA[:, :], OP.mult)
                        e.tensor_tensor(nX[:, :], nX[:, :], wA[:, :],
                                        OP.subtract)
                        e.tensor_tensor(nX[:, :], nX[:, :], m2[:, :],
                                        OP.mult)
                        e.tensor_tensor(d[:, :], d[:, :], nX[:, :], OP.add)
                        d3 = d[:, :].rearrange("p (b k) -> p b k", k=32)
                        return e.scalar_tensor_tensor(
                            dst_fn(s).rearrange("p (b k) -> p b k", k=32), d3,
                            -16.0, G_b, OP.add, OP.mult)
                    p.v(wchain, inc="s_v")
                    p.wait("vector", "s_v")
                else:
                    dstf = dst_fn(s)
                    dst3 = dstf.rearrange("p (b k) -> p b k", k=32)
                    def mchain(e, src3=src3, dstf=dstf, dst3=dst3,
                               inv_b=inv_b, G_b=G_b):
                        e.tensor_tensor(u3, src3, inv_b, OP.mult)
                        e.tensor_scalar(cbuf[:, :], u[:, :], 16.5, None,
                                        OP.add)
                        e.scalar_tensor_tensor(d[:, :], cbuf[:, :], -1.0,
                                               u[:, :], OP.mult, OP.add)
                        e.tensor_scalar(m1[:, :], d[:, :], -15.5, None,
                                        OP.is_ge)
                        e.tensor_scalar(m2[:, :], d[:, :], -16.5, None,
                                        OP.is_lt)
                        e.scalar_tensor_tensor(m1[:, :], m2[:, :], -1.0,
                                               m1[:, :], OP.mult, OP.add)
                        e.tensor_tensor(nX[:, :], cbuf[:, :], m1[:, :],
                                        OP.add)
                        e.tensor_scalar(wA[:, :], d[:, :], -17.5, None,
                                        OP.is_ge)
                        e.tensor_scalar(m2[:, :], d[:, :], -14.5, None,
                                        OP.is_lt)
                        e.tensor_tensor(wA[:, :], wA[:, :], m2[:, :], OP.mult)
                        e.scalar_tensor_tensor(dstf, nX[:, :], -16.0,
                                               wA[:, :], OP.add, OP.mult)
                        e.tensor_scalar(rA[:, :], wA[:, :], -1.0, 1.0,
                                        OP.mult, OP.add)
                        for ci, mulk in enumerate((2.0, 0.5)):
                            e.tensor_scalar(cX[:, :], u[:, :], mulk, 16.5,
                                            OP.mult, OP.add)
                            e.scalar_tensor_tensor(d[:, :], cX[:, :], -1.0,
                                                   u[:, :], OP.mult, OP.add)
                            e.tensor_scalar(m1[:, :], d[:, :], -15.5, None,
                                            OP.is_ge)
                            e.tensor_scalar(m2[:, :], d[:, :], -16.5, None,
                                            OP.is_lt)
                            e.scalar_tensor_tensor(m1[:, :], m2[:, :], -1.0,
                                                   m1[:, :], OP.mult, OP.add)
                            e.tensor_tensor(nX[:, :], cX[:, :], m1[:, :],
                                            OP.add)
                            e.tensor_scalar(w2[:, :], d[:, :], -17.5, None,
                                            OP.is_ge)
                            e.tensor_scalar(m2[:, :], d[:, :], -14.5, None,
                                            OP.is_lt)
                            e.tensor_tensor(w2[:, :], w2[:, :], m2[:, :],
                                            OP.mult)
                            e.tensor_tensor(w2[:, :], rA[:, :], w2[:, :],
                                            OP.mult)
                            e.scalar_tensor_tensor(nX[:, :], nX[:, :], -16.0,
                                                   w2[:, :], OP.add, OP.mult)
                            e.tensor_tensor(dstf, dstf, nX[:, :], OP.add)
                            if ci == 0:
                                e.tensor_tensor(rA[:, :], rA[:, :], w2[:, :],
                                                OP.subtract)
                        return e.tensor_tensor(dst3, dst3, G_b, OP.mult)
                    p.v(mchain, inc="s_v")
                    p.wait("vector", "s_v")

        def emit_ffn0(slo):
            SW = 4
            NSW = NT // SW
            qs_specs = [("q_amax", 128, 128, F32), ("q_m", 128, 128, F32),
                        ("q_inv", 128, 128, F32), ("q_cmp", 128, 128, F32),
                        ("q_f", 128, 128, F32), ("q_G", 128, 128, F32),
                        ("q_u", 128, C, F32), ("q_c", 128, C, I32),
                        ("q_m1", 128, C, BF16), ("q_m2", 128, C, BF16),
                        ("q_nx", 128, C, BF16),
                        ("q_wa", 128, C, BF16), ("q_w2", 128, C, BF16),
                        ("q_ra", 128, C, BF16), ("q_ef", 128, 128, F32)]
            names = ("amax", "m", "inv", "cmp", "f", "G", "u", "c",
                     "m1", "m2", "nx", "wa", "w2", "ra", "ef")
            hs, close_qs = sbuf_group(*qs_specs)
            qs = dict(zip(names, hs))
            qs["d"] = sc_sq
            qs["cx"] = qs["c"]
            (wbuf, wqbuf), close_prep = sbuf_group(
                ("wbuf", 128, 8 * C, F32), ("wqbuf", 128, 8 * C, BF16))
            join_all()
            for widx, (wsrc, whbm, rows, cols) in enumerate(
                    ((l0_wfc, wfcq_hbm, FFN0, C), (l0_wp, wprojq_hbm, C, FFN0))):
                ntile_r = rows // 128
                secs_per_tile = cols // C
                for q4 in range(4):
                    nt_q = ntile_r // 4
                    p.wait("gpsimd", "s_out")
                    def ldw(e, wsrc=wsrc, q4=q4, nt_q=nt_q, cols=cols):
                        return e.dma_start(
                            out=bass.AP(wbuf, 0, [[8 * C, 128], [cols, nt_q],
                                                  [1, cols]]),
                            in_=bass.AP(wsrc.tensor, q4 * nt_q * 128 * cols,
                                        [[cols, 128], [128 * cols, nt_q],
                                         [1, cols]]))
                    p.wait("sync", "s_v")
                    p.wait("sync", "s_g")
                    p.y(ldw, inc="s_ld", n=16)
                    p.wait("vector", "s_ld")
                    p.wait("gpsimd", "s_ld")
                    nsec = nt_q * secs_per_tile
                    for off in range(0, nsec, 4):
                        kk = min(4, nsec - off)
                        emit_quant(
                            lambda s, off=off:
                                wbuf[:, (off + s) * C:(off + s + 1) * C],
                            lambda s, off=off:
                                wqbuf[:, (off + s) * C:(off + s + 1) * C],
                            qs, n_sec=kk)
                    def stw(e, whbm=whbm, q4=q4, nt_q=nt_q, cols=cols):
                        return e.dma_start(
                            out=bass.AP(whbm.tensor, q4 * nt_q * 128 * cols,
                                        [[cols, 128], [128 * cols, nt_q],
                                         [1, cols]]),
                            in_=bass.AP(wqbuf, 0, [[8 * C, 128], [cols, nt_q],
                                                   [1, cols]]))
                    p.wait("gpsimd", "s_v")
                    p.g(stw, inc="s_gp", n=16)
            close_prep()
            (xq_sw, wfcqT, wpjT, g_acc, hbuf, hq, hqT,
             bfcb), close_sw = sbuf_group(
                ("xq_sw", 128, NCH * SW * 128, BF16),
                ("wfcqT", 128, 8 * C, BF16),
                ("wpjT", 128, 4 * C, BF16),
                ("g_acc", 128, SW * C, F32),
                ("hbuf", 128, 4 * C, F32),
                ("hq", 128, 4 * C, BF16),
                ("hqT", 128, 512, BF16),
                ("bfcb", 128, C, BF16))
            bpjb = bfcb
            pfc, ppj, ptq = psum_phase(("pfc", 128, 1024, F32),
                                       ("ppj", 128, 1024, F32),
                                       ("ptq", 128, 4096, BF16))
            join_all()
            emit_bcast(bpjb, l0_bp)
            st_guard = {"v": 0}

            def tq_transposes(stens, scol0, sstride, dtens, dcol0, dstride, n):
                for g0 in range(0, n, 4):
                    k = min(4, n - g0)
                    p.wait("tensor", "s_v", st_guard["v"])
                    p.wait("tensor", "s_a")
                    p.wait("tensor", "s_v")
                    def trq(e, g0=g0, k=k, stens=stens, scol0=scol0,
                            sstride=sstride):
                        for j in range(k):
                            e.transpose(ptq[:, j * 1024:j * 1024 + 128],
                                        stens[:, scol0 + (g0 + j) * sstride:
                                              scol0 + (g0 + j) * sstride + 128],
                                        cb["c_ident"][:, :])
                        return e.nop()
                    p.t(trq, inc="s_t")
                    p.wait("scalar", "s_t", p.c("s_t"))
                    def evq(e, g0=g0, k=k, dtens=dtens, dcol0=dcol0,
                            dstride=dstride):
                        last = None
                        for j in range(k):
                            last = e.copy(
                                dtens[:, dcol0 + (g0 + j) * dstride:
                                      dcol0 + (g0 + j) * dstride + 128],
                                ptq[:, j * 1024:j * 1024 + 128])
                        return last
                    p.s(evq, inc="s_v")
                    st_guard["v"] = p.c("s_v")

            for sw in range(NSW):
                tiles = list(range(sw * SW, (sw + 1) * SW))
                p.wait("vector", "s_a")
                def stage_xn(e, tiles=tiles):
                    last = None
                    for tloc, ti in enumerate(tiles):
                        last = e.tensor_scalar(
                            hbuf[:, tloc * C:(tloc + 1) * C],
                            x_tm[:, ti * C:(ti + 1) * C],
                            stat[:, 2 * ti:2 * ti + 1],
                            stat[:, 2 * ti + 1:2 * ti + 2],
                            OP.subtract, OP.mult)
                    return last
                p.v(stage_xn, inc="s_v")
                p.wait("vector", "s_v")
                emit_quant(lambda s: hbuf[:, s * C:(s + 1) * C],
                           lambda s: hq[:, s * C:(s + 1) * C],
                           qs, n_sec=4, multi=True)
                for tloc in range(4):
                    tq_transposes(hq, tloc * C, 128, xq_sw, tloc * 128,
                                  SW * 128, NCH)
                if debug_name == "xq0" and sw == 0:
                    join_all(("gpsimd",))
                    p.dma("gpsimd",
                          lambda: bass.AP(dbg.tensor, 0,
                                          [[4096, 128], [1, 4096]]),
                          lambda: xq_sw[:, :], "s_gp")
                    p.wait("gpsimd", "s_gp", p.c("s_gp"))
                for q4 in range(4):
                    p.wait("sync", "s_t")
                    for c in range(NCH):
                        p.y(lambda e, c=c, q4=q4: e.dma_start_transpose(
                            wfcqT[:, c * C:(c + 1) * C],
                            bass.AP(wfcq_hbm.tensor, q4 * 1024 * C + c * 128,
                                    [[C, 1024], [1, 128]])),
                            inc="s_sc", n=16)
                    p.wait("gpsimd", "s_v")
                    emit_bcast(bfcb, l0_bfc, off=q4 * C)
                    wload = p.c("s_sc")
                    bload = p.c("s_gp")
                    for tloc in range(4):
                        p.wait("tensor", "s_sc", wload)
                        p.wait("tensor", "s_v")
                        for h in range(2):
                            def mfc(e, h=h, tloc=tloc):
                                last = None
                                for c in range(NCH):
                                    last = e.matmul(
                                        pfc[:, h * 512:(h + 1) * 512],
                                        xq_sw[:, c * SW * 128 + tloc * 128:
                                              c * SW * 128 + (tloc + 1) * 128],
                                        wfcqT[:, c * C + h * 512:
                                              c * C + (h + 1) * 512],
                                        start=(c == 0), stop=(c == NCH - 1))
                                return last
                            p.t(mfc, inc="s_t")
                        fc_done = p.c("s_t")
                        p.wait("vector", "s_t", fc_done)
                        p.wait("vector", "s_gp", bload)
                        def evfc(e, tloc=tloc):
                            e.scalar_tensor_tensor(
                                hbuf[:, tloc * C:tloc * C + 512],
                                pfc[:, 0:512], 1.0, bfcb[:, 0:512],
                                OP.mult, OP.add)
                            return e.scalar_tensor_tensor(
                                hbuf[:, tloc * C + 512:(tloc + 1) * C],
                                pfc[:, 512:1024], 1.0, bfcb[:, 512:1024],
                                OP.mult, OP.add)
                        p.v(evfc, inc="s_v")
                    p.wait("scalar", "s_v")
                    def gelus(e):
                        last = None
                        for tloc in range(4):
                            last = e.activation(
                                hbuf[:, tloc * C:(tloc + 1) * C],
                                hbuf[:, tloc * C:(tloc + 1) * C], AF.Gelu)
                        return last
                    p.s(gelus, inc="s_a")
                    p.wait("vector", "s_a")
                    emit_quant(lambda s: hbuf[:, s * C:(s + 1) * C],
                               lambda s: hq[:, s * C:(s + 1) * C],
                               qs, n_sec=4, multi=True)
                    if debug_name == "h159" and sw == 0:
                        join_all(("gpsimd",))
                        p.dma("gpsimd",
                              lambda q4=q4: bass.AP(
                                  dbg.tensor, q4 * 128 * 1024,
                                  [[1024, 128], [1, 1024]]),
                              lambda: hbuf[:, C:2 * C], "s_gp")
                        p.dma("gpsimd",
                              lambda q4=q4: bass.AP(
                                  dbg.tensor, 512 * 1024 + q4 * 128 * 1024,
                                  [[1024, 128], [1, 1024]]),
                              lambda: hq[:, C:2 * C], "s_gp")
                        p.wait("gpsimd", "s_gp", p.c("s_gp"))
                        p.wait("vector", "s_gp", p.c("s_gp"))
                        p.wait("tensor", "s_gp", p.c("s_gp"))
                    for wh in range(2):
                        p.wait("sync", "s_t")
                        for j in range(4):
                            nglob = q4 * 8 + wh * 4 + j
                            p.y(lambda e, j=j, nglob=nglob:
                                e.dma_start_transpose(
                                    wpjT[:, j * C:(j + 1) * C],
                                    bass.AP(wprojq_hbm.tensor, nglob * 128,
                                            [[FFN0, C], [1, 128]])),
                                inc="s_sc", n=16)
                        wpl = p.c("s_sc")
                        for tloc in range(4):
                            tq_transposes(hq, tloc * C + wh * 512, 128,
                                          hqT, 0, 128, 4)
                            p.wait("tensor", "s_sc", wpl)
                            p.wait("tensor", "s_v")
                            for h in range(2):
                                def mpj(e, h=h, wh=wh):
                                    last = None
                                    for j in range(4):
                                        last = e.matmul(
                                            ppj[:, h * 512:(h + 1) * 512],
                                            hqT[:, j * 128:(j + 1) * 128],
                                            wpjT[:, j * C + h * 512:
                                                 j * C + (h + 1) * 512],
                                            start=(j == 0), stop=(j == 3))
                                    return last
                                p.t(mpj, inc="s_t")
                            pj_done = p.c("s_t")
                            p.wait("vector", "s_t", pj_done)
                            def evpj(e, tloc=tloc, first=(q4 == 0 and wh == 0)):
                                last = None
                                for h in range(2):
                                    ga = g_acc[:, tloc * C + h * 512:
                                               tloc * C + (h + 1) * 512]
                                    if first:
                                        last = e.tensor_copy(
                                            ga, ppj[:, h * 512:(h + 1) * 512])
                                    else:
                                        last = e.scalar_tensor_tensor(
                                            ga, ppj[:, h * 512:(h + 1) * 512],
                                            1.0, ga, OP.mult, OP.add)
                                return last
                            p.v(evpj, inc="s_v")
                            p.wait("tensor", "s_v")
                p.wait("gpsimd", "s_v")
                emit_bcast(bpjb, l0_bp)
                for tloc, ti in enumerate(tiles):
                    ga = lambda tloc=tloc: g_acc[:, tloc * C:(tloc + 1) * C]
                    p.wait("vector", "s_gp")
                    p.v(lambda e, tloc=tloc: e.tensor_tensor(
                        g_acc[:, tloc * C:(tloc + 1) * C],
                        g_acc[:, tloc * C:(tloc + 1) * C],
                        bpjb[:, :], OP.add), inc="s_v")
                    p.wait("scalar", "s_v")
                    p.s(lambda e, tloc=tloc: e.activation(
                        g_acc[:, tloc * C:(tloc + 1) * C],
                        g_acc[:, tloc * C:(tloc + 1) * C],
                        AF.Silu), inc="s_a")
                    p.wait("vector", "s_a")
                    hb = hbuf[:, tloc * C:(tloc + 1) * C]
                    if ti < 4:
                        u = qs["u"]
                        p.v(lambda e, ti=ti: e.tensor_tensor(
                            u[:, :], ctx_tm[:, ti * C:(ti + 1) * C],
                            slo[:, ti * C:(ti + 1) * C], OP.add), inc="s_v")
                        p.wait("vector", "s_v")
                        p.v(lambda e, ti=ti, tloc=tloc, u=u, hb=hb:
                            e.tensor_tensor(
                                hb, u[:, :],
                                g_acc[:, tloc * C:(tloc + 1) * C], OP.mult),
                            inc="s_v")
                    else:
                        p.v(lambda e, ti=ti, tloc=tloc, hb=hb: e.tensor_tensor(
                            hb, ctx_tm[:, ti * C:(ti + 1) * C],
                            g_acc[:, tloc * C:(tloc + 1) * C], OP.mult),
                            inc="s_v")
                    p.wait("vector", "s_v")
                    p.v(lambda e, ti=ti, hb=hb: e.tensor_tensor(
                        x_tm[:, ti * C:(ti + 1) * C],
                        x_tm[:, ti * C:(ti + 1) * C], hb, OP.add),
                        inc="s_v")
                    p.wait("vector", "s_v")
            join_all()
            close_sw()
            close_qs()

        emit_ffn0(slo)
        close_slo()
        if debug_name == "x1d":
            join_all(("gpsimd",))
            p.dma("gpsimd", lambda: dbg.rearrange("(p a) m -> p (a m)", p=128),
                  lambda: x_tm[:, :], "s_gp")
            p.wait("gpsimd", "s_gp", p.c("s_gp"))

        def emit_ffn1(xnT):
            HQ = 4
            HCQ = FFN1 // 128 // HQ
            TQ4 = 4
            (wtmp,), close_wt = sbuf_group(("wtmp", 128, 2 * FFN1, BF16))
            join_all()
            for wsrc, wdst, rows, cols in ((l1_wg, wg_bf_hbm, FFN1, C),
                                           (l1_wu, wu_bf_hbm, FFN1, C),
                                           (l1_wd, wd_bf_hbm, C, FFN1)):
                ntr = rows // 128
                per = (2 * FFN1) // cols
                for t0 in range(0, ntr, per):
                    k = min(per, ntr - t0)
                    p.wait("gpsimd", "s_gp")
                    p.g(lambda e, wsrc=wsrc, t0=t0, k=k, cols=cols:
                        e.dma_start(
                            out=bass.AP(wtmp, 0, [[2 * FFN1, 128], [cols, k],
                                                  [1, cols]]),
                            in_=bass.AP(wsrc.tensor, t0 * 128 * cols,
                                        [[cols, 128], [128 * cols, k],
                                         [1, cols]])), inc="s_gp", n=16)
                    p.wait("gpsimd", "s_gp")
                    p.g(lambda e, wdst=wdst, t0=t0, k=k, cols=cols:
                        e.dma_start(
                            out=bass.AP(wdst.tensor, t0 * 128 * cols,
                                        [[cols, 128], [128 * cols, k],
                                         [1, cols]]),
                            in_=bass.AP(wtmp, 0, [[2 * FFN1, 128], [cols, k],
                                                  [1, cols]])), inc="s_gp",
                        n=16)
            close_wt()
            (wgT, wuT, wdTq, hh, hl, hf, g_acc, sg, su), close_f1 = sbuf_group(
                ("wgT", 128, NCH * 768, BF16),
                ("wuT", 128, NCH * 768, BF16),
                ("wdTq", 128, HCQ * C, BF16),
                ("hh", 128, HCQ * 512, BF16),
                ("hl", 128, HCQ * 512, BF16),
                ("hf", 128, 512, F32),
                ("g1_acc", 128, 4 * C, F32),
                ("sg1", 128, 512, F32),
                ("su1", 128, 512, F32))
            pgu, pdn = psum_phase(("pgu", 128, 1024, F32),
                                  ("pdn", 128, 1024, F32))
            join_all()
            for tq4 in range(TQ4):
                tcol0 = tq4 * 512
                for hq in range(HQ):
                    p.wait("sync", "s_t")
                    p.wait("sync", "s_v")
                    for c in range(NCH):
                        p.y(lambda e, c=c, hq=hq: e.dma_start_transpose(
                            wgT[:, c * 768:(c + 1) * 768],
                            bass.AP(wg_bf_hbm.tensor, hq * 768 * C + c * 128,
                                    [[C, 768], [1, 128]])), inc="s_sc", n=16)
                        p.y(lambda e, c=c, hq=hq: e.dma_start_transpose(
                            wuT[:, c * 768:(c + 1) * 768],
                            bass.AP(wu_bf_hbm.tensor, hq * 768 * C + c * 128,
                                    [[C, 768], [1, 128]])), inc="s_sc", n=16)
                    for hcl in range(HCQ):
                        hc = hq * HCQ + hcl
                        p.y(lambda e, hcl=hcl, hc=hc: e.dma_start_transpose(
                            wdTq[:, hcl * C:(hcl + 1) * C],
                            bass.AP(wd_bf_hbm.tensor, hc * 128,
                                    [[FFN1, C], [1, 128]])), inc="s_sc", n=16)
                    wload = p.c("s_sc")
                    p.wait("tensor", "s_sc", wload)
                    gu_guard = [0, 0]
                    gu_done = {}

                    def emit_gu_evac(hcl):
                        buf = pgu if hcl % 2 == 0 else pdn
                        p.wait("vector", "s_t", gu_done[hcl])
                        def evgu(e, buf=buf):
                            e.tensor_scalar(hf[:, :], buf[:, 0:512], C127L,
                                            None, OP.mult)
                            e.scalar_tensor_tensor(
                                sg[:, :], buf[:, 0:512], C127H, hf[:, :],
                                OP.mult, OP.add)
                            e.tensor_scalar(hf[:, :], buf[:, 512:1024], C127L,
                                            None, OP.mult)
                            return e.scalar_tensor_tensor(
                                su[:, :], buf[:, 512:1024], C127H, hf[:, :],
                                OP.mult, OP.add)
                        p.v(evgu, inc="s_v")
                        gu_guard[hcl % 2] = p.c("s_v")
                        p.wait("scalar", "s_v")
                        p.s(lambda e: e.activation(sg[:, :], sg[:, :],
                                                   AF.Silu), inc="s_a")
                        p.wait("vector", "s_a")
                        def evhh(e, hcl=hcl):
                            e.tensor_tensor(hf[:, :], sg[:, :], su[:, :],
                                            OP.mult)
                            e.tensor_copy(hh[:, hcl * 512:(hcl + 1) * 512],
                                          hf[:, :])
                            return e.scalar_tensor_tensor(
                                hl[:, hcl * 512:(hcl + 1) * 512],
                                hh[:, hcl * 512:(hcl + 1) * 512], -1.0,
                                hf[:, :], OP.mult, OP.add)
                        p.v(evhh, inc="s_v")

                    for hcl in range(HCQ):
                        buf = pgu if hcl % 2 == 0 else pdn
                        p.wait("tensor", "s_v", gu_guard[hcl % 2])
                        for which, wT, ps0 in (("g", wgT, 0), ("u", wuT, 512)):
                            def mgu(e, wT=wT, ps0=ps0, hcl=hcl, tcol0=tcol0,
                                    buf=buf):
                                last = None
                                for c in range(NCH):
                                    last = e.matmul(
                                        buf[:, ps0:ps0 + 512],
                                        wT[:, c * 768 + hcl * 128:
                                           c * 768 + (hcl + 1) * 128],
                                        xnT[:, c * T + tcol0:
                                            c * T + tcol0 + 512],
                                        start=(c == 0), stop=(c == NCH - 1))
                                return last
                            p.t(mgu, inc="s_t")
                        gu_done[hcl] = p.c("s_t")
                        if hcl > 0:
                            emit_gu_evac(hcl - 1)
                    emit_gu_evac(HCQ - 1)
                    hh_done = p.c("s_v")
                    p.wait("tensor", "s_v", hh_done)
                    dn_guard = [0, 0]
                    dn_done = {}

                    def emit_dn_evac(tloc, hq=hq):
                        dbuf = pdn if tloc % 2 == 0 else pgu
                        p.wait("vector", "s_t", dn_done[tloc])
                        def evdn(e, tloc=tloc, dbuf=dbuf, first=(hq == 0)):
                            last = None
                            for h in range(2):
                                ga = g_acc[:, tloc * C + h * 512:
                                           tloc * C + (h + 1) * 512]
                                if first:
                                    last = e.tensor_copy(
                                        ga, dbuf[:, h * 512:(h + 1) * 512])
                                else:
                                    last = e.scalar_tensor_tensor(
                                        ga, dbuf[:, h * 512:(h + 1) * 512],
                                        1.0, ga, OP.mult, OP.add)
                            return last
                        p.v(evdn, inc="s_v")
                        dn_guard[tloc % 2] = p.c("s_v")

                    for tloc in range(4):
                        dbuf = pdn if tloc % 2 == 0 else pgu
                        p.wait("tensor", "s_v", dn_guard[tloc % 2])
                        for h in range(2):
                            def mdn(e, tloc=tloc, h=h, dbuf=dbuf):
                                last = None
                                for part in (hh, hl):
                                    for hcl in range(HCQ):
                                        last = e.matmul(
                                            dbuf[:, h * 512:(h + 1) * 512],
                                            part[:, hcl * 512 + tloc * 128:
                                                 hcl * 512 + (tloc + 1) * 128],
                                            wdTq[:, hcl * C + h * 512:
                                                 hcl * C + (h + 1) * 512],
                                            start=(part is hh and hcl == 0),
                                            stop=(part is hl and
                                                  hcl == HCQ - 1))
                                return last
                            p.t(mdn, inc="s_t")
                        dn_done[tloc] = p.c("s_t")
                        if tloc > 0:
                            emit_dn_evac(tloc - 1)
                    emit_dn_evac(3)
                    p.wait("tensor", "s_v")
                if debug_name == "f1d":
                    join_all(("gpsimd",))
                    p.dma("gpsimd",
                          lambda tq4=tq4: bass.AP(
                              dbg.tensor, tq4 * 128 * 4096,
                              [[4096, 128], [1, 4096]]),
                          lambda: g_acc[:, :], "s_gp")
                    p.wait("gpsimd", "s_gp", p.c("s_gp"))
                    p.wait("vector", "s_gp", p.c("s_gp"))
                    p.wait("scalar", "s_gp", p.c("s_gp"))
                for tloc in range(4):
                    ti = tq4 * 4 + tloc
                    ga = g_acc[:, tloc * C:(tloc + 1) * C]
                    def ev127(e, ga=ga):
                        e.tensor_scalar(sc_sq[:, :], ga, C127L, None, OP.mult)
                        return e.scalar_tensor_tensor(
                            ga, ga, C127H, sc_sq[:, :], OP.mult, OP.add)
                    p.v(ev127, inc="s_v")
                    p.wait("scalar", "s_v")
                    p.s(lambda e, ga=ga: e.activation(
                        ga, ga, AF.Silu), inc="s_a")
                    p.wait("vector", "s_a")
                    p.v(lambda e, ti=ti, ga=ga: e.tensor_tensor(
                        sc_sq[:, :], ctx_tm[:, ti * C:(ti + 1) * C],
                        ga, OP.mult), inc="s_v")
                    p.wait("vector", "s_v")
                    p.v(lambda e, ti=ti: e.tensor_tensor(
                        x_tm[:, ti * C:(ti + 1) * C],
                        x_tm[:, ti * C:(ti + 1) * C], sc_sq[:, :],
                        OP.add), inc="s_v")
                    p.wait("vector", "s_v")
            join_all()
            close_f1()

        (xnT1,), close_xnT1 = sbuf_group(("xnT1", 128, NCH * T, BF16))
        emit_ln_xnT(1e-5, xnT1)
        emit_attention(1, xnT1)
        if debug_name == "ctx1":
            join_all(("gpsimd",))
            p.dma("gpsimd", lambda: dbg.rearrange("(p a) m -> p (a m)", p=128),
                  lambda: ctx_tm[:, :], "s_gp")
            p.wait("gpsimd", "s_gp", p.c("s_gp"))
        emit_attn_proj(1, xnT1)
        if debug_name == "s1":
            join_all(("gpsimd",))
            p.dma("gpsimd", lambda: dbg.rearrange("(p a) m -> p (a m)", p=128),
                  lambda: ctx_tm[:, :], "s_gp")
            p.wait("gpsimd", "s_gp", p.c("s_gp"))
        emit_ffn1(xnT1)
        close_xnT1()

        if debug_name == "xnt0":
            dump_dbg(lambda: xnT[:, :])
        if debug_name == "ctx0":
            dump_dbg(lambda: ctx_tm[:, :])

        join_all(("sync",))
        p.dma("sync",
              lambda: out.rearrange("(i p) m -> p i m", p=128),
              lambda: x_tm[:, :].rearrange("p (i m) -> p i m", m=C),
              "s_out")
        out_done = p.c("s_out")
        p.wait("gpsimd", "s_out", out_done)

        with contextlib.ExitStack() as es2:
            sems = {s: es2.enter_context(nc.semaphore(s)) for s in SEMS}
            p.sems = sems
            with nc.Block() as block:
                @block.tensor
                def _(e):
                    for f in p.ops["tensor"]:
                        f(e)

                @block.vector
                def _(e):
                    for f in p.ops["vector"]:
                        f(e)

                @block.scalar
                def _(e):
                    for f in p.ops["scalar"]:
                        f(e)

                @block.gpsimd
                def _(e):
                    for f in p.ops["gpsimd"]:
                        f(e)

                @block.sync
                def _(e):
                    for f in p.ops["sync"]:
                        f(e)
    return nc


def _prep_inputs(inputs):
    d = dict(inputs)
    feed = {}
    feed["L0_attn_w_qkv"] = np.ascontiguousarray(d["L0_attn_w_qkv"], np.float32)
    feed["L0_attn_w_proj"] = np.ascontiguousarray(d["L0_attn_w_proj"], np.float32)
    feed["L0_ffn_w_fc"] = np.ascontiguousarray(d["L0_ffn_w_fc"], np.float32)
    feed["L0_ffn_b_fc"] = np.ascontiguousarray(d["L0_ffn_b_fc"], np.float32)
    feed["L0_ffn_w_proj"] = np.ascontiguousarray(d["L0_ffn_w_proj"], np.float32)
    feed["L0_ffn_b_proj"] = np.ascontiguousarray(d["L0_ffn_b_proj"], np.float32)
    feed["L1_attn_Wq"] = np.ascontiguousarray(d["L1_attn_Wq"], np.float32).reshape(C, C)
    feed["L1_attn_bq"] = np.ascontiguousarray(d["L1_attn_bq"], np.float32).reshape(C)
    feed["L1_attn_Wk"] = np.ascontiguousarray(d["L1_attn_Wk"], np.float32).reshape(C, C)
    feed["L1_attn_bk"] = np.ascontiguousarray(d["L1_attn_bk"], np.float32).reshape(C)
    feed["L1_attn_Wv"] = np.ascontiguousarray(d["L1_attn_Wv"], np.float32).reshape(C, C)
    feed["L1_attn_bv"] = np.ascontiguousarray(d["L1_attn_bv"], np.float32).reshape(C)
    feed["L1_attn_Wo"] = np.ascontiguousarray(d["L1_attn_Wo"], np.float32).reshape(C, C)
    feed["L1_attn_bo"] = np.ascontiguousarray(d["L1_attn_bo"], np.float32)
    feed["L1_attn_gamma"] = np.ascontiguousarray(d["L1_attn_gamma"], np.float32)
    feed["L1_attn_beta"] = np.ascontiguousarray(d["L1_attn_beta"], np.float32)
    feed["L1_ffn_w_gate"] = np.ascontiguousarray(d["L1_ffn_w_gate"], np.int32)
    feed["L1_ffn_w_up"] = np.ascontiguousarray(d["L1_ffn_w_up"], np.int32)
    feed["L1_ffn_w_down"] = np.ascontiguousarray(d["L1_ffn_w_down"], np.int32)
    feed.update(_host_consts())
    xs = np.ascontiguousarray(np.asarray(d["L0_attn_x"], np.float32))
    return feed, xs


def kernel(**inputs):
    feed, xs = _prep_inputs(inputs)
    nc = build()
    in_maps = []
    for b in range(8):
        m = dict(feed)
        m["x"] = xs[b]
        in_maps.append(m)
    res = run_bass_kernel_spmd(nc, in_maps, core_ids=list(range(8)))
    return np.stack([np.asarray(res.results[b]["out"]) for b in range(8)], axis=0)
